# version 06
# baseline (speedup 1.0000x reference)
import numpy as np
import ml_dtypes
import concourse.bass as bass
import concourse.mybir as mybir
from concourse.bass_utils import run_bass_kernel_spmd
from contextlib import ExitStack

F32 = mybir.dt.float32
F32R = mybir.dt.float32r
BF16 = mybir.dt.bfloat16
I32 = mybir.dt.int32
ALU = mybir.AluOpType
AF = mybir.ActivationFunctionType
AX = mybir.AxisListType

D = 1024
CS = 256
NCH = 8
B = 4
TC = 512
NS_SCHED = [
    (5.1961444193, -5.1961418212),
    (2.5980691358, -0.6495181190),
    (2.5980578201, -0.6495166383),
    (2.5980284218, -0.6495127916),
    (2.5979520476, -0.6495027980),
    (2.5977536542, -0.6494768381),
    (2.5972384338, -0.6494094201),
    (2.5959013401, -0.6492344501),
    (2.5924371443, -0.6487810349),
    (2.5835036800, -0.6476114866),
    (2.5607332767, -0.6446274371),
    (2.5044075629, -0.6372283209),
    (2.3750656855, -0.6201377464),
    (2.1254870094, -0.5867349626),
    (1.7905710134, -0.5409002201),
    (1.5572128750, -0.5081481882),
    (1.5021226287, -0.5003031936),
]
NIT = len(NS_SCHED)
NSPLIT = 8
G4 = [[0, 1, 2, 3], [4, 5, 6, 7]]
GPAIR = [[0, 4], [1, 5], [2, 6], [3, 7]]
G8 = [[0, 1, 2, 3, 4, 5, 6, 7]]

NL = [TC >> l for l in range(10)]
NLP = [max(n, 2) for n in NL]
EOFF = []
_o = 0
for _l in range(10):
    EOFF.append(_o)
    _o += NCH * B * NLP[_l]
E_COLS = _o
RMS_EPS = 1e-8
LN_EPS = 1e-5
R75 = 1.0 / 7.5


def _r3(ap, p=128):
    return ap.rearrange("(a p) c -> a p c", p=p).transpose([1, 0, 2])


class Plan:
    def __init__(self):
        self.items = []
        self.count = {k: 0 for k in "TVCSGKH"}
        self.tcost = []

    def add(self, eng, fn, deps=(), cost=0):
        step = 16 if eng in "SG" else 1
        self.count[eng] += step
        tok = (eng, self.count[eng])
        self.items.append((eng, fn, [d for d in deps if d is not None], tok))
        self.tcost.append(cost if eng == "T" else 0)
        return tok

    def barrier(self):
        return [(k, v) for k, v in self.count.items() if v]

    def replay(self, keys, handle, sems, lo=0, hi=None):
        for eng, fn, deps, tok in self.items[lo:hi]:
            if eng not in keys:
                continue
            need = {}
            for k, val in deps:
                if k in keys:
                    continue
                sem = sems[k]
                kk = id(sem)
                need[kk] = (sem, max(val, need[kk][1]) if kk in need else val)
            for sem, val in need.values():
                handle.wait_ge(sem, val)
            fn(handle).then_inc(sems[eng], 16 if eng in "SG" else 1)


def build(dbg=False):
    nc = bass.Bass("TRN2", target_bir_lowering=False, debug=False)
    xs_in = nc.dram_tensor("xs", [B, NCH, 128, TC + 1], F32, kind="ExternalInput").ap()
    wq_in = nc.dram_tensor("wq_t", [D, D], F32, kind="ExternalInput").ap()
    wv_in = nc.dram_tensor("wv_t", [D, D], F32, kind="ExternalInput").ap()
    wc_in = nc.dram_tensor("wc_t", [D, D], F32, kind="ExternalInput").ap()
    nsW_in = nc.dram_tensor("nsW", [D, D], F32, kind="ExternalInput").ap()
    nsWs_in = nc.dram_tensor("nsWs", [D, CS], F32, kind="ExternalInput").ap()
    id_in = nc.dram_tensor("ident128", [128, 128], F32R, kind="ExternalInput").ap()
    qw_in = nc.dram_tensor("qw_t", [6, D, D], I32, kind="ExternalInput").ap()
    srep_in = nc.dram_tensor("srep_t", [6, D, D], F32, kind="ExternalInput").ap()
    sT_in = nc.dram_tensor("sT_t", [6, 64, D], F32, kind="ExternalInput").ap()
    bias_in = nc.dram_tensor("bias_t", [6, D], F32, kind="ExternalInput").ap()
    biasfm_in = nc.dram_tensor("bias_fm", [128, 48], F32, kind="ExternalInput").ap()
    la_in = nc.dram_tensor("la_t", [6, D, 32], F32, kind="ExternalInput").ap()
    lb_in = nc.dram_tensor("lb_t", [6, 32, D], F32, kind="ExternalInput").ap()
    gam_in = nc.dram_tensor("gam_fm", [128, 3 * NCH], F32, kind="ExternalInput").ap()
    bet_in = nc.dram_tensor("bet_fm", [128, 3 * NCH], F32, kind="ExternalInput").ap()
    pool_in = nc.dram_tensor("pool_m", [NCH, 128, 64], F32, kind="ExternalInput").ap()
    sel_in = nc.dram_tensor("sel_m", [8, 2], F32, kind="ExternalInput").ap()
    m0_in = nc.dram_tensor("m0v", [128, 2], F32, kind="ExternalInput").ap()
    idv_in = nc.dram_tensor("idvec_fm", [128, NCH], F32, kind="ExternalInput").ap()
    out_d = nc.dram_tensor("out", [B, NCH, 128, TC], F32, kind="ExternalOutput").ap()

    gin = nc.dram_tensor("ns_gin", [D, CS], F32).ap()
    gout = nc.dram_tensor("ns_gout", [4, D, CS], F32).ap()
    pexch = nc.dram_tensor("ns_pexch", [D, D], BF16).ap()
    pboth = nc.dram_tensor("ns_pboth", [2, D, D], BF16).ap()
    gdram = nc.dram_tensor("g_spill", [NCH, 128, B * TC], BF16).ap()
    sc_gin = nc.dram_tensor("sc_gin", [128, NCH, B], BF16).ap()
    sc_gout = nc.dram_tensor("sc_gout", [8, 128, NCH, B], BF16).ap()

    st = ExitStack()
    e = st.enter_context

    sid = e(nc.sbuf_tensor("sid", [128, 128], F32R))
    sones = e(nc.sbuf_tensor("sones", [128, 1], F32))
    sonesr = e(nc.sbuf_tensor("sonesr", [1, 128], F32))
    sonebf = e(nc.sbuf_tensor("sonebf", [1, TC], BF16))
    sonesbf = e(nc.sbuf_tensor("sonesbf", [128, 1], BF16))
    m0v = e(nc.sbuf_tensor("m0v_s", [128, 2], F32))
    m0a = e(nc.sbuf_tensor("m0a_s", [128, 1], F32))
    m0bv = e(nc.sbuf_tensor("m0b_s", [128, 1], F32))
    selbf = e(nc.sbuf_tensor("selbf", [8, 2], BF16))
    idvbf = e(nc.sbuf_tensor("idvbf", [128, 2 * NCH], BF16))
    sidbf = e(nc.sbuf_tensor("sidbf", [128, 128], BF16))
    rowS = e(nc.sbuf_tensor("rowS", [1, 3 * TC], F32))
    fscr = e(nc.sbuf_tensor("fscr", [128, TC], F32))
    fscr2 = e(nc.sbuf_tensor("fscr2", [128, 40], F32))
    fscr3 = e(nc.sbuf_tensor("fscr3", [128, 40], F32))
    idvm = e(nc.sbuf_tensor("idvm", [128, 8], F32))
    smallf = e(nc.sbuf_tensor("smallf", [128, 64], F32))
    cstf = e(nc.sbuf_tensor("cstf", [128, 4], F32))

    base0 = nc.sbuf_base

    def alias_pair(name, shape, dt1, dt2):
        b = nc.sbuf_base
        a1 = nc.alloc_sbuf_tensor(name + "_e", shape, dt1)
        nc.sbuf_base = b
        a2 = nc.alloc_sbuf_tensor(name + "_r", shape, dt2)
        return a1, a2

    xf_base = nc.sbuf_base
    Xf, Xfr = alias_pair("Xf", [128, NCH * D], F32, F32R)
    XTe, XTr = alias_pair("XT", [128, NCH * D], F32, F32R)
    xse_base = nc.sbuf_base
    Xse, Xsr = alias_pair("Xs", [128, NCH * CS], F32, F32R)
    Ame, Amr = alias_pair("Am", [128, NCH * CS], F32, F32R)
    ns_small = nc.alloc_sbuf_tensor("ns_small", [128, D + 64], F32)
    p1_end = nc.sbuf_base
    nc.sbuf_base = xse_base
    PBF = nc.alloc_sbuf_tensor("PBF", [128, NCH * D], BF16)
    nc.sbuf_base = xf_base
    P1T = nc.alloc_sbuf_tensor("P1T", [128, NCH * D], BF16)
    P2T = nc.alloc_sbuf_tensor("P2T", [128, NCH * D], BF16)
    p1t_end = nc.sbuf_base

    nc.sbuf_base = p1_end
    hB = nc.alloc_sbuf_tensor("hB", [128, NCH * B * TC], BF16)
    qt = [nc.alloc_sbuf_tensor(f"qt{i}", [128, NCH * D], BF16) for i in range(2)]
    qraw = nc.alloc_sbuf_tensor("qraw", [128, 512], I32)
    qfsc = nc.alloc_sbuf_tensor("qfsc", [128, 512], F32)
    srepc = nc.alloc_sbuf_tensor("srepc", [128, 512], F32)
    cvt32 = nc.alloc_sbuf_tensor("cvt32", [128, TC + 4], F32)
    xstg = nc.alloc_sbuf_tensor("xstg", [128, TC + 1], F32)
    nsT2 = nc.alloc_sbuf_tensor("nsT2", [64, 2 * D], BF16)
    lb2 = nc.alloc_sbuf_tensor("lb2", [32, 2 * D], BF16)
    labf = nc.alloc_sbuf_tensor("labf", [128, 6 * NCH * 32], BF16)
    biasc = nc.alloc_sbuf_tensor("biasc", [128, 48], F32)
    gamb = nc.alloc_sbuf_tensor("gamb", [128, 3 * NCH], F32)
    betb = nc.alloc_sbuf_tensor("betb", [128, 3 * NCH], F32)
    poolbf = nc.alloc_sbuf_tensor("poolbf", [128, NCH * 64], BF16)
    xbfB = nc.alloc_sbuf_tensor("xbfB", [128, NCH * (TC + 2)], BF16)
    sqB = nc.alloc_sbuf_tensor("sqB", [128, NCH * (TC + 2)], BF16)
    y1B = nc.alloc_sbuf_tensor("y1B", [128, NCH * TC], BF16)
    auxB = nc.alloc_sbuf_tensor("auxB", [64, TC], BF16)
    auxB2 = nc.alloc_sbuf_tensor("auxB2", [32, TC], BF16)
    p2_end = nc.sbuf_base

    nc.sbuf_base = p1t_end
    Ebuf = nc.alloc_sbuf_tensor("Ebuf", [128, E_COLS], BF16)
    wbf = nc.alloc_sbuf_tensor("wbf", [128, NCH * D], BF16)
    xtl2 = nc.alloc_sbuf_tensor("xtl2", [128, NCH * (TC + 1)], F32)
    zscr = nc.alloc_sbuf_tensor("zscr", [128, NCH * TC], BF16)
    cvt3 = nc.alloc_sbuf_tensor("cvt3", [128, D], F32)
    xbf3 = nc.alloc_sbuf_tensor("xbf3", [128, NCH * (TC + 2)], BF16)
    sqb3 = nc.alloc_sbuf_tensor("sqb3", [128, NCH * (TC + 2)], BF16)
    actB = nc.alloc_sbuf_tensor("actB", [128, NCH * TC], BF16)
    vtile = nc.alloc_sbuf_tensor("vtile", [128, NCH * TC], BF16)
    gt2 = nc.alloc_sbuf_tensor("gt2", [128, NCH * TC], BF16)
    qsh = nc.alloc_sbuf_tensor("qsh", [128, NCH * TC], BF16)
    otile = nc.alloc_sbuf_tensor("otile", [128, TC], F32)
    e9all = nc.alloc_sbuf_tensor("e9all", [128, NCH * B * 8], BF16)
    emid = nc.alloc_sbuf_tensor("emid", [128, NCH * B * 8], BF16)
    rmid = nc.alloc_sbuf_tensor("rmid", [128, NCH * B * 12], BF16)
    r9sel = nc.alloc_sbuf_tensor("r9sel", [128, NCH * B * 2], BF16)
    r9tm = nc.alloc_sbuf_tensor("r9tm", [8, D], BF16)
    qlastB = nc.alloc_sbuf_tensor("qlastB", [128, NCH * B * 2], BF16)
    xcolB = nc.alloc_sbuf_tensor("xcolB", [128, NCH * B * 2], BF16)
    p3_end = nc.sbuf_base
    nc.sbuf_base = max(p2_end, p3_end)

    pA = [e(nc.psum_tensor(f"pA{i}", [128, 512], F32)) for i in range(2)]
    pU = [e(nc.psum_tensor(f"pU{i}", [128, 512], F32)) for i in range(2)]
    pT = [e(nc.psum_tensor(f"pT{i}", [128, 512], F32)) for i in range(2)]
    pX = [e(nc.psum_tensor(f"pX{i}", [128, 512], F32)) for i in range(2)]
    pAs = [p[:, 0:CS] for p in pA]
    pUs = [p[:, 0:CS] for p in pU]
    pTs = [p[:, 0:128] for p in pT]
    pN1 = pT[0][0:1, 256:257]
    pN2 = pT[1][:, 256:257]
    pR0 = pT[0][0:1, 0:512]
    pR1 = pT[1][0:1, 0:512]

    dL = e(nc.semaphore("dL"))
    dO = e(nc.semaphore("dO"))
    dRs = e(nc.semaphore("dRs"))
    cc = e(nc.semaphore("cc"))
    tG = e(nc.semaphore("tG"))
    tU = e(nc.semaphore("tU"))
    tT = e(nc.semaphore("tT"))
    vA = e(nc.semaphore("vA"))
    vU = e(nc.semaphore("vU"))
    cT = e(nc.semaphore("cT"))
    nX = e(nc.semaphore("nX"))
    eT = e(nc.semaphore("eT"))
    eV = e(nc.semaphore("eV"))
    eC = e(nc.semaphore("eC"))
    eS = e(nc.semaphore("eS"))
    eG = e(nc.semaphore("eG"))
    eK = e(nc.semaphore("eK"))
    eH = e(nc.semaphore("eH"))
    SEMS = {"T": eT, "V": eV, "C": eC, "S": eS, "G": eG, "K": eK, "H": eH, "nX": nX}

    def XsE(k):
        return Xsr if k >= NSPLIT - 1 else Xse

    def XTk(r):
        return XTr if r >= NSPLIT else XTe

    def AmK(k):
        return Amr if k >= NSPLIT else Ame

    def XfR(r):
        return Xfr if r >= NSPLIT else Xf

    def tr_src(t, r):
        r8, tt = divmod(t, 8)
        o = r8 * D + tt * 128
        return XfR(r)[:, o:o + 128]

    def tr_dst(t, r):
        r8, tt = divmod(t, 8)
        o = tt * D + r8 * 128
        return XTk(r)[:, o:o + 128]

    ssq = ns_small[:, 0:D]
    sstat = ns_small[:, D:D + 8]
    sstat1 = ns_small[:, D + 8:D + 9]
    snrm_s = ns_small[0:1, D + 12:D + 13]
    snrm_r = ns_small[0:1, D + 14:D + 15]
    sscal = ns_small[:, D + 16:D + 17]

    P = Plan()
    state = {"pR_reader": None, "pB_readers": [], "pX_evac": [None, None],
             "pU_owner": None, "E_ready": {}, "pT_trans": [None, None]}

    def dma(eng, out, in_, deps=()):
        return P.add(eng, (lambda h, out=out, in_=in_:
                           h.dma_start(out=out, in_=in_)), deps)

    def gcp(out, in_, deps=()):
        return P.add("H", (lambda h, out=out, in_=in_:
                           h.tensor_copy(out, in_)), deps)

    def dmas(eng, out, in_, deps=()):
        return P.add(eng, (lambda h, out=out, in_=in_: h.dma_start(
            out=out, in_=in_, allow_slow_non_contiguous=True)), deps)

    def mmg(psum_ap, pairs, deps):
        def fn(h, psum_ap=psum_ap, pairs=pairs):
            ins = None
            n = len(pairs)
            for j, (l_, r_) in enumerate(pairs):
                ins = h.matmul(psum_ap, l_, r_, start=(j == 0),
                               stop=(j == n - 1))
            return ins
        try:
            cost = sum(128 + r_.shape[-1] for (l_, r_) in pairs)
        except Exception:
            cost = 640 * len(pairs)
        return P.add("T", fn, deps, cost=cost)

    def vi(fn, deps):
        return P.add("V", fn, deps)

    def ci(fn, deps):
        return P.add("C", fn, deps)

    def load_cvt(dst_ap, src_ap, stage_ap, prev, scale=None):
        ld = dma("S", stage_ap, src_ap, [prev] if prev else [])
        if scale is None:
            return P.add("V", lambda h, d=dst_ap, s_=stage_ap:
                         h.tensor_copy(d, s_), [ld])
        return P.add("V", lambda h, d=dst_ap, s_=stage_ap, sc=scale:
                     h.tensor_scalar(d, s_, sc, None, ALU.mult), [ld])

    ns_done = [("nX", 8)]

    c0 = P.add("V", lambda h: (h.memset(sonebf[:], 1.0),
                               h.memset(sonesbf[:], 1.0),
                               h.memset(cstf[:, 0:1], 1.0 / D),
                               h.memset(cstf[:, 1:2], LN_EPS),
                               h.memset(cstf[:, 2:3], RMS_EPS),
                               h.memset(cstf[:, 3:4], 1.0))[-1],
               [])
    t_g = dma("S", gamb[:, :], gam_in, [])
    t_b = dma("S", betb[:, :], bet_in, [])
    t_bias = dma("S", biasc[:, :], biasfm_in, [])
    t_m = dma("S", m0v[:, :], m0_in, [])
    t_m = vi(lambda h: (h.tensor_copy(m0a[:, :], m0v[:, 0:1]),
                        h.tensor_copy(m0bv[:, :], m0v[:, 1:2]))[-1], [t_m])
    t_i = dma("S", smallf[:, 0:NCH], idv_in, [])
    t_s = dma("S", smallf[0:8, 12:14], sel_in, [])
    c_i = vi(lambda h: h.tensor_copy(idvbf[:, 0:2 * NCH:2], smallf[:, 0:NCH]), [t_i])
    c_i2 = ci(lambda h: h.activation(idvm[:, 0:NCH], smallf[:, 0:NCH],
                                     AF.Copy, scale=m0bv.ap()),
              [t_i, t_m, c0])
    c_s = vi(lambda h: h.tensor_copy(selbf[:, :], smallf[0:8, 12:14]), [t_s])
    prev = dma("S", cvt32[:, 0:128], id_in.bitcast(F32), [c_s])
    prev = vi(lambda h: h.tensor_copy(sidbf[:, :], cvt32[:, 0:128]), [prev])
    prev = dma("S", cvt32[:, 0:512], pool_in.transpose([1, 0, 2]), [prev])
    prev = vi(lambda h: h.tensor_copy(poolbf[:, :], cvt32[:, 0:512]), [prev])
    la4 = labf.ap().rearrange("p (i a r) -> p i a r", a=NCH, r=32)
    for i in range(6):
        prev = load_cvt(la4[:, i], _r3(la_in[i]),
                        cvt32[:, 0:256].rearrange("p (a r) -> p a r", r=32),
                        prev)
    consts_done = prev

    def load_nsT(i, deps):
        s = (i % 2) * D
        t1 = dma("S", cvt32[0:64, 0:512], sT_in[i][:, 0:512], deps)
        v1 = vi(lambda h, i=i, s=s: h.tensor_scalar(
            nsT2[0:64, s:s + 512], cvt32[0:64, 0:512], -1.0, None, ALU.mult),
            [t1])
        t2 = dma("S", cvt32[0:64, 0:512], sT_in[i][:, 512:D], [v1])
        v2 = vi(lambda h, i=i, s=s: h.tensor_scalar(
            nsT2[0:64, s + 512:s + D], cvt32[0:64, 0:512], -1.0, None,
            ALU.mult), [t2])
        return v2

    def load_lb(i, deps):
        s = (i % 2) * D
        t1 = dma("S", cvt32[0:32, 0:512], lb_in[i][:, 0:512], deps)
        v1 = vi(lambda h, i=i, s=s: h.tensor_copy(
            lb2[0:32, s:s + 512], cvt32[0:32, 0:512]), [t1])
        t2 = dma("S", cvt32[0:32, 0:512], lb_in[i][:, 512:D], [v1])
        v2 = vi(lambda h, i=i, s=s: h.tensor_copy(
            lb2[0:32, s + 512:s + D], cvt32[0:32, 0:512]), [t2])
        return v2

    TCP = TC + 2

    def emit_ln(xtile, xbft, sqt, wo, dst_fn, ld_tok, extra=()):
        cvt = vi(lambda h: [h.tensor_copy(
            xbft[:, d8 * TCP: d8 * TCP + TC],
            xtile[:, d8 * (TC + 1) + wo: d8 * (TC + 1) + wo + TC])
            for d8 in range(NCH)][-1],
            [ld_tok, state.get("last_lnmm")] + list(extra))
        sq_toks = []
        for d8 in range(NCH):
            of = d8 * (TC + 1) + wo
            o = d8 * TCP
            tok = vi(lambda h, o=o, of=of: h.tensor_mul(
                sqt[:, o:o + TC], xtile[:, of:of + TC],
                xtile[:, of:of + TC]), [cvt])
            sq_toks.append(tok)
        d0 = [state["pR_reader"], cvt]
        s0 = mmg(pR0[0:1, 0:TC],
                 [(sonesbf[:], xbft[:, d8 * TCP: d8 * TCP + TC])
                  for d8 in range(NCH)], d0 + [c0])
        s1 = mmg(pR1[0:1, 0:TC],
                 [(sonesbf[:], sqt[:, d8 * TCP: d8 * TCP + TC])
                  for d8 in range(NCH)], sq_toks)
        state["last_lnmm"] = s1
        cst = ci(lambda h: (h.activation(rowS[0:1, 0:TC], pR0[0:1, 0:TC],
                                         AF.Copy, scale=cstf[0:1, 0:1]),
                            h.activation(rowS[0:1, TC:2 * TC], pR1[0:1, 0:TC],
                                         AF.Copy, scale=cstf[0:1, 0:1]))[-1],
                 [s0, s1, c0])
        state["pR_reader"] = cst
        v0 = vi(lambda h: (h.tensor_mul(rowS[0:1, 2 * TC:3 * TC],
                                        rowS[0:1, 0:TC], rowS[0:1, 0:TC]),
                           h.tensor_sub(rowS[0:1, 2 * TC:3 * TC],
                                        rowS[0:1, TC:2 * TC],
                                        rowS[0:1, 2 * TC:3 * TC]))[-1], [cst])
        c1 = ci(lambda h: h.activation(rowS[0:1, 2 * TC:3 * TC],
                                       rowS[0:1, 2 * TC:3 * TC], AF.Sqrt,
                                       bias=cstf[0:1, 1:2]), [v0])
        v1 = vi(lambda h: h.reciprocal(rowS[0:1, 2 * TC:3 * TC],
                                       rowS[0:1, 2 * TC:3 * TC]), [c1])
        bc = mmg(pA[0][:, 0:TC], [(sonesr[:], rowS[0:1, 0:TC])],
                 [v1] + state["pB_readers"])
        bc2 = mmg(pA[1][:, 0:TC], [(sonesr[:], rowS[0:1, 2 * TC:3 * TC])],
                  [v1])
        state["pB_readers"] = []
        outs = []
        for d8 in range(NCH):
            o = d8 * TCP
            tok = vi(lambda h, o=o, d8=d8: (
                h.tensor_sub(fscr[:, 0:TC], xbft[:, o:o + TC],
                             pA[0][:, 0:TC]),
                h.tensor_mul(dst_fn(d8), fscr[:, 0:TC], pA[1][:, 0:TC]))[-1],
                [bc, bc2])
            outs.append(tok)
        state["pB_readers"] = outs
        return outs

    def hb_ap(d8, b):
        return hB[:, (d8 * B + b) * TC:(d8 * B + b) * TC + TC]

    def y1_ap(d8):
        return y1B[:, d8 * TC:(d8 + 1) * TC]

    def emit_ln_p2(b, dst_fn):
        cvs = []
        p = state.get("xstg_own")
        for d8 in range(NCH):
            ldc = dma("S", xstg[:, :], xs_in[b, d8], [p] if p else [])
            cv = vi(lambda h, d8=d8: (
                h.tensor_copy(xbfB[:, d8 * TCP: d8 * TCP + TC],
                              xstg[:, 1:TC + 1]),
                h.tensor_mul(sqB[:, d8 * TCP: d8 * TCP + TC],
                             xstg[:, 1:TC + 1], xstg[:, 1:TC + 1]))[-1],
                [ldc])
            p = cv
            cvs.append(cv)
        state["xstg_own"] = p
        s0 = mmg(pX[0][0:1, 0:TC],
                 [(sonesbf[:], xbfB[:, d8 * TCP: d8 * TCP + TC])
                  for d8 in range(NCH)],
                 cvs + [c0, state["pX_evac"][0]])
        s1 = mmg(pX[1][0:1, 0:TC],
                 [(sonesbf[:], sqB[:, d8 * TCP: d8 * TCP + TC])
                  for d8 in range(NCH)], cvs + [state["pX_evac"][1]])
        cst = ci(lambda h: (h.activation(rowS[0:1, 0:TC], pX[0][0:1, 0:TC],
                                         AF.Copy, scale=cstf[0:1, 0:1]),
                            h.activation(rowS[0:1, TC:2 * TC],
                                         pX[1][0:1, 0:TC],
                                         AF.Copy, scale=cstf[0:1, 0:1]))[-1],
                 [s0, s1, c0])
        v0 = vi(lambda h: (h.tensor_mul(rowS[0:1, 2 * TC:3 * TC],
                                        rowS[0:1, 0:TC], rowS[0:1, 0:TC]),
                           h.tensor_sub(rowS[0:1, 2 * TC:3 * TC],
                                        rowS[0:1, TC:2 * TC],
                                        rowS[0:1, 2 * TC:3 * TC]))[-1], [cst])
        c1 = ci(lambda h: h.activation(rowS[0:1, 2 * TC:3 * TC],
                                       rowS[0:1, 2 * TC:3 * TC], AF.Sqrt,
                                       bias=cstf[0:1, 1:2]), [v0])
        v1 = vi(lambda h: h.reciprocal(rowS[0:1, 2 * TC:3 * TC],
                                       rowS[0:1, 2 * TC:3 * TC]), [c1])
        bc = mmg(pX[0][:, 0:TC], [(sonesr[:], rowS[0:1, 0:TC])],
                 [v1, cst] + state["pB_readers"])
        bc2 = mmg(pX[1][:, 0:TC], [(sonesr[:], rowS[0:1, 2 * TC:3 * TC])],
                  [v1, cst])
        state["pB_readers"] = []
        outs = []
        for d8 in range(NCH):
            o = d8 * TCP
            tok = vi(lambda h, o=o, d8=d8: (
                h.tensor_sub(fscr[:, 0:TC], xbfB[:, o:o + TC],
                             pX[0][:, 0:TC]),
                h.tensor_mul(dst_fn(d8), fscr[:, 0:TC], pX[1][:, 0:TC]))[-1],
                [bc, bc2])
            outs.append(tok)
        state["pB_readers"] = outs
        state["pX_evac"][0] = outs[-1]
        state["pX_evac"][1] = outs[-1]
        return outs

    h_init = []
    for b in range(B):
        outs = emit_ln_p2(b, lambda d8, b=b: hb_ap(d8, b))
        h_init.extend(outs)

    def dequant(i, buf, prev_tok, extra_deps):
        p = prev_tok
        for d8 in range(NCH):
            for hf in range(2):
                sl = slice(hf * 512, hf * 512 + 512)
                l1 = dma("S", qraw[:, :], _r3(qw_in[i])[:, d8, sl],
                         [p] + (extra_deps if d8 == 0 and hf == 0 else []))
                l2 = dma("S", srepc[:, :], _r3(srep_in[i])[:, d8, sl], [l1])
                o = d8 * D + hf * 512
                p = vi(lambda h, o=o, buf=buf: (
                    h.tensor_copy(qfsc[:, :], qraw[:, :]),
                    h.tensor_scalar(qfsc[:, :], qfsc[:, :], R75, None,
                                    ALU.mult),
                    h.tensor_mul(buf[:, o:o + 512], qfsc[:, :],
                                 srepc[:, :]))[-1], [l1, l2])
        return p

    def qlora(i, buf, src_fn, b, dq_tok, ns_tok, evac_kind, src_toks):
        sL = (i % 2) * D
        a_deps = src_toks + [state["pX_evac"][0]]
        xg0 = mmg(pX[0][0:64, 0:256],
                  [(poolbf[:, d8 * 64:(d8 + 1) * 64], src_fn(d8)[:, 0:256])
                   for d8 in range(NCH)], a_deps)
        xg1 = mmg(pX[0][0:64, 256:512],
                  [(poolbf[:, d8 * 64:(d8 + 1) * 64], src_fn(d8)[:, 256:512])
                   for d8 in range(NCH)], [])
        md0 = mmg(pX[1][0:32, 0:256],
                  [(la4[:, i, d8], src_fn(d8)[:, 0:256])
                   for d8 in range(NCH)], [state["pX_evac"][1]])
        md1 = mmg(pX[1][0:32, 256:512],
                  [(la4[:, i, d8], src_fn(d8)[:, 256:512])
                   for d8 in range(NCH)], [])
        ev = vi(lambda h: (h.tensor_copy(auxB[0:64, :], pX[0][0:64, :]),
                           h.tensor_copy(auxB2[0:32, :], pX[1][0:32, :]))[-1],
                [xg0, xg1, md0, md1])
        state["pX_evac"][0] = ev
        state["pX_evac"][1] = ev
        out_toks = []
        last_mm = None
        prev_vadd = None
        for ot in range(NCH):
            deps = [ev, dq_tok, ns_tok, state["pX_evac"][ot % 2]]
            pairs = [(buf[:, d8 * D + ot * 128: d8 * D + ot * 128 + 128],
                      src_fn(d8)) for d8 in range(NCH)]
            pairs.append((nsT2[0:64, sL + ot * 128: sL + ot * 128 + 128],
                          auxB[0:64, :]))
            pairs.append((lb2[0:32, sL + ot * 128: sL + ot * 128 + 128],
                          auxB2[0:32, :]))
            mm = mmg(pX[ot % 2][:, 0:TC], pairs, deps)
            last_mm = mm
            bcol = biasc[:, i * 8 + ot: i * 8 + ot + 1]
            if evac_kind == "y1":
                tokc = ci(lambda h, ot=ot, bcol=bcol: h.activation(
                    y1_ap(ot), pX[ot % 2][:, 0:TC], AF.Identity,
                    scale=cstf[:, 3:4], bias=bcol), [mm, t_bias, c0])
                tok = ci(lambda h, ot=ot: h.activation(
                    sqB[:, ot * TCP: ot * TCP + TC], y1_ap(ot),
                    AF.Square), [tokc, state.get("last_lnmm")])
            else:
                tokc = ci(lambda h, ot=ot, bcol=bcol: h.activation(
                    fscr[:, 0:TC], pX[ot % 2][:, 0:TC], AF.Identity,
                    scale=cstf[:, 3:4], bias=bcol),
                    [mm, t_bias, c0] +
                    ([prev_vadd] if prev_vadd else []))
                tok = vi(lambda h, ot=ot, b=b: h.tensor_add(
                    hb_ap(ot, b), hb_ap(ot, b), fscr[:, 0:TC]), [tokc])
                prev_vadd = tok
            state["pX_evac"][ot % 2] = tok
            out_toks.append(tok)
        return out_toks, last_mm

    def ffn_layernorm(blk, y_toks):
        sq_toks = y_toks
        s0 = mmg(pX[0][0:1, 0:TC], [(sonesbf[:], y1_ap(d8))
                                    for d8 in range(NCH)],
                 [state["pX_evac"][0]] + y_toks)
        s1 = mmg(pX[1][0:1, 0:TC],
                 [(sonesbf[:], sqB[:, d8 * TCP: d8 * TCP + TC])
                  for d8 in range(NCH)], sq_toks + [state["pX_evac"][1]])
        state["last_lnmm"] = s1
        cst = ci(lambda h: (h.activation(rowS[0:1, 0:TC], pX[0][0:1, 0:TC],
                                         AF.Copy, scale=cstf[0:1, 0:1]),
                            h.activation(rowS[0:1, TC:2 * TC],
                                         pX[1][0:1, 0:TC],
                                         AF.Copy, scale=cstf[0:1, 0:1]))[-1],
                 [s0, s1, c0])
        v0 = vi(lambda h: (h.tensor_mul(rowS[0:1, 2 * TC:3 * TC],
                                        rowS[0:1, 0:TC], rowS[0:1, 0:TC]),
                           h.tensor_sub(rowS[0:1, 2 * TC:3 * TC],
                                        rowS[0:1, TC:2 * TC],
                                        rowS[0:1, 2 * TC:3 * TC]))[-1], [cst])
        c1 = ci(lambda h: h.activation(rowS[0:1, 2 * TC:3 * TC],
                                       rowS[0:1, 2 * TC:3 * TC], AF.Sqrt,
                                       bias=cstf[0:1, 1:2]), [v0])
        v1 = vi(lambda h: h.reciprocal(rowS[0:1, 2 * TC:3 * TC],
                                       rowS[0:1, 2 * TC:3 * TC]), [c1])
        bc = mmg(pX[0][:, 0:TC], [(sonesr[:], rowS[0:1, 0:TC])],
                 [v1, cst] + state["pB_readers"])
        bc2 = mmg(pX[1][:, 0:TC], [(sonesr[:], rowS[0:1, 2 * TC:3 * TC])],
                  [v1, cst])
        state["pB_readers"] = []
        outs = []
        for d8 in range(NCH):
            t1 = vi(lambda h, d8=d8: (
                h.tensor_sub(fscr[:, 0:TC], y1_ap(d8), pX[0][:, 0:TC]),
                h.tensor_mul(y1_ap(d8), fscr[:, 0:TC], pX[1][:, 0:TC]))[-1],
                [bc, bc2])
            t2 = ci(lambda h, d8=d8, blk=blk: h.activation(
                y1_ap(d8), y1_ap(d8), AF.Gelu,
                bias=betb[:, blk * NCH + d8: blk * NCH + d8 + 1],
                scale=gamb[:, blk * NCH + d8: blk * NCH + d8 + 1]),
                [t1, t_g, t_b])
            outs.append(t2)
        state["pB_readers"] = outs
        state["pX_evac"][0] = outs[-1]
        state["pX_evac"][1] = outs[-1]
        return outs

    dq_prev = dequant(0, qt[0], consts_done, h_init)
    dqA = dq_prev
    dqB = None
    ns_prev = load_nsT(0, [consts_done])
    ns_prev = load_lb(0, [ns_prev])
    nsA = ns_prev
    nsB = None
    qt_read = [None, None]
    for blk in range(3):
        i0, i1 = 2 * blk, 2 * blk + 1
        dqB = dequant(i1, qt[1], dq_prev, [qt_read[1]] if qt_read[1] else [])
        dq_prev = dqB
        ns_prev = load_nsT(i1, [ns_prev] +
                           ([qt_read[1]] if qt_read[1] else []))
        ns_prev = load_lb(i1, [ns_prev])
        nsB = ns_prev
        for b in range(B):
            y_toks, lmm = qlora(i0, qt[0], lambda d8, b=b: hb_ap(d8, b), b,
                                dqA, nsA, "y1",
                                h_init if blk == 0 and b == 0 else [])
            qt_read[0] = lmm
            y2 = ffn_layernorm(blk, y_toks)
            _, lmm = qlora(i1, qt[1], lambda d8: y1_ap(d8), b, dqB, nsB,
                           "hupd", y2)
            qt_read[1] = lmm
        if blk < 2:
            dqA = dequant(2 * blk + 2, qt[0], dq_prev,
                          [qt_read[0]] if qt_read[0] else [])
            dq_prev = dqA
            ns_prev = load_nsT(2 * blk + 2, [ns_prev] +
                               ([qt_read[0]] if qt_read[0] else []))
            ns_prev = load_lb(2 * blk + 2, [ns_prev])
            nsA = ns_prev
    g_spills = []
    sil_prev = P.barrier()
    for d8 in range(NCH):
        s_tok = ci(lambda h, d8=d8: h.activation(
            hB[:, d8 * B * TC:(d8 + 1) * B * TC],
            hB[:, d8 * B * TC:(d8 + 1) * B * TC], AF.Silu),
            sil_prev if d8 == 0 else [])
        g_tok = dma("G", gdram[d8], hB[:, d8 * B * TC:(d8 + 1) * B * TC],
                    [s_tok])
        g_spills.append(g_tok)

    def E_all(l, d8, j0, n, step=1):
        off = EOFF[l] + d8 * B * NLP[l]
        apx = Ebuf[:, off: off + B * NLP[l]].rearrange("p (b j) -> p b j",
                                                      j=NLP[l])
        if step == 1:
            return apx[:, :, j0: j0 + n]
        return apx[:, :, j0: j0 + (n - 1) * step + 1: step]

    def E_blk(l, d8, b, j0, n, step=1):
        off = EOFF[l] + (d8 * B + b) * NLP[l] + j0
        if step == 1:
            return Ebuf[:, off: off + n]
        return Ebuf[:, off: off + (n - 1) * step + 1: step]

    P2_END = len(P.items)
    p3bar = P.barrier()
    p = None
    for d8 in range(NCH):
        ld = dma("S", cvt3[:, :], _r3(wq_in)[:, d8],
                 (p3bar + g_spills + ns_done) if d8 == 0 else [p])
        p = vi(lambda h, d8=d8: h.tensor_copy(wbf[:, d8 * D:(d8 + 1) * D],
                                              cvt3[:, :]), [ld])
    wq_done = p

    E0_ready = []
    for b in range(B):
        ld = dma("S", xtl2.ap().rearrange("p (a c) -> p a c", c=TC + 1),
                 xs_in[b].transpose([1, 0, 2]),
                 p3bar + ns_done + (state["pB_readers"] if b else []))
        xn_toks = emit_ln(xtl2, xbf3, sqb3, 0,
                          lambda d8: actB[:, d8 * TC:(d8 + 1) * TC], ld)
        for et in range(NCH):
            mm = mmg(pX[et % 2][:, 0:TC],
                     [(wbf[:, k8 * D + et * 128: k8 * D + et * 128 + 128],
                       actB[:, k8 * TC:(k8 + 1) * TC]) for k8 in range(NCH)],
                     xn_toks + [wq_done, state["pX_evac"][et % 2]])
            tok = vi(lambda h, et=et, b=b: h.tensor_copy(
                E_blk(0, et, b, 0, TC), pX[et % 2][:, 0:TC]), [mm])
            ca = ci(lambda h, et=et, b=b: h.activation(
                fscr2[:, 0:1], E_blk(0, et, b, 0, 1), AF.Copy,
                scale=m0a.ap()), [tok, t_m, c_i2])
            cb = ci(lambda h, et=et, b=b: h.activation(
                E_blk(0, et, b, 0, 1), fscr2[:, 0:1], AF.Identity,
                scale=cstf[:, 3:4], bias=idvm[:, et:et + 1]), [ca])
            state["pX_evac"][et % 2] = tok
            E0_ready.append(cb)
            E0_ready.append(tok)
        xn2 = emit_ln(xtl2, xbf3, sqb3, 1,
                      lambda d8: actB[:, d8 * TC:(d8 + 1) * TC], tok,
                      extra=[state["pX_evac"][0], state["pX_evac"][1]])
        for et in range(NCH):
            mm = mmg(pX[et % 2][:, 0:1],
                     [(wbf[:, k8 * D + et * 128: k8 * D + et * 128 + 128],
                       actB[:, k8 * TC + TC - 1: k8 * TC + TC])
                      for k8 in range(NCH)],
                     xn2 + [state["pX_evac"][et % 2]])
            tok2 = vi(lambda h, et=et, b=b: h.tensor_copy(
                qlastB[:, (et * B + b) * 2: (et * B + b) * 2 + 1],
                pX[et % 2][:, 0:1]), [mm])
            state["pX_evac"][et % 2] = tok2
    state["E_ready"][0] = list(E0_ready) + ns_done
    if dbg:
        dbg_E0 = nc.dram_tensor("dbg_E0", [128, E_COLS], BF16,
                                kind="ExternalOutput").ap()
        tok_dump = dma("S", dbg_E0[:, :], Ebuf[:, :], P.barrier())
        state["E_ready"][0] = state["E_ready"][0] + [tok_dump]

    def comb(m, lhs, rhs, outw, blend=False, in_ready=(), sqoff=0):
        if m > 512:
            per_b = m // B
            nb = max(1, 512 // per_b)
            outs = []
            for b0 in range(0, B, nb):
                ne = min(nb, B - b0)

                def wrap(f, b0=b0, ne=ne):
                    return lambda k8: f(k8)[:, b0:b0 + ne, :]
                outs += comb(per_b * ne, wrap(lhs), wrap(rhs), wrap(outw),
                             blend, in_ready, sqoff)
            return outs
        outs = []
        ztoks = []
        for d8 in range(NCH):
            deps = [state["pX_evac"][d8 % 2]] + list(in_ready)
            pairs = [(P1T[:, k8 * D + d8 * 128: k8 * D + d8 * 128 + 128],
                      lhs(k8)) for k8 in range(NCH)]
            pairs += [(P2T[:, k8 * D + d8 * 128: k8 * D + d8 * 128 + 128],
                       rhs(k8)) for k8 in range(NCH)]
            mm = mmg(pX[d8 % 2][:, 0:m], pairs, deps)
            ev = vi(lambda h, d8=d8, m=m: h.tensor_copy(
                zscr[:, d8 * TC: d8 * TC + m], pX[d8 % 2][:, 0:m]), [mm])
            evc = ci(lambda h, d8=d8, m=m: h.activation(
                sqb3[:, d8 * TCP: d8 * TCP + m], pX[d8 % 2][:, 0:m],
                AF.Square), [mm, ev, state.get("last_s0")])
            state["pX_evac"][d8 % 2] = evc
            ztoks.append(evc)
        s0 = mmg(pR0[0:1, 0:m],
                 [(sonesbf[:], sqb3[:, d8 * TCP: d8 * TCP + m])
                  for d8 in range(NCH)],
                 [state["pR_reader"]] + ztoks)
        state["last_s0"] = s0
        cst = ci(lambda h, m=m: h.activation(rowS[0:1, 0:m], pR0[0:1, 0:m],
                                             AF.Sqrt, scale=cstf[0:1, 0:1],
                                             bias=cstf[0:1, 2:3]), [s0, c0])
        state["pR_reader"] = cst
        v1 = vi(lambda h, m=m: h.reciprocal(rowS[0:1, 0:m], rowS[0:1, 0:m]),
                [cst])
        bc = mmg(pA[0][:, 0:m], [(sonesr[:], rowS[0:1, 0:m])],
                 [v1] + state["pB_readers"])
        state["pB_readers"] = []
        for d8 in range(NCH):
            if not blend:
                tok = vi(lambda h, d8=d8, m=m: h.tensor_mul(
                    outw(d8), zscr[:, d8 * TC: d8 * TC + m],
                    pA[0][:, 0:m]), [bc])
            else:
                v_a = vi(lambda h, d8=d8, m=m: h.tensor_mul(
                    fscr[:, 0:m], zscr[:, d8 * TC: d8 * TC + m],
                    pA[0][:, 0:m]),
                    [bc] + ([state["blend_c"]] if state.get("blend_c") else []))
                c_a = ci(lambda h, m=m: h.activation(
                    fscr2[:, 0:m], fscr[:, 0:m], AF.Copy, scale=m0a.ap()),
                    [v_a, t_m] + ([state["blend_v"]]
                                  if state.get("blend_v") else []))
                c_b = ci(lambda h, d8=d8, m=m: h.activation(
                    fscr3[:, 0:m], outw(d8), AF.Copy, scale=m0bv.ap()), [c_a])
                tok = vi(lambda h, d8=d8, m=m: h.tensor_add(
                    outw(d8), fscr2[:, 0:m], fscr3[:, 0:m]), [c_b])
                state["blend_c"] = c_a
                state["blend_v"] = tok
            outs.append(tok)
        state["pB_readers"] = outs
        return outs

    for l in range(1, 10):
        n = NL[l]
        state["E_ready"][l] = comb(
            B * n,
            lambda k8, l=l, n=n: E_all(l - 1, k8, 0, n, 2),
            lambda k8, l=l, n=n: E_all(l - 1, k8, 1, n, 2),
            lambda d8, l=l, n=n: E_all(l, d8, 0, n),
            in_ready=state["E_ready"][l - 1])
        if dbg and l == 6:
            dbg_row = nc.dram_tensor("dbg_row", [1, 3 * TC], F32,
                                     kind="ExternalOutput").ap()
            dbg_sq = nc.dram_tensor("dbg_sq", [128, NCH * (TC + 2)], BF16,
                                    kind="ExternalOutput").ap()
            dbg_z = nc.dram_tensor("dbg_z", [128, NCH * TC], BF16,
                                   kind="ExternalOutput").ap()
            td1 = dma("S", dbg_row[:, :], rowS[:, :], P.barrier())
            td2 = dma("S", dbg_sq[:, :], sqb3[:, :], [td1])
            td3 = dma("S", dbg_z[:, :], zscr[:, :], [td2])
            state["E_ready"][l] = state["E_ready"][l] + [td3]

    if dbg:
        dbg_Er = nc.dram_tensor("dbg_Er", [128, E_COLS], BF16,
                                kind="ExternalOutput").ap()
        tok_dump2 = dma("S", dbg_Er[:, :], Ebuf[:, :], P.barrier())
        for _l in range(10):
            state["E_ready"][_l] = state["E_ready"].get(_l, []) + [tok_dump2]
    sx = dmas("S", sc_gin[:, :, :],
              Ebuf[:, EOFF[9]: EOFF[9] + NCH * B * 2].rearrange(
                  "p (a b t) -> p a b t", b=B, t=2)[:, :, :, 0],
              state["E_ready"][9])
    kx = P.add("K", lambda h: h.collective_compute(
        "AllGather", ALU.bypass, replica_groups=G8,
        ins=[sc_gin[:, :, :]], outs=[sc_gout[:, :, :, :]]), [sx])
    e9_toks = []
    for ci_ in range(8):
        out_ap = e9all[:, ci_ * NCH * B: (ci_ + 1) * NCH * B].rearrange(
            "p (a b) -> p a b", b=B)
        tok = dma("S", out_ap, sc_gout[ci_], [kx])
        e9_toks.append(tok)
    if dbg:
        dbg_e9a = nc.dram_tensor("dbg_e9a", [128, NCH * B * 8], BF16,
                                 kind="ExternalOutput").ap()
        tk9 = dma("S", dbg_e9a[:, :], e9all[:, :], P.barrier())
        e9_toks = e9_toks + [tk9]

    def e9g(k8, i0, n, step=1):
        apx = e9all.ap().rearrange("p (c a b) -> p c a b", a=NCH, b=B
                                   )[:, :, k8, :].transpose([0, 2, 1])
        return apx[:, :, i0: i0 + (n - 1) * step + 1: step] if step > 1 \
            else apx[:, :, i0: i0 + n]

    def emg(k8, i0, n, step=1):
        apx = emid[:, k8 * B * 8: (k8 + 1) * B * 8].rearrange(
            "p (b i) -> p b i", i=8)
        return apx[:, :, i0: i0 + (n - 1) * step + 1: step] if step > 1 \
            else apx[:, :, i0: i0 + n]

    def rmidall(i0, n, step=1):
        apx = rmid.ap().rearrange("p (ab i) -> p ab i", i=12)
        if step == 1:
            return apx[:, :, i0: i0 + n]
        return apx[:, :, i0: i0 + (n - 1) * step + 1: step]

    def emidall(i0, n, step=1):
        apx = emid.ap().rearrange("p (ab i) -> p ab i", i=8)
        if step == 1:
            return apx[:, :, i0: i0 + n]
        return apx[:, :, i0: i0 + (n - 1) * step + 1: step]

    def e9all_c(i0, n, step=1):
        apx = e9all.ap().rearrange("p (i ab) -> p i ab", ab=NCH * B
                                   ).transpose([0, 2, 1])
        if step == 1:
            return apx[:, :, i0: i0 + n]
        return apx[:, :, i0: i0 + (n - 1) * step + 1: step]

    def rmg(k8, i0, n, step=1):
        apx = rmid[:, k8 * B * 12: (k8 + 1) * B * 12].rearrange(
            "p (b i) -> p b i", i=12)
        return apx[:, :, i0: i0 + (n - 1) * step + 1: step] if step > 1 \
            else apx[:, :, i0: i0 + n]

    m10 = comb(B * 4, lambda k8: e9g(k8, 0, 4, 2), lambda k8: e9g(k8, 1, 4, 2),
               lambda d8: emg(d8, 0, 4), in_ready=e9_toks)
    m11 = comb(B * 2, lambda k8: emg(k8, 0, 2, 2), lambda k8: emg(k8, 1, 2, 2),
               lambda d8: emg(d8, 4, 2), in_ready=m10)
    m12 = comb(B, lambda k8: emg(k8, 4, 1), lambda k8: emg(k8, 5, 1),
               lambda d8: emg(d8, 6, 1), in_ready=m11)
    m10d = comb(B, lambda k8: emg(k8, 4, 1), lambda k8: emg(k8, 2, 1),
                lambda d8: rmg(d8, 2, 1), in_ready=m12)
    cpy1a = gcp(rmidall(0, 1), emidall(0, 1), m12 + m10d)
    cpy1b = gcp(rmidall(1, 1), emidall(4, 1), [])
    cpy1c = gcp(rmidall(3, 1), emidall(6, 1), [])
    cpy1 = cpy1c
    m9d = comb(B * 3, lambda k8: rmg(k8, 0, 3), lambda k8: e9g(k8, 2, 3, 2),
               lambda d8: rmg(d8, 6, 3, 2), in_ready=m10d + [cpy1])
    cpy2a = gcp(rmidall(4, 1), e9all_c(0, 1), m9d)
    cpy2b = gcp(rmidall(5, 4, 2), rmidall(0, 4), [])
    cpy2 = cpy2b
    tr_toks = []
    sel_back = []
    for bb in range(B):
        tr_toks = []
        for d8 in range(NCH):
            pTb = pT[(bb * NCH + d8) % 2].ap().bitcast(BF16)
            tk = P.add("T", lambda h, bb=bb, d8=d8, pTb=pTb: h.transpose(
                pTb[0:8, 0:128],
                rmid[:, d8 * B * 12 + bb * 12 + 4: d8 * B * 12 + bb * 12 + 12],
                sidbf[:, :]),
                [cpy2, state["pT_trans"][(bb * NCH + d8) % 2],
                 state["pR_reader"], c0] + state.get("r9tm_guard", []))
            ck = ci(lambda h, bb=bb, d8=d8, pTb=pTb: h.copy(
                r9tm[0:8, d8 * 128: (d8 + 1) * 128],
                pTb[0:8, 0:128]), [tk])
            state["pT_trans"][(bb * NCH + d8) % 2] = ck
            tr_toks.append(ck)
        selmms = []
        for hh in range(2):
            mm = mmg(pX[hh][0:2, 0:TC],
                     [(selbf[:, :], r9tm[0:8, hh * TC:(hh + 1) * TC])],
                     tr_toks + [state["pX_evac"][hh]])
            tok = vi(lambda h, hh=hh: h.tensor_copy(
                r9tm[0:2, hh * TC:(hh + 1) * TC],
                pX[hh][0:2, 0:TC]), [mm])
            state["pX_evac"][hh] = tok
            selmms.append(tok)
        for d8 in range(NCH):
            pTb = pT[(bb * NCH + d8) % 2].ap().bitcast(BF16)
            tk = P.add("T", lambda h, d8=d8, pTb=pTb: h.transpose(
                pTb[0:128, 0:2],
                r9tm[0:2, d8 * 128: (d8 + 1) * 128],
                sidbf[0:2, 0:2]),
                selmms + [state["pT_trans"][(bb * NCH + d8) % 2]])
            ck = ci(lambda h, bb=bb, d8=d8, pTb=pTb: h.copy(
                r9sel[:, (d8 * B + bb) * 2: (d8 * B + bb) * 2 + 2],
                pTb[0:128, 0:2]), [tk])
            state["pT_trans"][(bb * NCH + d8) % 2] = ck
            sel_back.append(ck)
        state["r9tm_guard"] = sel_back[-NCH:]
    state["pR_reader"] = sel_back[-1]

    def E_4d(l, j0, n, step=1):
        apx = Ebuf[:, EOFF[l]: EOFF[l] + NCH * B * NLP[l]].rearrange(
            "p (ab j) -> p ab j", j=NLP[l])
        if step == 1:
            return apx[:, :, j0: j0 + n]
        return apx[:, :, j0: j0 + (n - 1) * step + 1: step]

    def r9sel_4d(which):
        apx = r9sel.ap().rearrange("p (ab w) -> p ab w", w=2)
        return apx[:, :, which:which + 1]

    def r9sel_col(k8, which):
        apx = r9sel[:, k8 * B * 2: (k8 + 1) * B * 2].rearrange(
            "p (b w) -> p b w", w=2)
        return apx[:, :, which:which + 1]

    MB = 9 * B
    all_er = sum((state["E_ready"][l] for l in range(10)), [])
    stage_deps = sel_back + [t_m] + all_er

    def stg_l(k8):
        return actB[:, k8 * TC: k8 * TC + MB]

    def stg_r(k8):
        return actB[:, k8 * TC + 64: k8 * TC + 64 + MB]

    aL3 = actB.ap().rearrange("p (a c) -> p a c", c=TC)
    selw0 = r9sel.ap().rearrange("p (a b w) -> p a b w", b=B, w=2)[:, :, :, 0]
    stg_toks = []
    for l in range(8, -1, -1):
        li = 8 - l
        ev = Ebuf[:, EOFF[l]: EOFF[l] + NCH * B * NLP[l]].rearrange(
            "p (a b j) -> p a b j", b=B, j=NLP[l])[:, :, :, 0]
        t1 = gcp(aL3[:, :, li * B: (li + 1) * B], selw0, stage_deps)
        t2 = gcp(aL3[:, :, 64 + li * B: 64 + (li + 1) * B], ev, [t1])
        stg_toks.append(t2)
    mb_toks = comb(MB, stg_l, stg_r, lambda d8: stg_r(d8), blend=True,
                   in_ready=stg_toks + sel_back + [t_m])
    sc_tok = {}
    for l in range(8, -1, -1):
        li = 8 - l
        ev = Ebuf[:, EOFF[l]: EOFF[l] + NCH * B * NLP[l]].rearrange(
            "p (a b j) -> p a b j", b=B, j=NLP[l])[:, :, :, 0]
        sc_tok[l] = gcp(ev, aL3[:, :, 64 + li * B: 64 + (li + 1) * B],
                        mb_toks)
    sc_all = [sc_tok[l] for l in range(9)]

    prev_ready = sel_back + sc_all
    for l in range(8, -1, -1):
        n = NL[l]
        half = NL[l + 1]
        if l == 8:
            ps = gcp(E_4d(8, 1, 1, 2), r9sel_4d(1),
                     prev_ready + state["E_ready"][8])
        else:
            ps = gcp(E_4d(l, 1, half, 2), E_4d(l + 1, 0, half),
                     prev_ready + state["E_ready"][l] +
                     [sc_tok[l + 1], sc_tok[l]])
        ready = [ps, sc_tok[l]]
        if n // 2 >= 2:
            dn = comb(B * (n // 2 - 1),
                      lambda k8, l=l, n=n: E_all(l + 1, k8, 0, n // 2 - 1),
                      lambda k8, l=l, n=n: E_all(l, k8, 2, n // 2 - 1, 2),
                      lambda d8, l=l, n=n: E_all(l, d8, 2, n // 2 - 1, 2),
                      in_ready=prev_ready + state["E_ready"][l] +
                      [sc_tok[l + 1]])
            ready += dn
        prev_ready = ready
        state["E_ready"][l] = ready
    qs_ready = prev_ready + sc_all
    xcol_toks = comb(B, lambda k8: r9sel_col(k8, 1),
                     lambda k8: qlastB[:, k8 * B * 2: (k8 + 1) * B * 2: 2],
                     lambda d8: xcolB[:, d8 * B * 2: (d8 + 1) * B * 2: 2],
                     in_ready=sel_back + state["E_ready"][0])

    p5bar = qs_ready + xcol_toks
    prev_outs = []
    prev_ymm = None
    for b in range(B):
        p = None
        for d8 in range(NCH):
            deps0 = []
            if d8 == 0:
                deps0 = (p5bar if b == 0 else []) +                     ([prev_ymm] if prev_ymm else [])
            ld = dma("S", cvt3[:, :], _r3(wv_in)[:, d8],
                     deps0 + ([p] if p else []))
            p = vi(lambda h, d8=d8: h.tensor_copy(
                wbf[:, d8 * D:(d8 + 1) * D], cvt3[:, :]), [ld])
        wv_done = p
        ldx = dma("S", xtl2.ap().rearrange("p (a c) -> p a c", c=TC + 1),
                  xs_in[b].transpose([1, 0, 2]),
                  state["pB_readers"] + prev_outs +
                  (sc_all if b == 0 else []))
        xn_toks = emit_ln(xtl2, xbf3, sqb3, 1,
                          lambda d8: actB[:, d8 * TC:(d8 + 1) * TC], ldx)
        v_toks = []
        for et in range(NCH):
            mm = mmg(pX[et % 2][:, 0:TC],
                     [(wbf[:, k8 * D + et * 128: k8 * D + et * 128 + 128],
                       actB[:, k8 * TC:(k8 + 1) * TC]) for k8 in range(NCH)],
                     xn_toks + [wv_done, state["pX_evac"][et % 2]])
            tok = vi(lambda h, et=et: h.tensor_copy(
                vtile[:, et * TC:(et + 1) * TC], pX[et % 2][:, 0:TC]), [mm])
            state["pX_evac"][et % 2] = tok
            v_toks.append(tok)
        qsh3 = qsh.ap().rearrange("p (a t) -> p a t", t=TC)
        E03 = Ebuf[:, EOFF[0]: EOFF[0] + NCH * B * TC].rearrange(
            "p (a b j) -> p a b j", b=B, j=TC)
        qd1 = gcp(qsh3[:, :, 0:TC - 1], E03[:, :, b, 1:TC],
                  (p5bar if b == 0 else []) + state.get("qsh_guard", []))
        qd2 = gcp(qsh3[:, :, TC - 1:TC],
                  xcolB.ap().rearrange("p (a q) -> p a q", q=B * 2
                                       )[:, :, b * 2: b * 2 + 1], [qd1])
        quv_toks = []
        for d8 in range(NCH):
            tok = vi(lambda h, d8=d8: h.tensor_mul(
                actB[:, d8 * TC: (d8 + 1) * TC],
                qsh[:, d8 * TC: (d8 + 1) * TC],
                vtile[:, d8 * TC: (d8 + 1) * TC]),
                [v_toks[d8], qd2])
            quv_toks.append(tok)
        state["qsh_guard"] = quv_toks
        p = None
        for d8 in range(NCH):
            ld = dma("S", cvt3[:, :], _r3(wc_in)[:, d8],
                     [quv_toks[-1]] if d8 == 0 else [p])
            p = vi(lambda h, d8=d8: h.tensor_copy(
                wbf[:, d8 * D:(d8 + 1) * D], cvt3[:, :]), [ld])
        wc_done = p
        gld = dma("G", gt2.ap().rearrange("p (a c) -> p a c", c=TC),
                  gdram[:, :, b * TC:(b + 1) * TC].transpose([1, 0, 2]),
                  (g_spills if b == 0 else []) + prev_outs)
        od_prev = None
        cur_outs = []
        for et in range(NCH):
            mm = mmg(pX[et % 2][:, 0:TC],
                     [(wbf[:, k8 * D + et * 128: k8 * D + et * 128 + 128],
                       actB[:, k8 * TC:(k8 + 1) * TC]) for k8 in range(NCH)],
                     quv_toks + [wc_done, state["pX_evac"][et % 2]])
            prev_ymm = mm
            ot_tok = vi(lambda h, et=et, b=b: (
                h.tensor_mul(otile[:, :], pX[et % 2][:, 0:TC],
                             gt2[:, et * TC:(et + 1) * TC]),
                h.tensor_add(otile[:, :], otile[:, :],
                             xtl2[:, et * (TC + 1) + 1: et * (TC + 1) + 1 + TC])
            )[-1], [mm, gld] + ([od_prev] if od_prev else []))
            state["pX_evac"][et % 2] = ot_tok
            od_prev = dma("S", out_d[b, et], otile[:, :], [ot_tok])
            cur_outs.append(ot_tok)
        prev_outs = cur_outs + [od_prev]

    if dbg:
        dbg_pt = nc.dram_tensor("dbg_pt", [2, D, D], BF16, kind="ExternalOutput").ap()
        dbg_g = nc.dram_tensor("dbg_g", [NCH, 128, B * TC], BF16, kind="ExternalOutput").ap()
        dbg_E = nc.dram_tensor("dbg_E", [128, E_COLS], BF16, kind="ExternalOutput").ap()
        dbg_sel = nc.dram_tensor("dbg_sel", [128, NCH * B * 2], BF16, kind="ExternalOutput").ap()
        dbg_xc = nc.dram_tensor("dbg_xc", [128, NCH * B * 2], BF16, kind="ExternalOutput").ap()
        dbg_ql = nc.dram_tensor("dbg_ql", [128, NCH * B * 2], BF16, kind="ExternalOutput").ap()
        dbg_e9 = nc.dram_tensor("dbg_e9", [128, NCH * B * 8], BF16, kind="ExternalOutput").ap()
        dbg_sg = nc.dram_tensor("dbg_sg", [8, 128, NCH, B], BF16, kind="ExternalOutput").ap()
        dbg_sgi = nc.dram_tensor("dbg_sgi", [128, NCH, B], BF16, kind="ExternalOutput").ap()
        bar = P.barrier()
        dma("S", dbg_pt[:, :, :], pboth[:, :, :], bar)
        dma("S", dbg_g[:, :, :], gdram[:, :, :], bar)
        dma("S", dbg_E[:, :], Ebuf[:, :], bar)
        dma("S", dbg_sel[:, :], r9sel[:, :], bar)
        dma("S", dbg_xc[:, :], xcolB[:, :], bar)
        dma("S", dbg_ql[:, :], qlastB[:, :], bar)
        dma("S", dbg_e9[:, :], e9all[:, :], bar)
        dma("S", dbg_sg[:, :, :, :], sc_gout[:, :, :, :], bar)
        dma("S", dbg_sgi[:, :, :], sc_gin[:, :, :], bar)
    FIN = dict(S=P.count["S"], G=P.count["G"], K=P.count["K"],
               T=P.count["T"], V=P.count["V"], C=P.count["C"])

    csum = []
    acc = 0
    for c_ in P.tcost[:P2_END]:
        acc += c_
        csum.append(acc)
    cuts = [0]
    j = 0
    for k in range(1, NIT):
        tgt = acc * k / NIT
        while j < P2_END and csum[j] < tgt:
            j += 1
        cuts.append(j)
    cuts.append(P2_END)

    block = e(nc.Block())

    @block.sync
    def _(sync):
        sync.dma_start(out=Xf.ap().rearrange("p (a d) -> p a d", d=D),
                       in_=_r3(nsW_in)).then_inc(dL, 16)
        sync.dma_start(out=Xse.ap().rearrange("p (a c) -> p a c", c=CS),
                       in_=_r3(nsWs_in)).then_inc(dL, 16)
        sync.dma_start(out=sid[:], in_=id_in[:]).then_inc(dL, 16)
        sync.dma_start(out=m0v[:, :], in_=m0_in).then_inc(dL, 16)
        for k in range(NIT):
            sync.wait_ge(vU, 8 * (k + 1))
            src = XsE(k).ap()
            if XsE(k) is Xsr:
                src = src.bitcast(F32)
            sync.dma_start(out=_r3(gin),
                           in_=src.rearrange("p (a c) -> p a c", c=CS)
                           ).then_inc(dL, 16)
            P.replay("S", sync, SEMS, lo=cuts[k], hi=cuts[k + 1])
        sync.wait_ge(nX, 7)
        sync.dma_start(out=_r3(pexch),
                       in_=PBF.ap().rearrange("p (a d) -> p a d", d=D)
                       ).then_inc(dL, 16)
        P.replay("S", sync, SEMS, lo=P2_END)
        sync.wait_ge(eS, FIN["S"])

    @block.vector
    def _(vector):
        vector.wait_ge(dL, 64)
        vector.memset(sones[:], 1.0)
        vector.memset(sonesr[:], 1.0)
        for r8 in range(NCH):
            xb = Xf[:, r8 * D:(r8 + 1) * D]
            vector.tensor_mul(ssq, xb, xb)
            vector.tensor_reduce(sstat[:, r8:r8 + 1], ssq, AX.X, ALU.add)
        vector.tensor_reduce(sstat1, sstat, AX.X, ALU.add).then_inc(nX, 1)
        vector.wait_ge(nX, 2)
        vector.reciprocal(snrm_r, snrm_s).then_inc(nX, 1)
        vector.wait_ge(nX, 4)
        vector.tensor_scalar(Xse[:], Xse[:], sscal, None, ALU.mult
                             ).then_inc(nX, 1)
        for k in range(NIT):
            a_k, b_k = NS_SCHED[k]
            for d1 in range(NCH):
                vector.wait_ge(tG, 2 + 8 * k + d1 + 1)
                dst = AmK(k)[:, d1 * CS:(d1 + 1) * CS]
                if k == 0:
                    vector.tensor_scalar(dst, pAs[d1 % 2], sscal, None,
                                         ALU.mult).then_inc(vA, 1)
                else:
                    vector.tensor_copy(dst, pAs[d1 % 2]).then_inc(vA, 1)
            for r8 in range(NCH):
                vector.wait_ge(tU, 8 * k + r8 + 1)
                sl = slice(r8 * CS, (r8 + 1) * CS)
                srcb = XsE(k - 1)[:, sl]
                dstb = XsE(k)[:, sl]
                vector.tensor_scalar(dstb, srcb, a_k, None, ALU.mult)
                vector.tensor_scalar(pUs[r8 % 2], pUs[r8 % 2], b_k,
                                     None, ALU.mult)
                if k == 0:
                    vector.tensor_scalar(pUs[r8 % 2], pUs[r8 % 2], sscal,
                                         None, ALU.mult)
                vector.tensor_add(dstb, dstb, pUs[r8 % 2]).then_inc(vU, 1)
            P.replay("V", vector, SEMS, lo=cuts[k], hi=cuts[k + 1])
        vector.wait_ge(cT, 64 * (NIT + 1))
        vector.tensor_copy(PBF[:, :], XTr.ap().bitcast(F32)).then_inc(nX, 2)
        P.replay("V", vector, SEMS, lo=P2_END)

    @block.scalar
    def _(scalar):
        scalar.wait_ge(tG, 1)
        scalar.activation(snrm_s, pN1, AF.Sqrt).then_inc(nX, 1)
        scalar.wait_ge(tG, 2)
        scalar.copy(sscal, pN2).then_inc(nX, 1)
        for r in range(NIT + 1):
            if r >= 1:
                scalar.wait_ge(tU, 8 * r)
            for t in range(64):
                scalar.wait_ge(tT, 64 * r + t + 1)
                dst = tr_dst(t, r)
                if r < NSPLIT:
                    dst = dst.bitcast(F32)
                scalar.copy(dst, pTs[t % 2]).then_inc(cT, 1)
            if 1 <= r < NIT:
                P.replay("C", scalar, SEMS, lo=cuts[r - 1], hi=cuts[r])
        P.replay("C", scalar, SEMS, lo=cuts[NIT - 1])

    @block.gpsimd
    def _(gpsimd):
        for k in range(NIT):
            gpsimd.wait_ge(dL, 16 * (4 + k + 1))
            gpsimd.collective_compute(
                "AllGather", ALU.bypass, replica_groups=G4,
                ins=[gin[:, :]], outs=[gout[:, :, :]],
            ).then_inc(cc, 1)
            P.replay("G", gpsimd, SEMS, lo=cuts[k - 1] if k >= 1 else 0,
                     hi=cuts[k])
            gpsimd.wait_ge(cc, k + 1)
            gpsimd.wait_ge(tG, 2 + 8 * (k + 1))
            gpsimd.wait_ge(tT, 64 * (k + 1))
            tgt = Xfr if (k + 1) >= NSPLIT else Xf
            for g in range(4):
                gpsimd.dma_start(
                    out=tgt.ap().rearrange("p (a d) -> p a d", d=D
                                           )[:, :, g * CS:(g + 1) * CS],
                    in_=_r3(gout[g]),
                ).then_inc(dRs, 16)
        gpsimd.wait_ge(dL, 16 * (4 + NIT + 1))
        gpsimd.collective_compute(
            "AllGather", ALU.bypass, replica_groups=GPAIR,
            ins=[pexch[:, :]], outs=[pboth[:, :, :]],
        ).then_inc(cc, 1)
        gpsimd.wait_ge(cc, NIT + 1)
        gpsimd.dma_start(out=P1T.ap().rearrange("p (a d) -> p a d", d=D),
                         in_=_r3(pboth[0])).then_inc(dRs, 16)
        gpsimd.dma_start(out=P2T.ap().rearrange("p (a d) -> p a d", d=D),
                         in_=_r3(pboth[1])).then_inc(dRs, 16)
        gpsimd.wait_ge(dRs, 64 * NIT + 32)
        gpsimd.memset(smallf[0:1, 63:64], 0.0).then_inc(nX, 1)
        P.replay("GKH", gpsimd, SEMS, lo=cuts[NIT - 1])
        gpsimd.wait_ge(eG, FIN["G"])
        gpsimd.wait_ge(eK, FIN["K"])

    @block.tensor
    def _(tensor):
        tensor.wait_ge(nX, 1)
        tensor.matmul(pN1, sones[:], sstat1, start=True, stop=True
                      ).then_inc(tG, 1)
        tensor.wait_ge(nX, 3)
        tensor.matmul(pN2, sonesr[:], snrm_r, start=True, stop=True
                      ).then_inc(tG, 1)
        tensor.wait_ge(dL, 64)
        tensor.wait_ge(nX, 4)
        for t in range(64):
            if t >= 2:
                tensor.wait_ge(cT, t - 1)
            tensor.transpose(pTs[t % 2], tr_src(t, 0).bitcast(F32)
                             if False else tr_src(t, 0),
                             sid.ap().bitcast(F32)).then_inc(tT, 1)
        for k in range(NIT):
            P.replay("T", tensor, SEMS, lo=cuts[k - 1] if k >= 1 else 0,
                     hi=cuts[k])
            late = k >= NSPLIT
            tensor.wait_ge(dRs, 64 * k)
            tensor.wait_ge(vU, 8 * k)
            if k == 0:
                tensor.wait_ge(nX, 5)
            for d1 in range(NCH):
                if d1 >= 2:
                    tensor.wait_ge(vA, 8 * k + d1 - 1)
                ins = None
                for r8 in range(NCH):
                    o = r8 * D + d1 * 128
                    lhs = XfR(k)[:, o:o + 128]
                    rhs = XsE(k - 1)[:, r8 * CS:(r8 + 1) * CS]
                    ins = tensor.matmul(pAs[d1 % 2], lhs, rhs,
                                        start=(r8 == 0), stop=(r8 == NCH - 1))
                ins.then_inc(tG, 1)
            tensor.wait_ge(vA, 8 * (k + 1))
            tensor.wait_ge(cT, 64 * (k + 1))
            for r8 in range(NCH):
                if r8 >= 2:
                    tensor.wait_ge(vU, 8 * k + r8 - 1)
                ins = None
                for d8 in range(NCH):
                    o = d8 * D + r8 * 128
                    ins = tensor.matmul(pUs[r8 % 2], XTk(k)[:, o:o + 128],
                                        AmK(k)[:, d8 * CS:(d8 + 1) * CS],
                                        start=(d8 == 0), stop=(d8 == NCH - 1))
                ins.then_inc(tU, 1)
            tensor.wait_ge(dRs, 64 * (k + 1))
            rr = k + 1
            for t in range(64):
                tensor.wait_ge(cT, 64 * rr + t - 1)
                out_ap = (pTs[t % 2].bitcast(F32R) if rr >= NSPLIT
                          else pTs[t % 2])
                id_ap = sid.ap() if rr >= NSPLIT else sid.ap().bitcast(F32)
                tensor.transpose(out_ap, tr_src(t, rr), id_ap).then_inc(tT, 1)
        P.replay("T", tensor, SEMS, lo=cuts[NIT - 1])

    build.P = P
    build.cuts = cuts
    build.P2_END = P2_END
    build.FIN = FIN
    st.pop_all()
    return nc


def kernel(**inputs):
    x = np.asarray(inputs["L0_attn_x"], np.float32)
    Wq = np.asarray(inputs["L0_attn_Wq"], np.float32)
    Wv = np.asarray(inputs["L0_attn_Wv"], np.float32)
    Wc = np.asarray(inputs["L0_attn_Wc"], np.float32)
    Wp1 = np.asarray(inputs["L0_attn_Wp1"], np.float32)
    Wp2 = np.asarray(inputs["L0_attn_Wp2"], np.float32)
    ident = np.asarray(inputs["L0_attn_identity"], np.float32)
    qw = np.asarray(inputs["L0_ffn_qweights"], np.int32)
    scales = np.asarray(inputs["L0_ffn_scales"], np.float32)
    biases = np.asarray(inputs["L0_ffn_biases"], np.float32)
    la = np.asarray(inputs["L0_ffn_lora_a"], np.float32)
    lb = np.asarray(inputs["L0_ffn_lora_b"], np.float32)
    gam = np.asarray(inputs["L0_ffn_ln_gamma"], np.float32)
    bet = np.asarray(inputs["L0_ffn_ln_beta"], np.float32)

    bf = ml_dtypes.bfloat16
    eye = np.eye(128, dtype=np.float32)
    qw_t = np.ascontiguousarray(qw.transpose(0, 2, 1))
    srep_t = np.ascontiguousarray(
        np.repeat(scales, 16, axis=2).transpose(0, 2, 1))
    sT_t = np.ascontiguousarray(scales.transpose(0, 2, 1))
    la_t = np.ascontiguousarray(la.transpose(0, 2, 1))
    lb_t = np.ascontiguousarray(lb.transpose(0, 2, 1))
    gam_fm = np.ascontiguousarray(
        gam.reshape(3, NCH, 128).transpose(2, 0, 1).reshape(128, 3 * NCH))
    bet_fm = np.ascontiguousarray(
        bet.reshape(3, NCH, 128).transpose(2, 0, 1).reshape(128, 3 * NCH))
    bias_fm = np.ascontiguousarray(
        biases.reshape(6, NCH, 128).transpose(2, 0, 1).reshape(128, 48))
    pool = np.zeros((NCH, 128, 64), np.float32)
    for d8 in range(NCH):
        for p_ in range(128):
            g_ = d8 * 8 + p_ // 16
            pool[d8, p_, g_] = 1.0
    idvec_fm = np.ascontiguousarray(
        ident.reshape(NCH, 128).T)
    wq_t = np.ascontiguousarray(Wq.T)
    wv_t = np.ascontiguousarray(Wv.T)
    wc_t = np.ascontiguousarray(Wc.T)

    xpad = np.concatenate([np.zeros((B, 1, D), np.float32), x], axis=1)

    nc = build()
    in_maps = []
    for c in range(8):
        W = Wp1 if c < 4 else Wp2
        s_ = c % 4
        xs = xpad[:, TC * c: TC * c + TC + 1, :]
        xs_fm = np.ascontiguousarray(
            xs.transpose(0, 2, 1).reshape(B, NCH, 128, TC + 1))
        sel = np.zeros((8, 2), np.float32)
        if c > 0:
            sel[c - 1, 0] = 1.0
        sel[c, 1] = 1.0
        m0 = np.zeros((128, 2), np.float32)
        m0[:, 0] = 0.0 if c == 0 else 1.0
        m0[:, 1] = 1.0 if c == 0 else 0.0
        in_maps.append({
            "xs": xs_fm, "wq_t": wq_t, "wv_t": wv_t, "wc_t": wc_t,
            "nsW": W, "nsWs": np.ascontiguousarray(W[:, CS * s_:CS * (s_ + 1)]),
            "ident128": eye, "qw_t": qw_t, "srep_t": srep_t, "sT_t": sT_t,
            "bias_t": biases, "bias_fm": bias_fm, "la_t": la_t, "lb_t": lb_t,
            "gam_fm": gam_fm, "bet_fm": bet_fm, "pool_m": pool,
            "sel_m": sel, "m0v": m0, "idvec_fm": idvec_fm,
        })
    res = run_bass_kernel_spmd(nc, in_maps, core_ids=list(range(8)))
    out = np.empty((B, 4096, D), np.float32)
    for c in range(8):
        o = res.results[c]["out"]
        out[:, TC * c: TC * (c + 1), :] = o.reshape(B, D, TC).transpose(0, 2, 1)
    return out


# version 07
# speedup vs baseline: 1.1020x; 1.1020x over previous
import numpy as np
import ml_dtypes
import concourse.bass as bass
import concourse.mybir as mybir
from concourse.bass_utils import run_bass_kernel_spmd
from contextlib import ExitStack

F32 = mybir.dt.float32
F32R = mybir.dt.float32r
BF16 = mybir.dt.bfloat16
I32 = mybir.dt.int32
ALU = mybir.AluOpType
AF = mybir.ActivationFunctionType
AX = mybir.AxisListType

D = 1024
CS = 256
NCH = 8
B = 4
TC = 512
NS_SCHED = [
    (5.1961444193, -5.1961418212),
    (2.5980691358, -0.6495181190),
    (2.5980578201, -0.6495166383),
    (2.5980284218, -0.6495127916),
    (2.5979520476, -0.6495027980),
    (2.5977536542, -0.6494768381),
    (2.5972384338, -0.6494094201),
    (2.5959013401, -0.6492344501),
    (2.5924371443, -0.6487810349),
    (2.5835036800, -0.6476114866),
    (2.5607332767, -0.6446274371),
    (2.5044075629, -0.6372283209),
    (2.3750656855, -0.6201377464),
    (2.1254870094, -0.5867349626),
    (1.7905710134, -0.5409002201),
    (1.5572128750, -0.5081481882),
    (1.5021226287, -0.5003031936),
]
NIT = len(NS_SCHED)
NSPLIT = 8
G4 = [[0, 1, 2, 3], [4, 5, 6, 7]]
GPAIR = [[0, 4], [1, 5], [2, 6], [3, 7]]
G8 = [[0, 1, 2, 3, 4, 5, 6, 7]]

NL = [TC >> l for l in range(10)]
NLP = [max(n, 2) for n in NL]
EOFF = []
_o = 0
for _l in range(10):
    EOFF.append(_o)
    _o += NCH * B * NLP[_l]
E_COLS = _o
RMS_EPS = 1e-8
LN_EPS = 1e-5
R75 = 1.0 / 7.5


def _r3(ap, p=128):
    return ap.rearrange("(a p) c -> a p c", p=p).transpose([1, 0, 2])


class Plan:
    def __init__(self):
        self.items = []
        self.count = {k: 0 for k in "TVCSGKH"}
        self.tcost = []

    def add(self, eng, fn, deps=(), cost=0):
        step = 16 if eng in "SG" else 1
        self.count[eng] += step
        tok = (eng, self.count[eng])
        self.items.append((eng, fn, [d for d in deps if d is not None], tok))
        self.tcost.append(cost if eng == "T" else 0)
        return tok

    def barrier(self):
        return [(k, v) for k, v in self.count.items() if v]

    def replay(self, keys, handle, sems, lo=0, hi=None):
        for eng, fn, deps, tok in self.items[lo:hi]:
            if eng not in keys:
                continue
            need = {}
            for k, val in deps:
                if k in keys:
                    continue
                sem = sems[k]
                kk = id(sem)
                need[kk] = (sem, max(val, need[kk][1]) if kk in need else val)
            for sem, val in need.values():
                handle.wait_ge(sem, val)
            fn(handle).then_inc(sems[eng], 16 if eng in "SG" else 1)


def build(dbg=False):
    nc = bass.Bass("TRN2", target_bir_lowering=False, debug=False)
    xs_in = nc.dram_tensor("xs", [B, NCH, 128, TC + 1], F32, kind="ExternalInput").ap()
    wq_in = nc.dram_tensor("wq_t", [D, D], F32, kind="ExternalInput").ap()
    wv_in = nc.dram_tensor("wv_t", [D, D], F32, kind="ExternalInput").ap()
    wc_in = nc.dram_tensor("wc_t", [D, D], F32, kind="ExternalInput").ap()
    nsW_in = nc.dram_tensor("nsW", [D, D], F32, kind="ExternalInput").ap()
    nsWs_in = nc.dram_tensor("nsWs", [D, CS], F32, kind="ExternalInput").ap()
    id_in = nc.dram_tensor("ident128", [128, 128], F32R, kind="ExternalInput").ap()
    qw_in = nc.dram_tensor("qw_t", [6, D, D], I32, kind="ExternalInput").ap()
    srep_in = nc.dram_tensor("srep_t", [6, D, D], F32, kind="ExternalInput").ap()
    sT_in = nc.dram_tensor("sT_t", [6, 64, D], F32, kind="ExternalInput").ap()
    bias_in = nc.dram_tensor("bias_t", [6, D], F32, kind="ExternalInput").ap()
    biasfm_in = nc.dram_tensor("bias_fm", [128, 48], F32, kind="ExternalInput").ap()
    la_in = nc.dram_tensor("la_t", [6, D, 32], F32, kind="ExternalInput").ap()
    lb_in = nc.dram_tensor("lb_t", [6, 32, D], F32, kind="ExternalInput").ap()
    gam_in = nc.dram_tensor("gam_fm", [128, 3 * NCH], F32, kind="ExternalInput").ap()
    bet_in = nc.dram_tensor("bet_fm", [128, 3 * NCH], F32, kind="ExternalInput").ap()
    pool_in = nc.dram_tensor("pool_m", [NCH, 128, 64], F32, kind="ExternalInput").ap()
    sel_in = nc.dram_tensor("sel_m", [8, 2], F32, kind="ExternalInput").ap()
    m0_in = nc.dram_tensor("m0v", [128, 2], F32, kind="ExternalInput").ap()
    idv_in = nc.dram_tensor("idvec_fm", [128, NCH], F32, kind="ExternalInput").ap()
    out_d = nc.dram_tensor("out", [B, NCH, 128, TC], F32, kind="ExternalOutput").ap()

    gin = nc.dram_tensor("ns_gin", [D, CS], F32).ap()
    gout = nc.dram_tensor("ns_gout", [4, D, CS], F32).ap()
    gin2 = nc.dram_tensor("ns_gin2", [2 * 128, D], F32).ap()
    gout2 = nc.dram_tensor("ns_gout2", [4, 2 * 128, D], F32).ap()
    pexch = nc.dram_tensor("ns_pexch", [D, D], BF16).ap()
    pboth = nc.dram_tensor("ns_pboth", [2, D, D], BF16).ap()
    gdram = nc.dram_tensor("g_spill", [NCH, 128, B * TC], BF16).ap()
    sc_gin = nc.dram_tensor("sc_gin", [128, NCH, B], BF16).ap()
    sc_gout = nc.dram_tensor("sc_gout", [8, 128, NCH, B], BF16).ap()

    st = ExitStack()
    e = st.enter_context

    sid = e(nc.sbuf_tensor("sid", [128, 128], F32R))
    sones = e(nc.sbuf_tensor("sones", [128, 1], F32))
    sonesr = e(nc.sbuf_tensor("sonesr", [1, 128], F32))
    sonebf = e(nc.sbuf_tensor("sonebf", [1, TC], BF16))
    sonesbf = e(nc.sbuf_tensor("sonesbf", [128, 1], BF16))
    m0v = e(nc.sbuf_tensor("m0v_s", [128, 2], F32))
    m0a = e(nc.sbuf_tensor("m0a_s", [128, 1], F32))
    m0bv = e(nc.sbuf_tensor("m0b_s", [128, 1], F32))
    selbf = e(nc.sbuf_tensor("selbf", [8, 2], BF16))
    idvbf = e(nc.sbuf_tensor("idvbf", [128, 2 * NCH], BF16))
    sidbf = e(nc.sbuf_tensor("sidbf", [128, 128], BF16))
    rowS = e(nc.sbuf_tensor("rowS", [1, 3 * TC], F32))
    fscr = e(nc.sbuf_tensor("fscr", [128, TC], F32))
    fscr2 = e(nc.sbuf_tensor("fscr2", [128, 40], F32))
    fscr3 = e(nc.sbuf_tensor("fscr3", [128, 40], F32))
    idvm = e(nc.sbuf_tensor("idvm", [128, 8], F32))
    smallf = e(nc.sbuf_tensor("smallf", [128, 64], F32))
    cstf = e(nc.sbuf_tensor("cstf", [128, 4], F32))

    base0 = nc.sbuf_base

    def alias_pair(name, shape, dt1, dt2):
        b = nc.sbuf_base
        a1 = nc.alloc_sbuf_tensor(name + "_e", shape, dt1)
        nc.sbuf_base = b
        a2 = nc.alloc_sbuf_tensor(name + "_r", shape, dt2)
        return a1, a2

    xf_base = nc.sbuf_base
    Xf, Xfr = alias_pair("Xf", [128, NCH * D], F32, F32R)
    XTe, XTr = alias_pair("XT", [128, NCH * D], F32, F32R)
    xse_base = nc.sbuf_base
    Xse, Xsr = alias_pair("Xs", [128, NCH * CS], F32, F32R)
    am_base = nc.sbuf_base
    Ame, Amr = alias_pair("Am", [128, NCH * CS], F32, F32R)
    sv_b = nc.sbuf_base
    nc.sbuf_base = am_base
    XTs = nc.alloc_sbuf_tensor("XTs", [128, 2 * D], F32)
    nc.sbuf_base = sv_b
    ns_small = nc.alloc_sbuf_tensor("ns_small", [128, D + 64], F32)
    p1_end = nc.sbuf_base
    nc.sbuf_base = xse_base
    PBF = nc.alloc_sbuf_tensor("PBF", [128, NCH * D], BF16)
    nc.sbuf_base = xf_base
    P1T = nc.alloc_sbuf_tensor("P1T", [128, NCH * D], BF16)
    P2T = nc.alloc_sbuf_tensor("P2T", [128, NCH * D], BF16)
    p1t_end = nc.sbuf_base

    nc.sbuf_base = p1_end
    hB = nc.alloc_sbuf_tensor("hB", [128, NCH * B * TC], BF16)
    qt = [nc.alloc_sbuf_tensor(f"qt{i}", [128, NCH * D], BF16) for i in range(2)]
    qraw = nc.alloc_sbuf_tensor("qraw", [128, 512], I32)
    qfsc = nc.alloc_sbuf_tensor("qfsc", [128, 512], F32)
    srepc = nc.alloc_sbuf_tensor("srepc", [128, 512], F32)
    cvt32 = nc.alloc_sbuf_tensor("cvt32", [128, TC + 4], F32)
    xstg = nc.alloc_sbuf_tensor("xstg", [128, TC + 1], F32)
    nsT2 = nc.alloc_sbuf_tensor("nsT2", [64, 2 * D], BF16)
    lb2 = nc.alloc_sbuf_tensor("lb2", [32, 2 * D], BF16)
    labf = nc.alloc_sbuf_tensor("labf", [128, 6 * NCH * 32], BF16)
    biasc = nc.alloc_sbuf_tensor("biasc", [128, 48], F32)
    gamb = nc.alloc_sbuf_tensor("gamb", [128, 3 * NCH], F32)
    betb = nc.alloc_sbuf_tensor("betb", [128, 3 * NCH], F32)
    poolbf = nc.alloc_sbuf_tensor("poolbf", [128, NCH * 64], BF16)
    xbfB = nc.alloc_sbuf_tensor("xbfB", [128, NCH * (TC + 2)], BF16)
    sqB = nc.alloc_sbuf_tensor("sqB", [128, NCH * (TC + 2)], BF16)
    y1B = nc.alloc_sbuf_tensor("y1B", [128, NCH * TC], BF16)
    auxB = nc.alloc_sbuf_tensor("auxB", [64, TC], BF16)
    auxB2 = nc.alloc_sbuf_tensor("auxB2", [32, TC], BF16)
    p2_end = nc.sbuf_base

    nc.sbuf_base = p1t_end
    Ebuf = nc.alloc_sbuf_tensor("Ebuf", [128, E_COLS], BF16)
    wbf = nc.alloc_sbuf_tensor("wbf", [128, NCH * D], BF16)
    xtl2 = nc.alloc_sbuf_tensor("xtl2", [128, NCH * (TC + 1)], F32)
    zscr = nc.alloc_sbuf_tensor("zscr", [128, NCH * TC], BF16)
    cvt3 = nc.alloc_sbuf_tensor("cvt3", [128, D], F32)
    xbf3 = nc.alloc_sbuf_tensor("xbf3", [128, NCH * (TC + 2)], BF16)
    sqb3 = nc.alloc_sbuf_tensor("sqb3", [128, NCH * (TC + 2)], BF16)
    actB = nc.alloc_sbuf_tensor("actB", [128, NCH * TC], BF16)
    vtile = nc.alloc_sbuf_tensor("vtile", [128, NCH * TC], BF16)
    gt2 = nc.alloc_sbuf_tensor("gt2", [128, NCH * TC], BF16)
    qsh = nc.alloc_sbuf_tensor("qsh", [128, NCH * TC], BF16)
    otile = nc.alloc_sbuf_tensor("otile", [128, TC], F32)
    e9all = nc.alloc_sbuf_tensor("e9all", [128, NCH * B * 8], BF16)
    emid = nc.alloc_sbuf_tensor("emid", [128, NCH * B * 8], BF16)
    rmid = nc.alloc_sbuf_tensor("rmid", [128, NCH * B * 12], BF16)
    r9sel = nc.alloc_sbuf_tensor("r9sel", [128, NCH * B * 2], BF16)
    r9tm = nc.alloc_sbuf_tensor("r9tm", [8, D], BF16)
    qlastB = nc.alloc_sbuf_tensor("qlastB", [128, NCH * B * 2], BF16)
    xcolB = nc.alloc_sbuf_tensor("xcolB", [128, NCH * B * 2], BF16)
    p3_end = nc.sbuf_base
    nc.sbuf_base = max(p2_end, p3_end)

    pA = [e(nc.psum_tensor(f"pA{i}", [128, 512], F32)) for i in range(2)]
    pU = [e(nc.psum_tensor(f"pU{i}", [128, 512], F32)) for i in range(2)]
    pT = [e(nc.psum_tensor(f"pT{i}", [128, 512], F32)) for i in range(2)]
    pX = [e(nc.psum_tensor(f"pX{i}", [128, 512], F32)) for i in range(2)]
    pAs = [p[:, 0:CS] for p in pA]
    pUs = [p[:, 0:CS] for p in pU]
    pTs = [p[:, 0:128] for p in pT]
    pN1 = pT[0][0:1, 256:257]
    pN2 = pT[1][:, 256:257]
    pR0 = pT[0][0:1, 0:512]
    pR1 = pT[1][0:1, 0:512]

    dL = e(nc.semaphore("dL"))
    dO = e(nc.semaphore("dO"))
    dRs = e(nc.semaphore("dRs"))
    cc = e(nc.semaphore("cc"))
    tG = e(nc.semaphore("tG"))
    tU = e(nc.semaphore("tU"))
    tT = e(nc.semaphore("tT"))
    vA = e(nc.semaphore("vA"))
    vU = e(nc.semaphore("vU"))
    cT = e(nc.semaphore("cT"))
    nX = e(nc.semaphore("nX"))
    eT = e(nc.semaphore("eT"))
    eV = e(nc.semaphore("eV"))
    eC = e(nc.semaphore("eC"))
    eS = e(nc.semaphore("eS"))
    eG = e(nc.semaphore("eG"))
    eK = e(nc.semaphore("eK"))
    eH = e(nc.semaphore("eH"))
    SEMS = {"T": eT, "V": eV, "C": eC, "S": eS, "G": eG, "K": eK, "H": eH, "nX": nX}

    def XsE(k):
        return Xsr if k >= NSPLIT - 1 else Xse

    def XTk(r):
        return XTr if r >= NSPLIT else XTe

    def AmK(k):
        return Amr if k >= NSPLIT else Ame

    def XfR(r):
        return Xfr if r >= NSPLIT else Xf

    def tr_src(t, r):
        r8, tt = divmod(t, 8)
        o = r8 * D + tt * 128
        return XfR(r)[:, o:o + 128]

    def tr_dst(t, r):
        r8, tt = divmod(t, 8)
        o = tt * D + r8 * 128
        return XTk(r)[:, o:o + 128]

    ssq = ns_small[:, 0:D]
    sstat = ns_small[:, D:D + 8]
    sstat1 = ns_small[:, D + 8:D + 9]
    snrm_s = ns_small[0:1, D + 12:D + 13]
    snrm_r = ns_small[0:1, D + 14:D + 15]
    sscal = ns_small[:, D + 16:D + 17]

    P = Plan()
    state = {"pR_reader": None, "pB_readers": [], "pX_evac": [None, None],
             "pU_owner": None, "E_ready": {}, "pT_trans": [None, None]}

    def dma(eng, out, in_, deps=()):
        return P.add(eng, (lambda h, out=out, in_=in_:
                           h.dma_start(out=out, in_=in_)), deps)

    def gcp(out, in_, deps=()):
        return P.add("H", (lambda h, out=out, in_=in_:
                           h.tensor_copy(out, in_)), deps)

    def dmas(eng, out, in_, deps=()):
        return P.add(eng, (lambda h, out=out, in_=in_: h.dma_start(
            out=out, in_=in_, allow_slow_non_contiguous=True)), deps)

    def mmg(psum_ap, pairs, deps):
        def fn(h, psum_ap=psum_ap, pairs=pairs):
            ins = None
            n = len(pairs)
            for j, (l_, r_) in enumerate(pairs):
                ins = h.matmul(psum_ap, l_, r_, start=(j == 0),
                               stop=(j == n - 1))
            return ins
        try:
            cost = sum(128 + r_.shape[-1] for (l_, r_) in pairs)
        except Exception:
            cost = 640 * len(pairs)
        return P.add("T", fn, deps, cost=cost)

    def vi(fn, deps):
        return P.add("V", fn, deps)

    def ci(fn, deps):
        return P.add("C", fn, deps)

    def load_cvt(dst_ap, src_ap, stage_ap, prev, scale=None):
        ld = dma("S", stage_ap, src_ap, [prev] if prev else [])
        if scale is None:
            return P.add("V", lambda h, d=dst_ap, s_=stage_ap:
                         h.tensor_copy(d, s_), [ld])
        return P.add("V", lambda h, d=dst_ap, s_=stage_ap, sc=scale:
                     h.tensor_scalar(d, s_, sc, None, ALU.mult), [ld])

    ns_done = [("nX", 8)]

    c0 = P.add("V", lambda h: (h.memset(sonebf[:], 1.0),
                               h.memset(sonesbf[:], 1.0),
                               h.memset(cstf[:, 0:1], 1.0 / D),
                               h.memset(cstf[:, 1:2], LN_EPS),
                               h.memset(cstf[:, 2:3], RMS_EPS),
                               h.memset(cstf[:, 3:4], 1.0))[-1],
               [])
    t_g = dma("S", gamb[:, :], gam_in, [])
    t_b = dma("S", betb[:, :], bet_in, [])
    t_bias = dma("S", biasc[:, :], biasfm_in, [])
    t_m = dma("S", m0v[:, :], m0_in, [])
    t_m = vi(lambda h: (h.tensor_copy(m0a[:, :], m0v[:, 0:1]),
                        h.tensor_copy(m0bv[:, :], m0v[:, 1:2]))[-1], [t_m])
    t_i = dma("S", smallf[:, 0:NCH], idv_in, [])
    t_s = dma("S", smallf[0:8, 12:14], sel_in, [])
    c_i = vi(lambda h: h.tensor_copy(idvbf[:, 0:2 * NCH:2], smallf[:, 0:NCH]), [t_i])
    c_i2 = ci(lambda h: h.activation(idvm[:, 0:NCH], smallf[:, 0:NCH],
                                     AF.Copy, scale=m0bv.ap()),
              [t_i, t_m, c0])
    c_s = vi(lambda h: h.tensor_copy(selbf[:, :], smallf[0:8, 12:14]), [t_s])
    prev = dma("S", cvt32[:, 0:128], id_in.bitcast(F32), [c_s])
    prev = vi(lambda h: h.tensor_copy(sidbf[:, :], cvt32[:, 0:128]), [prev])
    prev = dma("S", cvt32[:, 0:512], pool_in.transpose([1, 0, 2]), [prev])
    prev = vi(lambda h: h.tensor_copy(poolbf[:, :], cvt32[:, 0:512]), [prev])
    la4 = labf.ap().rearrange("p (i a r) -> p i a r", a=NCH, r=32)
    for i in range(6):
        prev = load_cvt(la4[:, i], _r3(la_in[i]),
                        cvt32[:, 0:256].rearrange("p (a r) -> p a r", r=32),
                        prev)
    consts_done = prev

    def load_nsT(i, deps):
        s = (i % 2) * D
        t1 = dma("S", cvt32[0:64, 0:512], sT_in[i][:, 0:512], deps)
        v1 = vi(lambda h, i=i, s=s: h.tensor_scalar(
            nsT2[0:64, s:s + 512], cvt32[0:64, 0:512], -1.0, None, ALU.mult),
            [t1])
        t2 = dma("S", cvt32[0:64, 0:512], sT_in[i][:, 512:D], [v1])
        v2 = vi(lambda h, i=i, s=s: h.tensor_scalar(
            nsT2[0:64, s + 512:s + D], cvt32[0:64, 0:512], -1.0, None,
            ALU.mult), [t2])
        return v2

    def load_lb(i, deps):
        s = (i % 2) * D
        t1 = dma("S", cvt32[0:32, 0:512], lb_in[i][:, 0:512], deps)
        v1 = vi(lambda h, i=i, s=s: h.tensor_copy(
            lb2[0:32, s:s + 512], cvt32[0:32, 0:512]), [t1])
        t2 = dma("S", cvt32[0:32, 0:512], lb_in[i][:, 512:D], [v1])
        v2 = vi(lambda h, i=i, s=s: h.tensor_copy(
            lb2[0:32, s + 512:s + D], cvt32[0:32, 0:512]), [t2])
        return v2

    TCP = TC + 2

    def emit_ln(xtile, xbft, sqt, wo, dst_fn, ld_tok, extra=()):
        cvt = vi(lambda h: [h.tensor_copy(
            xbft[:, d8 * TCP: d8 * TCP + TC],
            xtile[:, d8 * (TC + 1) + wo: d8 * (TC + 1) + wo + TC])
            for d8 in range(NCH)][-1],
            [ld_tok, state.get("last_lnmm")] + list(extra))
        sq_toks = []
        for d8 in range(NCH):
            of = d8 * (TC + 1) + wo
            o = d8 * TCP
            tok = vi(lambda h, o=o, of=of: h.tensor_mul(
                sqt[:, o:o + TC], xtile[:, of:of + TC],
                xtile[:, of:of + TC]), [cvt])
            sq_toks.append(tok)
        d0 = [state["pR_reader"], cvt]
        s0 = mmg(pR0[0:1, 0:TC],
                 [(sonesbf[:], xbft[:, d8 * TCP: d8 * TCP + TC])
                  for d8 in range(NCH)], d0 + [c0])
        s1 = mmg(pR1[0:1, 0:TC],
                 [(sonesbf[:], sqt[:, d8 * TCP: d8 * TCP + TC])
                  for d8 in range(NCH)], sq_toks)
        state["last_lnmm"] = s1
        cst = ci(lambda h: (h.activation(rowS[0:1, 0:TC], pR0[0:1, 0:TC],
                                         AF.Copy, scale=cstf[0:1, 0:1]),
                            h.activation(rowS[0:1, TC:2 * TC], pR1[0:1, 0:TC],
                                         AF.Copy, scale=cstf[0:1, 0:1]))[-1],
                 [s0, s1, c0])
        state["pR_reader"] = cst
        v0 = vi(lambda h: (h.tensor_mul(rowS[0:1, 2 * TC:3 * TC],
                                        rowS[0:1, 0:TC], rowS[0:1, 0:TC]),
                           h.tensor_sub(rowS[0:1, 2 * TC:3 * TC],
                                        rowS[0:1, TC:2 * TC],
                                        rowS[0:1, 2 * TC:3 * TC]))[-1], [cst])
        c1 = ci(lambda h: h.activation(rowS[0:1, 2 * TC:3 * TC],
                                       rowS[0:1, 2 * TC:3 * TC], AF.Sqrt,
                                       bias=cstf[0:1, 1:2]), [v0])
        v1 = vi(lambda h: h.reciprocal(rowS[0:1, 2 * TC:3 * TC],
                                       rowS[0:1, 2 * TC:3 * TC]), [c1])
        bc = mmg(pA[0][:, 0:TC], [(sonesr[:], rowS[0:1, 0:TC])],
                 [v1] + state["pB_readers"])
        bc2 = mmg(pA[1][:, 0:TC], [(sonesr[:], rowS[0:1, 2 * TC:3 * TC])],
                  [v1])
        state["pB_readers"] = []
        outs = []
        for d8 in range(NCH):
            o = d8 * TCP
            tok = vi(lambda h, o=o, d8=d8: (
                h.tensor_sub(fscr[:, 0:TC], xbft[:, o:o + TC],
                             pA[0][:, 0:TC]),
                h.tensor_mul(dst_fn(d8), fscr[:, 0:TC], pA[1][:, 0:TC]))[-1],
                [bc, bc2])
            outs.append(tok)
        state["pB_readers"] = outs
        return outs

    def hb_ap(d8, b):
        return hB[:, (d8 * B + b) * TC:(d8 * B + b) * TC + TC]

    def y1_ap(d8):
        return y1B[:, d8 * TC:(d8 + 1) * TC]

    def emit_ln_p2(b, dst_fn):
        cvs = []
        p = state.get("xstg_own")
        for d8 in range(NCH):
            ldc = dma("S", xstg[:, :], xs_in[b, d8], [p] if p else [])
            cv = vi(lambda h, d8=d8: (
                h.tensor_copy(xbfB[:, d8 * TCP: d8 * TCP + TC],
                              xstg[:, 1:TC + 1]),
                h.tensor_mul(sqB[:, d8 * TCP: d8 * TCP + TC],
                             xstg[:, 1:TC + 1], xstg[:, 1:TC + 1]))[-1],
                [ldc])
            p = cv
            cvs.append(cv)
        state["xstg_own"] = p
        s0 = mmg(pX[0][0:1, 0:TC],
                 [(sonesbf[:], xbfB[:, d8 * TCP: d8 * TCP + TC])
                  for d8 in range(NCH)],
                 cvs + [c0, state["pX_evac"][0]])
        s1 = mmg(pX[1][0:1, 0:TC],
                 [(sonesbf[:], sqB[:, d8 * TCP: d8 * TCP + TC])
                  for d8 in range(NCH)], cvs + [state["pX_evac"][1]])
        cst = ci(lambda h: (h.activation(rowS[0:1, 0:TC], pX[0][0:1, 0:TC],
                                         AF.Copy, scale=cstf[0:1, 0:1]),
                            h.activation(rowS[0:1, TC:2 * TC],
                                         pX[1][0:1, 0:TC],
                                         AF.Copy, scale=cstf[0:1, 0:1]))[-1],
                 [s0, s1, c0])
        v0 = vi(lambda h: (h.tensor_mul(rowS[0:1, 2 * TC:3 * TC],
                                        rowS[0:1, 0:TC], rowS[0:1, 0:TC]),
                           h.tensor_sub(rowS[0:1, 2 * TC:3 * TC],
                                        rowS[0:1, TC:2 * TC],
                                        rowS[0:1, 2 * TC:3 * TC]))[-1], [cst])
        c1 = ci(lambda h: h.activation(rowS[0:1, 2 * TC:3 * TC],
                                       rowS[0:1, 2 * TC:3 * TC], AF.Sqrt,
                                       bias=cstf[0:1, 1:2]), [v0])
        v1 = vi(lambda h: h.reciprocal(rowS[0:1, 2 * TC:3 * TC],
                                       rowS[0:1, 2 * TC:3 * TC]), [c1])
        bc = mmg(pX[0][:, 0:TC], [(sonesr[:], rowS[0:1, 0:TC])],
                 [v1, cst] + state["pB_readers"])
        bc2 = mmg(pX[1][:, 0:TC], [(sonesr[:], rowS[0:1, 2 * TC:3 * TC])],
                  [v1, cst])
        state["pB_readers"] = []
        outs = []
        for d8 in range(NCH):
            o = d8 * TCP
            tok = vi(lambda h, o=o, d8=d8: (
                h.tensor_sub(fscr[:, 0:TC], xbfB[:, o:o + TC],
                             pX[0][:, 0:TC]),
                h.tensor_mul(dst_fn(d8), fscr[:, 0:TC], pX[1][:, 0:TC]))[-1],
                [bc, bc2])
            outs.append(tok)
        state["pB_readers"] = outs
        state["pX_evac"][0] = outs[-1]
        state["pX_evac"][1] = outs[-1]
        return outs

    h_init = []
    for b in range(B):
        outs = emit_ln_p2(b, lambda d8, b=b: hb_ap(d8, b))
        h_init.extend(outs)

    def dequant(i, buf, prev_tok, extra_deps):
        p = prev_tok
        for d8 in range(NCH):
            for hf in range(2):
                sl = slice(hf * 512, hf * 512 + 512)
                l1 = dma("S", qraw[:, :], _r3(qw_in[i])[:, d8, sl],
                         [p] + (extra_deps if d8 == 0 and hf == 0 else []))
                l2 = dma("S", srepc[:, :], _r3(srep_in[i])[:, d8, sl], [l1])
                o = d8 * D + hf * 512
                p = vi(lambda h, o=o, buf=buf: (
                    h.tensor_copy(qfsc[:, :], qraw[:, :]),
                    h.tensor_scalar(qfsc[:, :], qfsc[:, :], R75, None,
                                    ALU.mult),
                    h.tensor_mul(buf[:, o:o + 512], qfsc[:, :],
                                 srepc[:, :]))[-1], [l1, l2])
        return p

    def qlora(i, buf, src_fn, b, dq_tok, ns_tok, evac_kind, src_toks):
        sL = (i % 2) * D
        a_deps = src_toks + [state["pX_evac"][0]]
        xg0 = mmg(pX[0][0:64, 0:256],
                  [(poolbf[:, d8 * 64:(d8 + 1) * 64], src_fn(d8)[:, 0:256])
                   for d8 in range(NCH)], a_deps)
        xg1 = mmg(pX[0][0:64, 256:512],
                  [(poolbf[:, d8 * 64:(d8 + 1) * 64], src_fn(d8)[:, 256:512])
                   for d8 in range(NCH)], [])
        md0 = mmg(pX[1][0:32, 0:256],
                  [(la4[:, i, d8], src_fn(d8)[:, 0:256])
                   for d8 in range(NCH)], [state["pX_evac"][1]])
        md1 = mmg(pX[1][0:32, 256:512],
                  [(la4[:, i, d8], src_fn(d8)[:, 256:512])
                   for d8 in range(NCH)], [])
        ev = vi(lambda h: (h.tensor_copy(auxB[0:64, :], pX[0][0:64, :]),
                           h.tensor_copy(auxB2[0:32, :], pX[1][0:32, :]))[-1],
                [xg0, xg1, md0, md1])
        state["pX_evac"][0] = ev
        state["pX_evac"][1] = ev
        out_toks = []
        last_mm = None
        prev_vadd = None
        for ot in range(NCH):
            deps = [ev, dq_tok, ns_tok, state["pX_evac"][ot % 2]]
            pairs = [(buf[:, d8 * D + ot * 128: d8 * D + ot * 128 + 128],
                      src_fn(d8)) for d8 in range(NCH)]
            pairs.append((nsT2[0:64, sL + ot * 128: sL + ot * 128 + 128],
                          auxB[0:64, :]))
            pairs.append((lb2[0:32, sL + ot * 128: sL + ot * 128 + 128],
                          auxB2[0:32, :]))
            mm = mmg(pX[ot % 2][:, 0:TC], pairs, deps)
            last_mm = mm
            bcol = biasc[:, i * 8 + ot: i * 8 + ot + 1]
            if evac_kind == "y1":
                tokc = ci(lambda h, ot=ot, bcol=bcol: h.activation(
                    y1_ap(ot), pX[ot % 2][:, 0:TC], AF.Identity,
                    scale=cstf[:, 3:4], bias=bcol), [mm, t_bias, c0])
                tok = ci(lambda h, ot=ot: h.activation(
                    sqB[:, ot * TCP: ot * TCP + TC], y1_ap(ot),
                    AF.Square), [tokc, state.get("last_lnmm")])
            else:
                tokc = ci(lambda h, ot=ot, bcol=bcol: h.activation(
                    fscr[:, 0:TC], pX[ot % 2][:, 0:TC], AF.Identity,
                    scale=cstf[:, 3:4], bias=bcol),
                    [mm, t_bias, c0] +
                    ([prev_vadd] if prev_vadd else []))
                tok = vi(lambda h, ot=ot, b=b: h.tensor_add(
                    hb_ap(ot, b), hb_ap(ot, b), fscr[:, 0:TC]), [tokc])
                prev_vadd = tok
            state["pX_evac"][ot % 2] = tok
            out_toks.append(tok)
        return out_toks, last_mm

    def ffn_layernorm(blk, y_toks):
        sq_toks = y_toks
        s0 = mmg(pX[0][0:1, 0:TC], [(sonesbf[:], y1_ap(d8))
                                    for d8 in range(NCH)],
                 [state["pX_evac"][0]] + y_toks)
        s1 = mmg(pX[1][0:1, 0:TC],
                 [(sonesbf[:], sqB[:, d8 * TCP: d8 * TCP + TC])
                  for d8 in range(NCH)], sq_toks + [state["pX_evac"][1]])
        state["last_lnmm"] = s1
        cst = ci(lambda h: (h.activation(rowS[0:1, 0:TC], pX[0][0:1, 0:TC],
                                         AF.Copy, scale=cstf[0:1, 0:1]),
                            h.activation(rowS[0:1, TC:2 * TC],
                                         pX[1][0:1, 0:TC],
                                         AF.Copy, scale=cstf[0:1, 0:1]))[-1],
                 [s0, s1, c0])
        v0 = vi(lambda h: (h.tensor_mul(rowS[0:1, 2 * TC:3 * TC],
                                        rowS[0:1, 0:TC], rowS[0:1, 0:TC]),
                           h.tensor_sub(rowS[0:1, 2 * TC:3 * TC],
                                        rowS[0:1, TC:2 * TC],
                                        rowS[0:1, 2 * TC:3 * TC]))[-1], [cst])
        c1 = ci(lambda h: h.activation(rowS[0:1, 2 * TC:3 * TC],
                                       rowS[0:1, 2 * TC:3 * TC], AF.Sqrt,
                                       bias=cstf[0:1, 1:2]), [v0])
        v1 = vi(lambda h: h.reciprocal(rowS[0:1, 2 * TC:3 * TC],
                                       rowS[0:1, 2 * TC:3 * TC]), [c1])
        bc = mmg(pX[0][:, 0:TC], [(sonesr[:], rowS[0:1, 0:TC])],
                 [v1, cst] + state["pB_readers"])
        bc2 = mmg(pX[1][:, 0:TC], [(sonesr[:], rowS[0:1, 2 * TC:3 * TC])],
                  [v1, cst])
        state["pB_readers"] = []
        outs = []
        for d8 in range(NCH):
            t1 = vi(lambda h, d8=d8: (
                h.tensor_sub(fscr[:, 0:TC], y1_ap(d8), pX[0][:, 0:TC]),
                h.tensor_mul(y1_ap(d8), fscr[:, 0:TC], pX[1][:, 0:TC]))[-1],
                [bc, bc2])
            t2 = ci(lambda h, d8=d8, blk=blk: h.activation(
                y1_ap(d8), y1_ap(d8), AF.Gelu,
                bias=betb[:, blk * NCH + d8: blk * NCH + d8 + 1],
                scale=gamb[:, blk * NCH + d8: blk * NCH + d8 + 1]),
                [t1, t_g, t_b])
            outs.append(t2)
        state["pB_readers"] = outs
        state["pX_evac"][0] = outs[-1]
        state["pX_evac"][1] = outs[-1]
        return outs

    dq_prev = dequant(0, qt[0], consts_done, h_init)
    dqA = dq_prev
    dqB = None
    ns_prev = load_nsT(0, [consts_done])
    ns_prev = load_lb(0, [ns_prev])
    nsA = ns_prev
    nsB = None
    qt_read = [None, None]
    for blk in range(3):
        i0, i1 = 2 * blk, 2 * blk + 1
        dqB = dequant(i1, qt[1], dq_prev, [qt_read[1]] if qt_read[1] else [])
        dq_prev = dqB
        ns_prev = load_nsT(i1, [ns_prev] +
                           ([qt_read[1]] if qt_read[1] else []))
        ns_prev = load_lb(i1, [ns_prev])
        nsB = ns_prev
        for b in range(B):
            y_toks, lmm = qlora(i0, qt[0], lambda d8, b=b: hb_ap(d8, b), b,
                                dqA, nsA, "y1",
                                h_init if blk == 0 and b == 0 else [])
            qt_read[0] = lmm
            y2 = ffn_layernorm(blk, y_toks)
            _, lmm = qlora(i1, qt[1], lambda d8: y1_ap(d8), b, dqB, nsB,
                           "hupd", y2)
            qt_read[1] = lmm
        if blk < 2:
            dqA = dequant(2 * blk + 2, qt[0], dq_prev,
                          [qt_read[0]] if qt_read[0] else [])
            dq_prev = dqA
            ns_prev = load_nsT(2 * blk + 2, [ns_prev] +
                               ([qt_read[0]] if qt_read[0] else []))
            ns_prev = load_lb(2 * blk + 2, [ns_prev])
            nsA = ns_prev
    g_spills = []
    sil_prev = P.barrier()
    for d8 in range(NCH):
        s_tok = ci(lambda h, d8=d8: h.activation(
            hB[:, d8 * B * TC:(d8 + 1) * B * TC],
            hB[:, d8 * B * TC:(d8 + 1) * B * TC], AF.Silu),
            sil_prev if d8 == 0 else [])
        g_tok = dma("G", gdram[d8], hB[:, d8 * B * TC:(d8 + 1) * B * TC],
                    [s_tok])
        g_spills.append(g_tok)

    def E_all(l, d8, j0, n, step=1):
        off = EOFF[l] + d8 * B * NLP[l]
        apx = Ebuf[:, off: off + B * NLP[l]].rearrange("p (b j) -> p b j",
                                                      j=NLP[l])
        if step == 1:
            return apx[:, :, j0: j0 + n]
        return apx[:, :, j0: j0 + (n - 1) * step + 1: step]

    def E_blk(l, d8, b, j0, n, step=1):
        off = EOFF[l] + (d8 * B + b) * NLP[l] + j0
        if step == 1:
            return Ebuf[:, off: off + n]
        return Ebuf[:, off: off + (n - 1) * step + 1: step]

    P2_END = len(P.items)
    p3bar = P.barrier()
    p = None
    for d8 in range(NCH):
        ld = dma("S", cvt3[:, :], _r3(wq_in)[:, d8],
                 (p3bar + g_spills + ns_done) if d8 == 0 else [p])
        p = vi(lambda h, d8=d8: h.tensor_copy(wbf[:, d8 * D:(d8 + 1) * D],
                                              cvt3[:, :]), [ld])
    wq_done = p

    E0_ready = []
    for b in range(B):
        ld = dma("S", xtl2.ap().rearrange("p (a c) -> p a c", c=TC + 1),
                 xs_in[b].transpose([1, 0, 2]),
                 p3bar + ns_done + (state["pB_readers"] if b else []))
        xn_toks = emit_ln(xtl2, xbf3, sqb3, 0,
                          lambda d8: actB[:, d8 * TC:(d8 + 1) * TC], ld)
        for et in range(NCH):
            mm = mmg(pX[et % 2][:, 0:TC],
                     [(wbf[:, k8 * D + et * 128: k8 * D + et * 128 + 128],
                       actB[:, k8 * TC:(k8 + 1) * TC]) for k8 in range(NCH)],
                     xn_toks + [wq_done, state["pX_evac"][et % 2]])
            tok = vi(lambda h, et=et, b=b: h.tensor_copy(
                E_blk(0, et, b, 0, TC), pX[et % 2][:, 0:TC]), [mm])
            ca = ci(lambda h, et=et, b=b: h.activation(
                fscr2[:, 0:1], E_blk(0, et, b, 0, 1), AF.Copy,
                scale=m0a.ap()), [tok, t_m, c_i2])
            cb = ci(lambda h, et=et, b=b: h.activation(
                E_blk(0, et, b, 0, 1), fscr2[:, 0:1], AF.Identity,
                scale=cstf[:, 3:4], bias=idvm[:, et:et + 1]), [ca])
            state["pX_evac"][et % 2] = tok
            E0_ready.append(cb)
            E0_ready.append(tok)
        xn2 = emit_ln(xtl2, xbf3, sqb3, 1,
                      lambda d8: actB[:, d8 * TC:(d8 + 1) * TC], tok,
                      extra=[state["pX_evac"][0], state["pX_evac"][1]])
        for et in range(NCH):
            mm = mmg(pX[et % 2][:, 0:1],
                     [(wbf[:, k8 * D + et * 128: k8 * D + et * 128 + 128],
                       actB[:, k8 * TC + TC - 1: k8 * TC + TC])
                      for k8 in range(NCH)],
                     xn2 + [state["pX_evac"][et % 2]])
            tok2 = vi(lambda h, et=et, b=b: h.tensor_copy(
                qlastB[:, (et * B + b) * 2: (et * B + b) * 2 + 1],
                pX[et % 2][:, 0:1]), [mm])
            state["pX_evac"][et % 2] = tok2
    state["E_ready"][0] = list(E0_ready) + ns_done
    if dbg:
        dbg_E0 = nc.dram_tensor("dbg_E0", [128, E_COLS], BF16,
                                kind="ExternalOutput").ap()
        tok_dump = dma("S", dbg_E0[:, :], Ebuf[:, :], P.barrier())
        state["E_ready"][0] = state["E_ready"][0] + [tok_dump]

    def comb(m, lhs, rhs, outw, blend=False, in_ready=(), sqoff=0):
        if m > 512:
            per_b = m // B
            nb = max(1, 512 // per_b)
            outs = []
            for b0 in range(0, B, nb):
                ne = min(nb, B - b0)

                def wrap(f, b0=b0, ne=ne):
                    return lambda k8: f(k8)[:, b0:b0 + ne, :]
                outs += comb(per_b * ne, wrap(lhs), wrap(rhs), wrap(outw),
                             blend, in_ready, sqoff)
            return outs
        outs = []
        ztoks = []
        for d8 in range(NCH):
            deps = [state["pX_evac"][d8 % 2]] + list(in_ready)
            pairs = [(P1T[:, k8 * D + d8 * 128: k8 * D + d8 * 128 + 128],
                      lhs(k8)) for k8 in range(NCH)]
            pairs += [(P2T[:, k8 * D + d8 * 128: k8 * D + d8 * 128 + 128],
                       rhs(k8)) for k8 in range(NCH)]
            mm = mmg(pX[d8 % 2][:, 0:m], pairs, deps)
            ev = vi(lambda h, d8=d8, m=m: h.tensor_copy(
                zscr[:, d8 * TC: d8 * TC + m], pX[d8 % 2][:, 0:m]), [mm])
            evc = ci(lambda h, d8=d8, m=m: h.activation(
                sqb3[:, d8 * TCP: d8 * TCP + m], pX[d8 % 2][:, 0:m],
                AF.Square), [mm, ev, state.get("last_s0")])
            state["pX_evac"][d8 % 2] = evc
            ztoks.append(evc)
        s0 = mmg(pR0[0:1, 0:m],
                 [(sonesbf[:], sqb3[:, d8 * TCP: d8 * TCP + m])
                  for d8 in range(NCH)],
                 [state["pR_reader"]] + ztoks)
        state["last_s0"] = s0
        cst = ci(lambda h, m=m: h.activation(rowS[0:1, 0:m], pR0[0:1, 0:m],
                                             AF.Sqrt, scale=cstf[0:1, 0:1],
                                             bias=cstf[0:1, 2:3]), [s0, c0])
        state["pR_reader"] = cst
        v1 = vi(lambda h, m=m: h.reciprocal(rowS[0:1, 0:m], rowS[0:1, 0:m]),
                [cst])
        bc = mmg(pA[0][:, 0:m], [(sonesr[:], rowS[0:1, 0:m])],
                 [v1] + state["pB_readers"])
        state["pB_readers"] = []
        for d8 in range(NCH):
            if not blend:
                tok = vi(lambda h, d8=d8, m=m: h.tensor_mul(
                    outw(d8), zscr[:, d8 * TC: d8 * TC + m],
                    pA[0][:, 0:m]), [bc])
            else:
                v_a = vi(lambda h, d8=d8, m=m: h.tensor_mul(
                    fscr[:, 0:m], zscr[:, d8 * TC: d8 * TC + m],
                    pA[0][:, 0:m]),
                    [bc] + ([state["blend_c"]] if state.get("blend_c") else []))
                c_a = ci(lambda h, m=m: h.activation(
                    fscr2[:, 0:m], fscr[:, 0:m], AF.Copy, scale=m0a.ap()),
                    [v_a, t_m] + ([state["blend_v"]]
                                  if state.get("blend_v") else []))
                c_b = ci(lambda h, d8=d8, m=m: h.activation(
                    fscr3[:, 0:m], outw(d8), AF.Copy, scale=m0bv.ap()), [c_a])
                tok = vi(lambda h, d8=d8, m=m: h.tensor_add(
                    outw(d8), fscr2[:, 0:m], fscr3[:, 0:m]), [c_b])
                state["blend_c"] = c_a
                state["blend_v"] = tok
            outs.append(tok)
        state["pB_readers"] = outs
        return outs

    for l in range(1, 10):
        n = NL[l]
        state["E_ready"][l] = comb(
            B * n,
            lambda k8, l=l, n=n: E_all(l - 1, k8, 0, n, 2),
            lambda k8, l=l, n=n: E_all(l - 1, k8, 1, n, 2),
            lambda d8, l=l, n=n: E_all(l, d8, 0, n),
            in_ready=state["E_ready"][l - 1])
        if dbg and l == 6:
            dbg_row = nc.dram_tensor("dbg_row", [1, 3 * TC], F32,
                                     kind="ExternalOutput").ap()
            dbg_sq = nc.dram_tensor("dbg_sq", [128, NCH * (TC + 2)], BF16,
                                    kind="ExternalOutput").ap()
            dbg_z = nc.dram_tensor("dbg_z", [128, NCH * TC], BF16,
                                   kind="ExternalOutput").ap()
            td1 = dma("S", dbg_row[:, :], rowS[:, :], P.barrier())
            td2 = dma("S", dbg_sq[:, :], sqb3[:, :], [td1])
            td3 = dma("S", dbg_z[:, :], zscr[:, :], [td2])
            state["E_ready"][l] = state["E_ready"][l] + [td3]

    if dbg:
        dbg_Er = nc.dram_tensor("dbg_Er", [128, E_COLS], BF16,
                                kind="ExternalOutput").ap()
        tok_dump2 = dma("S", dbg_Er[:, :], Ebuf[:, :], P.barrier())
        for _l in range(10):
            state["E_ready"][_l] = state["E_ready"].get(_l, []) + [tok_dump2]
    sx = dmas("S", sc_gin[:, :, :],
              Ebuf[:, EOFF[9]: EOFF[9] + NCH * B * 2].rearrange(
                  "p (a b t) -> p a b t", b=B, t=2)[:, :, :, 0],
              state["E_ready"][9])
    kx = P.add("K", lambda h: h.collective_compute(
        "AllGather", ALU.bypass, replica_groups=G8,
        ins=[sc_gin[:, :, :]], outs=[sc_gout[:, :, :, :]]), [sx])
    e9_toks = []
    for ci_ in range(8):
        out_ap = e9all[:, ci_ * NCH * B: (ci_ + 1) * NCH * B].rearrange(
            "p (a b) -> p a b", b=B)
        tok = dma("S", out_ap, sc_gout[ci_], [kx])
        e9_toks.append(tok)
    if dbg:
        dbg_e9a = nc.dram_tensor("dbg_e9a", [128, NCH * B * 8], BF16,
                                 kind="ExternalOutput").ap()
        tk9 = dma("S", dbg_e9a[:, :], e9all[:, :], P.barrier())
        e9_toks = e9_toks + [tk9]

    def e9g(k8, i0, n, step=1):
        apx = e9all.ap().rearrange("p (c a b) -> p c a b", a=NCH, b=B
                                   )[:, :, k8, :].transpose([0, 2, 1])
        return apx[:, :, i0: i0 + (n - 1) * step + 1: step] if step > 1 \
            else apx[:, :, i0: i0 + n]

    def emg(k8, i0, n, step=1):
        apx = emid[:, k8 * B * 8: (k8 + 1) * B * 8].rearrange(
            "p (b i) -> p b i", i=8)
        return apx[:, :, i0: i0 + (n - 1) * step + 1: step] if step > 1 \
            else apx[:, :, i0: i0 + n]

    def rmidall(i0, n, step=1):
        apx = rmid.ap().rearrange("p (ab i) -> p ab i", i=12)
        if step == 1:
            return apx[:, :, i0: i0 + n]
        return apx[:, :, i0: i0 + (n - 1) * step + 1: step]

    def emidall(i0, n, step=1):
        apx = emid.ap().rearrange("p (ab i) -> p ab i", i=8)
        if step == 1:
            return apx[:, :, i0: i0 + n]
        return apx[:, :, i0: i0 + (n - 1) * step + 1: step]

    def e9all_c(i0, n, step=1):
        apx = e9all.ap().rearrange("p (i ab) -> p i ab", ab=NCH * B
                                   ).transpose([0, 2, 1])
        if step == 1:
            return apx[:, :, i0: i0 + n]
        return apx[:, :, i0: i0 + (n - 1) * step + 1: step]

    def rmg(k8, i0, n, step=1):
        apx = rmid[:, k8 * B * 12: (k8 + 1) * B * 12].rearrange(
            "p (b i) -> p b i", i=12)
        return apx[:, :, i0: i0 + (n - 1) * step + 1: step] if step > 1 \
            else apx[:, :, i0: i0 + n]

    m10 = comb(B * 4, lambda k8: e9g(k8, 0, 4, 2), lambda k8: e9g(k8, 1, 4, 2),
               lambda d8: emg(d8, 0, 4), in_ready=e9_toks)
    m11 = comb(B * 2, lambda k8: emg(k8, 0, 2, 2), lambda k8: emg(k8, 1, 2, 2),
               lambda d8: emg(d8, 4, 2), in_ready=m10)
    m12 = comb(B, lambda k8: emg(k8, 4, 1), lambda k8: emg(k8, 5, 1),
               lambda d8: emg(d8, 6, 1), in_ready=m11)
    m10d = comb(B, lambda k8: emg(k8, 4, 1), lambda k8: emg(k8, 2, 1),
                lambda d8: rmg(d8, 2, 1), in_ready=m12)
    cpy1a = gcp(rmidall(0, 1), emidall(0, 1), m12 + m10d)
    cpy1b = gcp(rmidall(1, 1), emidall(4, 1), [])
    cpy1c = gcp(rmidall(3, 1), emidall(6, 1), [])
    cpy1 = cpy1c
    m9d = comb(B * 3, lambda k8: rmg(k8, 0, 3), lambda k8: e9g(k8, 2, 3, 2),
               lambda d8: rmg(d8, 6, 3, 2), in_ready=m10d + [cpy1])
    cpy2a = gcp(rmidall(4, 1), e9all_c(0, 1), m9d)
    cpy2b = gcp(rmidall(5, 4, 2), rmidall(0, 4), [])
    cpy2 = cpy2b
    tr_toks = []
    sel_back = []
    for bb in range(B):
        tr_toks = []
        for d8 in range(NCH):
            pTb = pT[(bb * NCH + d8) % 2].ap().bitcast(BF16)
            tk = P.add("T", lambda h, bb=bb, d8=d8, pTb=pTb: h.transpose(
                pTb[0:8, 0:128],
                rmid[:, d8 * B * 12 + bb * 12 + 4: d8 * B * 12 + bb * 12 + 12],
                sidbf[:, :]),
                [cpy2, state["pT_trans"][(bb * NCH + d8) % 2],
                 state["pR_reader"], c0] + state.get("r9tm_guard", []))
            ck = ci(lambda h, bb=bb, d8=d8, pTb=pTb: h.copy(
                r9tm[0:8, d8 * 128: (d8 + 1) * 128],
                pTb[0:8, 0:128]), [tk])
            state["pT_trans"][(bb * NCH + d8) % 2] = ck
            tr_toks.append(ck)
        selmms = []
        for hh in range(2):
            mm = mmg(pX[hh][0:2, 0:TC],
                     [(selbf[:, :], r9tm[0:8, hh * TC:(hh + 1) * TC])],
                     tr_toks + [state["pX_evac"][hh]])
            tok = vi(lambda h, hh=hh: h.tensor_copy(
                r9tm[0:2, hh * TC:(hh + 1) * TC],
                pX[hh][0:2, 0:TC]), [mm])
            state["pX_evac"][hh] = tok
            selmms.append(tok)
        for d8 in range(NCH):
            pTb = pT[(bb * NCH + d8) % 2].ap().bitcast(BF16)
            tk = P.add("T", lambda h, d8=d8, pTb=pTb: h.transpose(
                pTb[0:128, 0:2],
                r9tm[0:2, d8 * 128: (d8 + 1) * 128],
                sidbf[0:2, 0:2]),
                selmms + [state["pT_trans"][(bb * NCH + d8) % 2]])
            ck = ci(lambda h, bb=bb, d8=d8, pTb=pTb: h.copy(
                r9sel[:, (d8 * B + bb) * 2: (d8 * B + bb) * 2 + 2],
                pTb[0:128, 0:2]), [tk])
            state["pT_trans"][(bb * NCH + d8) % 2] = ck
            sel_back.append(ck)
        state["r9tm_guard"] = sel_back[-NCH:]
    state["pR_reader"] = sel_back[-1]

    def E_4d(l, j0, n, step=1):
        apx = Ebuf[:, EOFF[l]: EOFF[l] + NCH * B * NLP[l]].rearrange(
            "p (ab j) -> p ab j", j=NLP[l])
        if step == 1:
            return apx[:, :, j0: j0 + n]
        return apx[:, :, j0: j0 + (n - 1) * step + 1: step]

    def r9sel_4d(which):
        apx = r9sel.ap().rearrange("p (ab w) -> p ab w", w=2)
        return apx[:, :, which:which + 1]

    def r9sel_col(k8, which):
        apx = r9sel[:, k8 * B * 2: (k8 + 1) * B * 2].rearrange(
            "p (b w) -> p b w", w=2)
        return apx[:, :, which:which + 1]

    MB = 9 * B
    all_er = sum((state["E_ready"][l] for l in range(10)), [])
    stage_deps = sel_back + [t_m] + all_er

    def stg_l(k8):
        return actB[:, k8 * TC: k8 * TC + MB]

    def stg_r(k8):
        return actB[:, k8 * TC + 64: k8 * TC + 64 + MB]

    aL3 = actB.ap().rearrange("p (a c) -> p a c", c=TC)
    selw0 = r9sel.ap().rearrange("p (a b w) -> p a b w", b=B, w=2)[:, :, :, 0]
    stg_toks = []
    for l in range(8, -1, -1):
        li = 8 - l
        ev = Ebuf[:, EOFF[l]: EOFF[l] + NCH * B * NLP[l]].rearrange(
            "p (a b j) -> p a b j", b=B, j=NLP[l])[:, :, :, 0]
        t1 = gcp(aL3[:, :, li * B: (li + 1) * B], selw0, stage_deps)
        t2 = gcp(aL3[:, :, 64 + li * B: 64 + (li + 1) * B], ev, [t1])
        stg_toks.append(t2)
    mb_toks = comb(MB, stg_l, stg_r, lambda d8: stg_r(d8), blend=True,
                   in_ready=stg_toks + sel_back + [t_m])
    sc_tok = {}
    for l in range(8, -1, -1):
        li = 8 - l
        ev = Ebuf[:, EOFF[l]: EOFF[l] + NCH * B * NLP[l]].rearrange(
            "p (a b j) -> p a b j", b=B, j=NLP[l])[:, :, :, 0]
        sc_tok[l] = gcp(ev, aL3[:, :, 64 + li * B: 64 + (li + 1) * B],
                        mb_toks)
    sc_all = [sc_tok[l] for l in range(9)]

    prev_ready = sel_back + sc_all
    for l in range(8, -1, -1):
        n = NL[l]
        half = NL[l + 1]
        if l == 8:
            ps = gcp(E_4d(8, 1, 1, 2), r9sel_4d(1),
                     prev_ready + state["E_ready"][8])
        else:
            ps = gcp(E_4d(l, 1, half, 2), E_4d(l + 1, 0, half),
                     prev_ready + state["E_ready"][l] +
                     [sc_tok[l + 1], sc_tok[l]])
        ready = [ps, sc_tok[l]]
        if n // 2 >= 2:
            dn = comb(B * (n // 2 - 1),
                      lambda k8, l=l, n=n: E_all(l + 1, k8, 0, n // 2 - 1),
                      lambda k8, l=l, n=n: E_all(l, k8, 2, n // 2 - 1, 2),
                      lambda d8, l=l, n=n: E_all(l, d8, 2, n // 2 - 1, 2),
                      in_ready=prev_ready + state["E_ready"][l] +
                      [sc_tok[l + 1]])
            ready += dn
        prev_ready = ready
        state["E_ready"][l] = ready
    qs_ready = prev_ready + sc_all
    xcol_toks = comb(B, lambda k8: r9sel_col(k8, 1),
                     lambda k8: qlastB[:, k8 * B * 2: (k8 + 1) * B * 2: 2],
                     lambda d8: xcolB[:, d8 * B * 2: (d8 + 1) * B * 2: 2],
                     in_ready=sel_back + state["E_ready"][0])

    p5bar = qs_ready + xcol_toks
    prev_outs = []
    prev_ymm = None
    for b in range(B):
        p = None
        for d8 in range(NCH):
            deps0 = []
            if d8 == 0:
                deps0 = (p5bar if b == 0 else []) +                     ([prev_ymm] if prev_ymm else [])
            ld = dma("S", cvt3[:, :], _r3(wv_in)[:, d8],
                     deps0 + ([p] if p else []))
            p = vi(lambda h, d8=d8: h.tensor_copy(
                wbf[:, d8 * D:(d8 + 1) * D], cvt3[:, :]), [ld])
        wv_done = p
        ldx = dma("S", xtl2.ap().rearrange("p (a c) -> p a c", c=TC + 1),
                  xs_in[b].transpose([1, 0, 2]),
                  state["pB_readers"] + prev_outs +
                  (sc_all if b == 0 else []))
        xn_toks = emit_ln(xtl2, xbf3, sqb3, 1,
                          lambda d8: actB[:, d8 * TC:(d8 + 1) * TC], ldx)
        v_toks = []
        for et in range(NCH):
            mm = mmg(pX[et % 2][:, 0:TC],
                     [(wbf[:, k8 * D + et * 128: k8 * D + et * 128 + 128],
                       actB[:, k8 * TC:(k8 + 1) * TC]) for k8 in range(NCH)],
                     xn_toks + [wv_done, state["pX_evac"][et % 2]])
            tok = vi(lambda h, et=et: h.tensor_copy(
                vtile[:, et * TC:(et + 1) * TC], pX[et % 2][:, 0:TC]), [mm])
            state["pX_evac"][et % 2] = tok
            v_toks.append(tok)
        qsh3 = qsh.ap().rearrange("p (a t) -> p a t", t=TC)
        E03 = Ebuf[:, EOFF[0]: EOFF[0] + NCH * B * TC].rearrange(
            "p (a b j) -> p a b j", b=B, j=TC)
        qd1 = gcp(qsh3[:, :, 0:TC - 1], E03[:, :, b, 1:TC],
                  (p5bar if b == 0 else []) + state.get("qsh_guard", []))
        qd2 = gcp(qsh3[:, :, TC - 1:TC],
                  xcolB.ap().rearrange("p (a q) -> p a q", q=B * 2
                                       )[:, :, b * 2: b * 2 + 1], [qd1])
        quv_toks = []
        for d8 in range(NCH):
            tok = vi(lambda h, d8=d8: h.tensor_mul(
                actB[:, d8 * TC: (d8 + 1) * TC],
                qsh[:, d8 * TC: (d8 + 1) * TC],
                vtile[:, d8 * TC: (d8 + 1) * TC]),
                [v_toks[d8], qd2])
            quv_toks.append(tok)
        state["qsh_guard"] = quv_toks
        p = None
        for d8 in range(NCH):
            ld = dma("S", cvt3[:, :], _r3(wc_in)[:, d8],
                     [quv_toks[-1]] if d8 == 0 else [p])
            p = vi(lambda h, d8=d8: h.tensor_copy(
                wbf[:, d8 * D:(d8 + 1) * D], cvt3[:, :]), [ld])
        wc_done = p
        gld = dma("G", gt2.ap().rearrange("p (a c) -> p a c", c=TC),
                  gdram[:, :, b * TC:(b + 1) * TC].transpose([1, 0, 2]),
                  (g_spills if b == 0 else []) + prev_outs)
        od_prev = None
        cur_outs = []
        for et in range(NCH):
            mm = mmg(pX[et % 2][:, 0:TC],
                     [(wbf[:, k8 * D + et * 128: k8 * D + et * 128 + 128],
                       actB[:, k8 * TC:(k8 + 1) * TC]) for k8 in range(NCH)],
                     quv_toks + [wc_done, state["pX_evac"][et % 2]])
            prev_ymm = mm
            ot_tok = vi(lambda h, et=et, b=b: (
                h.tensor_mul(otile[:, :], pX[et % 2][:, 0:TC],
                             gt2[:, et * TC:(et + 1) * TC]),
                h.tensor_add(otile[:, :], otile[:, :],
                             xtl2[:, et * (TC + 1) + 1: et * (TC + 1) + 1 + TC])
            )[-1], [mm, gld] + ([od_prev] if od_prev else []))
            state["pX_evac"][et % 2] = ot_tok
            od_prev = dma("S", out_d[b, et], otile[:, :], [ot_tok])
            cur_outs.append(ot_tok)
        prev_outs = cur_outs + [od_prev]

    if dbg:
        dbg_pt = nc.dram_tensor("dbg_pt", [2, D, D], BF16, kind="ExternalOutput").ap()
        dbg_g = nc.dram_tensor("dbg_g", [NCH, 128, B * TC], BF16, kind="ExternalOutput").ap()
        dbg_E = nc.dram_tensor("dbg_E", [128, E_COLS], BF16, kind="ExternalOutput").ap()
        dbg_sel = nc.dram_tensor("dbg_sel", [128, NCH * B * 2], BF16, kind="ExternalOutput").ap()
        dbg_xc = nc.dram_tensor("dbg_xc", [128, NCH * B * 2], BF16, kind="ExternalOutput").ap()
        dbg_ql = nc.dram_tensor("dbg_ql", [128, NCH * B * 2], BF16, kind="ExternalOutput").ap()
        dbg_e9 = nc.dram_tensor("dbg_e9", [128, NCH * B * 8], BF16, kind="ExternalOutput").ap()
        dbg_sg = nc.dram_tensor("dbg_sg", [8, 128, NCH, B], BF16, kind="ExternalOutput").ap()
        dbg_sgi = nc.dram_tensor("dbg_sgi", [128, NCH, B], BF16, kind="ExternalOutput").ap()
        bar = P.barrier()
        dma("S", dbg_pt[:, :, :], pboth[:, :, :], bar)
        dma("S", dbg_g[:, :, :], gdram[:, :, :], bar)
        dma("S", dbg_E[:, :], Ebuf[:, :], bar)
        dma("S", dbg_sel[:, :], r9sel[:, :], bar)
        dma("S", dbg_xc[:, :], xcolB[:, :], bar)
        dma("S", dbg_ql[:, :], qlastB[:, :], bar)
        dma("S", dbg_e9[:, :], e9all[:, :], bar)
        dma("S", dbg_sg[:, :, :, :], sc_gout[:, :, :, :], bar)
        dma("S", dbg_sgi[:, :, :], sc_gin[:, :, :], bar)
    FIN = dict(S=P.count["S"], G=P.count["G"], K=P.count["K"],
               T=P.count["T"], V=P.count["V"], C=P.count["C"])

    csum = []
    acc = 0
    for c_ in P.tcost[:P2_END]:
        acc += c_
        csum.append(acc)
    cuts = [0]
    j = 0
    for k in range(1, NIT):
        tgt = acc * k / NIT
        while j < P2_END and csum[j] < tgt:
            j += 1
        cuts.append(j)
    cuts.append(P2_END)

    block = e(nc.Block())

    @block.sync
    def _(sync):
        sync.dma_start(out=Xf.ap().rearrange("p (a d) -> p a d", d=D),
                       in_=_r3(nsW_in)).then_inc(dL, 16)
        sync.dma_start(out=Xse.ap().rearrange("p (a c) -> p a c", c=CS),
                       in_=_r3(nsWs_in)).then_inc(dL, 16)
        sync.dma_start(out=sid[:], in_=id_in[:]).then_inc(dL, 16)
        sync.dma_start(out=m0v[:, :], in_=m0_in).then_inc(dL, 16)
        for k in range(NIT):
            sync.wait_ge(vU, 8 * (k + 1))
            src = XsE(k).ap()
            if XsE(k) is Xsr:
                src = src.bitcast(F32)
            sync.dma_start(out=_r3(gin),
                           in_=src.rearrange("p (a c) -> p a c", c=CS)
                           ).then_inc(dL, 16)
            sync.wait_ge(cT, 64 + 16 * (k + 1))
            sync.dma_start(out=_r3(gin2),
                           in_=XTs.ap().rearrange("p (h d) -> p h d", d=D)
                           ).then_inc(dL, 16)
            P.replay("S", sync, SEMS, lo=cuts[k], hi=cuts[k + 1])
        sync.wait_ge(nX, 7)
        sync.dma_start(out=_r3(pexch),
                       in_=PBF.ap().rearrange("p (a d) -> p a d", d=D)
                       ).then_inc(dL, 16)
        P.replay("S", sync, SEMS, lo=P2_END)
        sync.wait_ge(eS, FIN["S"])

    @block.vector
    def _(vector):
        vector.wait_ge(dL, 64)
        vector.memset(sones[:], 1.0)
        vector.memset(sonesr[:], 1.0)
        for r8 in range(NCH):
            xb = Xf[:, r8 * D:(r8 + 1) * D]
            vector.tensor_mul(ssq, xb, xb)
            vector.tensor_reduce(sstat[:, r8:r8 + 1], ssq, AX.X, ALU.add)
        vector.tensor_reduce(sstat1, sstat, AX.X, ALU.add).then_inc(nX, 1)
        vector.wait_ge(nX, 2)
        vector.reciprocal(snrm_r, snrm_s).then_inc(nX, 1)
        vector.wait_ge(nX, 4)
        vector.tensor_scalar(Xse[:], Xse[:], sscal, None, ALU.mult
                             ).then_inc(nX, 1)
        for k in range(NIT):
            a_k, b_k = NS_SCHED[k]
            for d1 in range(NCH):
                if d1 == 0 and k >= 1:
                    vector.wait_ge(dL, 64 + 32 * k)
                vector.wait_ge(tG, 2 + 8 * k + d1 + 1)
                dst = AmK(k)[:, d1 * CS:(d1 + 1) * CS]
                if k == 0:
                    vector.tensor_scalar(dst, pAs[d1 % 2], sscal, None,
                                         ALU.mult).then_inc(vA, 1)
                else:
                    vector.tensor_copy(dst, pAs[d1 % 2]).then_inc(vA, 1)
            for r8 in range(NCH):
                vector.wait_ge(tU, 8 * k + r8 + 1)
                sl = slice(r8 * CS, (r8 + 1) * CS)
                srcb = XsE(k - 1)[:, sl]
                dstb = XsE(k)[:, sl]
                vector.tensor_scalar(dstb, srcb, a_k, None, ALU.mult)
                vector.tensor_scalar(pUs[r8 % 2], pUs[r8 % 2], b_k,
                                     None, ALU.mult)
                if k == 0:
                    vector.tensor_scalar(pUs[r8 % 2], pUs[r8 % 2], sscal,
                                         None, ALU.mult)
                vector.tensor_add(dstb, dstb, pUs[r8 % 2]).then_inc(vU, 1)
            P.replay("V", vector, SEMS, lo=cuts[k], hi=cuts[k + 1])
        vector.wait_ge(dRs, 128 * NIT)
        vector.tensor_copy(PBF[:, :], XTr.ap().bitcast(F32)).then_inc(nX, 2)
        P.replay("V", vector, SEMS, lo=P2_END)

    @block.scalar
    def _(scalar):
        scalar.wait_ge(tG, 1)
        scalar.activation(snrm_s, pN1, AF.Sqrt).then_inc(nX, 1)
        scalar.wait_ge(tG, 2)
        scalar.copy(sscal, pN2).then_inc(nX, 1)
        for t in range(64):
            scalar.wait_ge(tT, t + 1)
            dst = tr_dst(t, 0).bitcast(F32)
            scalar.copy(dst, pTs[t % 2]).then_inc(cT, 1)
        for r in range(1, NIT + 1):
            k = r - 1
            scalar.wait_ge(tU, 8 * r)
            if k >= 1:
                scalar.wait_ge(dL, 64 + 32 * k)
            for t in range(16):
                scalar.wait_ge(tT, 64 + 16 * k + t + 1)
                r8, h = divmod(t, 2)
                scalar.copy(XTs[:, h * D + r8 * 128: h * D + r8 * 128 + 128],
                            pTs[t % 2]).then_inc(cT, 1)
            if r < NIT:
                P.replay("C", scalar, SEMS, lo=cuts[r - 1], hi=cuts[r])
        P.replay("C", scalar, SEMS, lo=cuts[NIT - 1])

    @block.gpsimd
    def _(gpsimd):
        for k in range(NIT):
            gpsimd.wait_ge(dL, 64 + 32 * k + 16)
            gpsimd.collective_compute(
                "AllGather", ALU.bypass, replica_groups=G4,
                ins=[gin[:, :]], outs=[gout[:, :, :]],
            ).then_inc(cc, 1)
            gpsimd.wait_ge(dL, 64 + 32 * k + 32)
            gpsimd.wait_ge(cc, 2 * k + 1)
            gpsimd.collective_compute(
                "AllGather", ALU.bypass, replica_groups=G4,
                ins=[gin2[:, :]], outs=[gout2[:, :, :]],
            ).then_inc(cc, 1)
            P.replay("G", gpsimd, SEMS, lo=cuts[k - 1] if k >= 1 else 0,
                     hi=cuts[k])
            gpsimd.wait_ge(cc, 2 * (k + 1))
            gpsimd.wait_ge(tG, 2 + 8 * (k + 1))
            gpsimd.wait_ge(tU, 8 * (k + 1))
            gpsimd.wait_ge(tT, 64)
            tgt = Xfr if (k + 1) >= NSPLIT else Xf
            for g in range(4):
                gpsimd.dma_start(
                    out=tgt.ap().rearrange("p (a d) -> p a d", d=D
                                           )[:, :, g * CS:(g + 1) * CS],
                    in_=_r3(gout[g]),
                ).then_inc(dRs, 16)
            tgt2 = XTr if (k + 1) >= NSPLIT else XTe
            for g in range(4):
                gpsimd.dma_start(
                    out=tgt2.ap().rearrange("p (a d) -> p a d", d=D
                                            )[:, 2 * g: 2 * g + 2, :],
                    in_=_r3(gout2[g]),
                ).then_inc(dRs, 16)
        gpsimd.wait_ge(dL, 64 + 32 * NIT + 16)
        gpsimd.collective_compute(
            "AllGather", ALU.bypass, replica_groups=GPAIR,
            ins=[pexch[:, :]], outs=[pboth[:, :, :]],
        ).then_inc(cc, 1)
        gpsimd.wait_ge(cc, 2 * NIT + 1)
        gpsimd.dma_start(out=P1T.ap().rearrange("p (a d) -> p a d", d=D),
                         in_=_r3(pboth[0])).then_inc(dRs, 16)
        gpsimd.dma_start(out=P2T.ap().rearrange("p (a d) -> p a d", d=D),
                         in_=_r3(pboth[1])).then_inc(dRs, 16)
        gpsimd.wait_ge(dRs, 128 * NIT + 32)
        gpsimd.memset(smallf[0:1, 63:64], 0.0).then_inc(nX, 1)
        P.replay("GKH", gpsimd, SEMS, lo=cuts[NIT - 1])
        gpsimd.wait_ge(eG, FIN["G"])
        gpsimd.wait_ge(eK, FIN["K"])

    @block.tensor
    def _(tensor):
        tensor.wait_ge(nX, 1)
        tensor.matmul(pN1, sones[:], sstat1, start=True, stop=True
                      ).then_inc(tG, 1)
        tensor.wait_ge(nX, 3)
        tensor.matmul(pN2, sonesr[:], snrm_r, start=True, stop=True
                      ).then_inc(tG, 1)
        tensor.wait_ge(dL, 64)
        tensor.wait_ge(nX, 4)
        for t in range(64):
            if t >= 2:
                tensor.wait_ge(cT, t - 1)
            tensor.transpose(pTs[t % 2], tr_src(t, 0).bitcast(F32)
                             if False else tr_src(t, 0),
                             sid.ap().bitcast(F32)).then_inc(tT, 1)
        for k in range(NIT):
            P.replay("T", tensor, SEMS, lo=cuts[k - 1] if k >= 1 else 0,
                     hi=cuts[k])
            late = k >= NSPLIT
            if k >= 1:
                tensor.wait_ge(dRs, 128 * k - 64)
            tensor.wait_ge(vU, 8 * k)
            if k == 0:
                tensor.wait_ge(nX, 5)
            for d1 in range(NCH):
                if d1 >= 2:
                    tensor.wait_ge(vA, 8 * k + d1 - 1)
                ins = None
                for r8 in range(NCH):
                    o = r8 * D + d1 * 128
                    lhs = XfR(k)[:, o:o + 128]
                    rhs = XsE(k - 1)[:, r8 * CS:(r8 + 1) * CS]
                    ins = tensor.matmul(pAs[d1 % 2], lhs, rhs,
                                        start=(r8 == 0), stop=(r8 == NCH - 1))
                ins.then_inc(tG, 1)
            tensor.wait_ge(vA, 8 * (k + 1))
            if k == 0:
                tensor.wait_ge(cT, 64)
            else:
                tensor.wait_ge(dRs, 128 * k)
            for r8 in range(NCH):
                if r8 >= 2:
                    tensor.wait_ge(vU, 8 * k + r8 - 1)
                ins = None
                for d8 in range(NCH):
                    o = d8 * D + r8 * 128
                    ins = tensor.matmul(pUs[r8 % 2], XTk(k)[:, o:o + 128],
                                        AmK(k)[:, d8 * CS:(d8 + 1) * CS],
                                        start=(d8 == 0), stop=(d8 == NCH - 1))
                ins.then_inc(tU, 1)
            tensor.wait_ge(vU, 8 * (k + 1))
            src_r = XsE(k) is Xsr
            for t in range(16):
                tensor.wait_ge(cT, 64 + 16 * k + t - 1)
                r8, h = divmod(t, 2)
                blk_src = XsE(k)[:, r8 * CS + h * 128: r8 * CS + h * 128 + 128]
                out_ap = pTs[t % 2].bitcast(F32R) if src_r else pTs[t % 2]
                id_ap = sid.ap() if src_r else sid.ap().bitcast(F32)
                tensor.transpose(out_ap, blk_src, id_ap).then_inc(tT, 1)
        P.replay("T", tensor, SEMS, lo=cuts[NIT - 1])

    build.P = P
    build.cuts = cuts
    build.P2_END = P2_END
    build.FIN = FIN
    st.pop_all()
    return nc


def kernel(**inputs):
    x = np.asarray(inputs["L0_attn_x"], np.float32)
    Wq = np.asarray(inputs["L0_attn_Wq"], np.float32)
    Wv = np.asarray(inputs["L0_attn_Wv"], np.float32)
    Wc = np.asarray(inputs["L0_attn_Wc"], np.float32)
    Wp1 = np.asarray(inputs["L0_attn_Wp1"], np.float32)
    Wp2 = np.asarray(inputs["L0_attn_Wp2"], np.float32)
    ident = np.asarray(inputs["L0_attn_identity"], np.float32)
    qw = np.asarray(inputs["L0_ffn_qweights"], np.int32)
    scales = np.asarray(inputs["L0_ffn_scales"], np.float32)
    biases = np.asarray(inputs["L0_ffn_biases"], np.float32)
    la = np.asarray(inputs["L0_ffn_lora_a"], np.float32)
    lb = np.asarray(inputs["L0_ffn_lora_b"], np.float32)
    gam = np.asarray(inputs["L0_ffn_ln_gamma"], np.float32)
    bet = np.asarray(inputs["L0_ffn_ln_beta"], np.float32)

    bf = ml_dtypes.bfloat16
    eye = np.eye(128, dtype=np.float32)
    qw_t = np.ascontiguousarray(qw.transpose(0, 2, 1))
    srep_t = np.ascontiguousarray(
        np.repeat(scales, 16, axis=2).transpose(0, 2, 1))
    sT_t = np.ascontiguousarray(scales.transpose(0, 2, 1))
    la_t = np.ascontiguousarray(la.transpose(0, 2, 1))
    lb_t = np.ascontiguousarray(lb.transpose(0, 2, 1))
    gam_fm = np.ascontiguousarray(
        gam.reshape(3, NCH, 128).transpose(2, 0, 1).reshape(128, 3 * NCH))
    bet_fm = np.ascontiguousarray(
        bet.reshape(3, NCH, 128).transpose(2, 0, 1).reshape(128, 3 * NCH))
    bias_fm = np.ascontiguousarray(
        biases.reshape(6, NCH, 128).transpose(2, 0, 1).reshape(128, 48))
    pool = np.zeros((NCH, 128, 64), np.float32)
    for d8 in range(NCH):
        for p_ in range(128):
            g_ = d8 * 8 + p_ // 16
            pool[d8, p_, g_] = 1.0
    idvec_fm = np.ascontiguousarray(
        ident.reshape(NCH, 128).T)
    wq_t = np.ascontiguousarray(Wq.T)
    wv_t = np.ascontiguousarray(Wv.T)
    wc_t = np.ascontiguousarray(Wc.T)

    xpad = np.concatenate([np.zeros((B, 1, D), np.float32), x], axis=1)

    nc = build()
    in_maps = []
    for c in range(8):
        W = Wp1 if c < 4 else Wp2
        s_ = c % 4
        xs = xpad[:, TC * c: TC * c + TC + 1, :]
        xs_fm = np.ascontiguousarray(
            xs.transpose(0, 2, 1).reshape(B, NCH, 128, TC + 1))
        sel = np.zeros((8, 2), np.float32)
        if c > 0:
            sel[c - 1, 0] = 1.0
        sel[c, 1] = 1.0
        m0 = np.zeros((128, 2), np.float32)
        m0[:, 0] = 0.0 if c == 0 else 1.0
        m0[:, 1] = 1.0 if c == 0 else 0.0
        in_maps.append({
            "xs": xs_fm, "wq_t": wq_t, "wv_t": wv_t, "wc_t": wc_t,
            "nsW": W, "nsWs": np.ascontiguousarray(W[:, CS * s_:CS * (s_ + 1)]),
            "ident128": eye, "qw_t": qw_t, "srep_t": srep_t, "sT_t": sT_t,
            "bias_t": biases, "bias_fm": bias_fm, "la_t": la_t, "lb_t": lb_t,
            "gam_fm": gam_fm, "bet_fm": bet_fm, "pool_m": pool,
            "sel_m": sel, "m0v": m0, "idvec_fm": idvec_fm,
        })
    res = run_bass_kernel_spmd(nc, in_maps, core_ids=list(range(8)))
    out = np.empty((B, 4096, D), np.float32)
    for c in range(8):
        o = res.results[c]["out"]
        out[:, TC * c: TC * (c + 1), :] = o.reshape(B, D, TC).transpose(0, 2, 1)
    return out


# version 08
# speedup vs baseline: 1.1179x; 1.0145x over previous
import numpy as np
import ml_dtypes
import concourse.bass as bass
import concourse.mybir as mybir
from concourse.bass_utils import run_bass_kernel_spmd
from contextlib import ExitStack

F32 = mybir.dt.float32
F32R = mybir.dt.float32r
BF16 = mybir.dt.bfloat16
I32 = mybir.dt.int32
ALU = mybir.AluOpType
AF = mybir.ActivationFunctionType
AX = mybir.AxisListType

D = 1024
CS = 256
NCH = 8
B = 4
TC = 512
NS_SCHED = [
    (5.1961444193, -5.1961418212),
    (2.5980691358, -0.6495181190),
    (2.5980578201, -0.6495166383),
    (2.5980284218, -0.6495127916),
    (2.5979520476, -0.6495027980),
    (2.5977536542, -0.6494768381),
    (2.5972384338, -0.6494094201),
    (2.5959013401, -0.6492344501),
    (2.5924371443, -0.6487810349),
    (2.5835036800, -0.6476114866),
    (2.5607332767, -0.6446274371),
    (2.5044075629, -0.6372283209),
    (2.3750656855, -0.6201377464),
    (2.1254870094, -0.5867349626),
    (1.7905710134, -0.5409002201),
    (1.5572128750, -0.5081481882),
    (1.5021226287, -0.5003031936),
]
NIT = len(NS_SCHED)
NSPLIT = 8
G4 = [[0, 1, 2, 3], [4, 5, 6, 7]]
GPAIR = [[0, 4], [1, 5], [2, 6], [3, 7]]
G8 = [[0, 1, 2, 3, 4, 5, 6, 7]]

NL = [TC >> l for l in range(10)]
NLP = [max(n, 2) for n in NL]
EOFF = []
_o = 0
for _l in range(10):
    EOFF.append(_o)
    _o += NCH * B * NLP[_l]
E_COLS = _o
RMS_EPS = 1e-8
LN_EPS = 1e-5
R75 = 1.0 / 7.5


def _r3(ap, p=128):
    return ap.rearrange("(a p) c -> a p c", p=p).transpose([1, 0, 2])


class Plan:
    def __init__(self):
        self.items = []
        self.count = {k: 0 for k in "TVCSGKH"}
        self.tcost = []

    def add(self, eng, fn, deps=(), cost=0):
        step = 16 if eng in "SG" else 1
        self.count[eng] += step
        tok = (eng, self.count[eng])
        self.items.append((eng, fn, [d for d in deps if d is not None], tok))
        self.tcost.append(cost if eng == "T" else 0)
        return tok

    def barrier(self):
        return [(k, v) for k, v in self.count.items() if v]

    def replay(self, keys, handle, sems, lo=0, hi=None):
        for eng, fn, deps, tok in self.items[lo:hi]:
            if eng not in keys:
                continue
            need = {}
            for k, val in deps:
                if k in keys:
                    continue
                sem = sems[k]
                kk = id(sem)
                need[kk] = (sem, max(val, need[kk][1]) if kk in need else val)
            for sem, val in need.values():
                handle.wait_ge(sem, val)
            fn(handle).then_inc(sems[eng], 16 if eng in "SG" else 1)


def build(dbg=False):
    nc = bass.Bass("TRN2", target_bir_lowering=False, debug=False)
    xs_in = nc.dram_tensor("xs", [B, NCH, 128, TC + 1], F32, kind="ExternalInput").ap()
    wq_in = nc.dram_tensor("wq_t", [D, D], F32, kind="ExternalInput").ap()
    wv_in = nc.dram_tensor("wv_t", [D, D], F32, kind="ExternalInput").ap()
    wc_in = nc.dram_tensor("wc_t", [D, D], F32, kind="ExternalInput").ap()
    nsW_in = nc.dram_tensor("nsW", [D, D], F32, kind="ExternalInput").ap()
    nsWs_in = nc.dram_tensor("nsWs", [D, CS], F32, kind="ExternalInput").ap()
    id_in = nc.dram_tensor("ident128", [128, 128], F32R, kind="ExternalInput").ap()
    qw_in = nc.dram_tensor("qw_t", [6, D, D], I32, kind="ExternalInput").ap()
    srep_in = nc.dram_tensor("srep_t", [6, D, D], F32, kind="ExternalInput").ap()
    sT_in = nc.dram_tensor("sT_t", [6, 64, D], F32, kind="ExternalInput").ap()
    bias_in = nc.dram_tensor("bias_t", [6, D], F32, kind="ExternalInput").ap()
    biasfm_in = nc.dram_tensor("bias_fm", [128, 48], F32, kind="ExternalInput").ap()
    la_in = nc.dram_tensor("la_t", [6, D, 32], F32, kind="ExternalInput").ap()
    lb_in = nc.dram_tensor("lb_t", [6, 32, D], F32, kind="ExternalInput").ap()
    gam_in = nc.dram_tensor("gam_fm", [128, 3 * NCH], F32, kind="ExternalInput").ap()
    bet_in = nc.dram_tensor("bet_fm", [128, 3 * NCH], F32, kind="ExternalInput").ap()
    pool_in = nc.dram_tensor("pool_m", [NCH, 128, 64], F32, kind="ExternalInput").ap()
    sel_in = nc.dram_tensor("sel_m", [8, 2], F32, kind="ExternalInput").ap()
    m0_in = nc.dram_tensor("m0v", [128, 2], F32, kind="ExternalInput").ap()
    idv_in = nc.dram_tensor("idvec_fm", [128, NCH], F32, kind="ExternalInput").ap()
    out_d = nc.dram_tensor("out", [B, NCH, 128, TC], F32, kind="ExternalOutput").ap()

    gin = nc.dram_tensor("ns_gin", [D, CS], F32).ap()
    gout = nc.dram_tensor("ns_gout", [4, D, CS], F32).ap()
    gin2 = nc.dram_tensor("ns_gin2", [2 * 128, D], F32).ap()
    gout2 = nc.dram_tensor("ns_gout2", [4, 2 * 128, D], F32).ap()
    pexch = nc.dram_tensor("ns_pexch", [D, D], BF16).ap()
    pboth = nc.dram_tensor("ns_pboth", [2, D, D], BF16).ap()
    gdram = nc.dram_tensor("g_spill", [NCH, 128, B * TC], BF16).ap()
    sc_gin = nc.dram_tensor("sc_gin", [128, NCH, B], BF16).ap()
    sc_gout = nc.dram_tensor("sc_gout", [8, 128, NCH, B], BF16).ap()

    st = ExitStack()
    e = st.enter_context

    sid = e(nc.sbuf_tensor("sid", [128, 128], F32R))
    sones = e(nc.sbuf_tensor("sones", [128, 1], F32))
    sonesr = e(nc.sbuf_tensor("sonesr", [1, 128], F32))
    sonebf = e(nc.sbuf_tensor("sonebf", [1, TC], BF16))
    sonesbf = e(nc.sbuf_tensor("sonesbf", [128, 1], BF16))
    m0v = e(nc.sbuf_tensor("m0v_s", [128, 2], F32))
    m0a = e(nc.sbuf_tensor("m0a_s", [128, 1], F32))
    m0bv = e(nc.sbuf_tensor("m0b_s", [128, 1], F32))
    selbf = e(nc.sbuf_tensor("selbf", [8, 2], BF16))
    idvbf = e(nc.sbuf_tensor("idvbf", [128, 2 * NCH], BF16))
    sidbf = e(nc.sbuf_tensor("sidbf", [128, 128], BF16))
    rowS = e(nc.sbuf_tensor("rowS", [1, 3 * TC], F32))
    fscr = e(nc.sbuf_tensor("fscr", [128, TC], F32))
    fscr2 = e(nc.sbuf_tensor("fscr2", [128, 40], F32))
    fscr3 = e(nc.sbuf_tensor("fscr3", [128, 40], F32))
    idvm = e(nc.sbuf_tensor("idvm", [128, 8], F32))
    smallf = e(nc.sbuf_tensor("smallf", [128, 64], F32))
    cstf = e(nc.sbuf_tensor("cstf", [128, 4], F32))

    base0 = nc.sbuf_base

    def alias_pair(name, shape, dt1, dt2):
        b = nc.sbuf_base
        a1 = nc.alloc_sbuf_tensor(name + "_e", shape, dt1)
        nc.sbuf_base = b
        a2 = nc.alloc_sbuf_tensor(name + "_r", shape, dt2)
        return a1, a2

    xf_base = nc.sbuf_base
    Xf, Xfr = alias_pair("Xf", [128, NCH * D], F32, F32R)
    XTe, XTr = alias_pair("XT", [128, NCH * D], F32, F32R)
    xse_base = nc.sbuf_base
    Xse, Xsr = alias_pair("Xs", [128, NCH * CS], F32, F32R)
    am_base = nc.sbuf_base
    Ame, Amr = alias_pair("Am", [128, NCH * CS], F32, F32R)
    sv_b = nc.sbuf_base
    nc.sbuf_base = am_base
    XTs = nc.alloc_sbuf_tensor("XTs", [128, 2 * D], F32)
    nc.sbuf_base = sv_b
    ns_small = nc.alloc_sbuf_tensor("ns_small", [128, D + 64], F32)
    p1_end = nc.sbuf_base
    nc.sbuf_base = xse_base
    PBF = nc.alloc_sbuf_tensor("PBF", [128, NCH * D], BF16)
    nc.sbuf_base = xf_base
    P1T = nc.alloc_sbuf_tensor("P1T", [128, NCH * D], BF16)
    P2T = nc.alloc_sbuf_tensor("P2T", [128, NCH * D], BF16)
    p1t_end = nc.sbuf_base

    nc.sbuf_base = p1_end
    hB = nc.alloc_sbuf_tensor("hB", [128, NCH * B * TC], BF16)
    qt = [nc.alloc_sbuf_tensor(f"qt{i}", [128, NCH * D], BF16) for i in range(2)]
    qraw = nc.alloc_sbuf_tensor("qraw", [128, 512], I32)
    qfsc = nc.alloc_sbuf_tensor("qfsc", [128, 512], F32)
    srepc = nc.alloc_sbuf_tensor("srepc", [128, 512], F32)
    cvt32 = nc.alloc_sbuf_tensor("cvt32", [128, TC + 4], F32)
    xstg = nc.alloc_sbuf_tensor("xstg", [128, TC + 1], F32)
    nsT2 = nc.alloc_sbuf_tensor("nsT2", [64, 2 * D], BF16)
    lb2 = nc.alloc_sbuf_tensor("lb2", [32, 2 * D], BF16)
    labf = nc.alloc_sbuf_tensor("labf", [128, 6 * NCH * 32], BF16)
    biasc = nc.alloc_sbuf_tensor("biasc", [128, 48], F32)
    gamb = nc.alloc_sbuf_tensor("gamb", [128, 3 * NCH], F32)
    betb = nc.alloc_sbuf_tensor("betb", [128, 3 * NCH], F32)
    poolbf = nc.alloc_sbuf_tensor("poolbf", [128, NCH * 64], BF16)
    xbfB = nc.alloc_sbuf_tensor("xbfB", [128, NCH * (TC + 2)], BF16)
    sqB = nc.alloc_sbuf_tensor("sqB", [128, NCH * (TC + 2)], BF16)
    y1B = nc.alloc_sbuf_tensor("y1B", [128, NCH * TC], BF16)
    auxB = nc.alloc_sbuf_tensor("auxB", [64, TC], BF16)
    auxB2 = nc.alloc_sbuf_tensor("auxB2", [32, TC], BF16)
    p2_end = nc.sbuf_base

    nc.sbuf_base = p1t_end
    Ebuf = nc.alloc_sbuf_tensor("Ebuf", [128, E_COLS], BF16)
    wbf = nc.alloc_sbuf_tensor("wbf", [128, NCH * D], BF16)
    xtl2 = nc.alloc_sbuf_tensor("xtl2", [128, NCH * (TC + 1)], F32)
    zscr = nc.alloc_sbuf_tensor("zscr", [128, NCH * TC], BF16)
    cvt3 = nc.alloc_sbuf_tensor("cvt3", [128, D], F32)
    xbf3 = nc.alloc_sbuf_tensor("xbf3", [128, NCH * (TC + 2)], BF16)
    sqb3 = nc.alloc_sbuf_tensor("sqb3", [128, NCH * (TC + 2)], BF16)
    actB = nc.alloc_sbuf_tensor("actB", [128, NCH * TC], BF16)
    vtile = nc.alloc_sbuf_tensor("vtile", [128, NCH * TC], BF16)
    gt2 = nc.alloc_sbuf_tensor("gt2", [128, NCH * TC], BF16)
    qsh = nc.alloc_sbuf_tensor("qsh", [128, NCH * TC], BF16)
    otile = nc.alloc_sbuf_tensor("otile", [128, TC], F32)
    e9all = nc.alloc_sbuf_tensor("e9all", [128, NCH * B * 8], BF16)
    emid = nc.alloc_sbuf_tensor("emid", [128, NCH * B * 8], BF16)
    rmid = nc.alloc_sbuf_tensor("rmid", [128, NCH * B * 12], BF16)
    r9sel = nc.alloc_sbuf_tensor("r9sel", [128, NCH * B * 2], BF16)
    r9tm = nc.alloc_sbuf_tensor("r9tm", [8, D], BF16)
    qlastB = nc.alloc_sbuf_tensor("qlastB", [128, NCH * B * 2], BF16)
    xcolB = nc.alloc_sbuf_tensor("xcolB", [128, NCH * B * 2], BF16)
    p3_end = nc.sbuf_base
    nc.sbuf_base = max(p2_end, p3_end)

    pA = [e(nc.psum_tensor(f"pA{i}", [128, 512], F32)) for i in range(2)]
    pU = [e(nc.psum_tensor(f"pU{i}", [128, 512], F32)) for i in range(2)]
    pT = [e(nc.psum_tensor(f"pT{i}", [128, 512], F32)) for i in range(2)]
    pX = [e(nc.psum_tensor(f"pX{i}", [128, 512], F32)) for i in range(2)]
    pAs = [p[:, 0:CS] for p in pA]
    pUs = [p[:, 0:CS] for p in pU]
    pTs = [p[:, 0:128] for p in pT]
    pN1 = pT[0][0:1, 256:257]
    pN2 = pT[1][:, 256:257]
    pR0 = pT[0][0:1, 0:512]
    pR1 = pT[1][0:1, 0:512]

    dL = e(nc.semaphore("dL"))
    dO = e(nc.semaphore("dO"))
    dRs = e(nc.semaphore("dRs"))
    cc = e(nc.semaphore("cc"))
    tG = e(nc.semaphore("tG"))
    tU = e(nc.semaphore("tU"))
    tT = e(nc.semaphore("tT"))
    vA = e(nc.semaphore("vA"))
    vU = e(nc.semaphore("vU"))
    cT = e(nc.semaphore("cT"))
    nX = e(nc.semaphore("nX"))
    eT = e(nc.semaphore("eT"))
    eV = e(nc.semaphore("eV"))
    eC = e(nc.semaphore("eC"))
    eS = e(nc.semaphore("eS"))
    eG = e(nc.semaphore("eG"))
    eK = e(nc.semaphore("eK"))
    eH = e(nc.semaphore("eH"))
    SEMS = {"T": eT, "V": eV, "C": eC, "S": eS, "G": eG, "K": eK, "H": eH, "nX": nX}

    def XsE(k):
        return Xsr if k >= NSPLIT - 1 else Xse

    def XTk(r):
        return XTr if r >= NSPLIT else XTe

    def AmK(k):
        return Amr if k >= NSPLIT else Ame

    def XfR(r):
        return Xfr if r >= NSPLIT else Xf

    def tr_src(t, r):
        r8, tt = divmod(t, 8)
        o = r8 * D + tt * 128
        return XfR(r)[:, o:o + 128]

    def tr_dst(t, r):
        r8, tt = divmod(t, 8)
        o = tt * D + r8 * 128
        return XTk(r)[:, o:o + 128]

    ssq = ns_small[:, 0:D]
    sstat = ns_small[:, D:D + 8]
    sstat1 = ns_small[:, D + 8:D + 9]
    snrm_s = ns_small[0:1, D + 12:D + 13]
    snrm_r = ns_small[0:1, D + 14:D + 15]
    sscal = ns_small[:, D + 16:D + 17]

    P = Plan()
    state = {"pR_reader": None, "pB_readers": [], "pX_evac": [None, None],
             "pU_owner": None, "E_ready": {}, "pT_trans": [None, None]}

    def dma(eng, out, in_, deps=()):
        return P.add(eng, (lambda h, out=out, in_=in_:
                           h.dma_start(out=out, in_=in_)), deps)

    def gcp(out, in_, deps=()):
        return P.add("H", (lambda h, out=out, in_=in_:
                           h.tensor_copy(out, in_)), deps)

    def dmas(eng, out, in_, deps=()):
        return P.add(eng, (lambda h, out=out, in_=in_: h.dma_start(
            out=out, in_=in_, allow_slow_non_contiguous=True)), deps)

    def mmg(psum_ap, pairs, deps):
        def fn(h, psum_ap=psum_ap, pairs=pairs):
            ins = None
            n = len(pairs)
            for j, (l_, r_) in enumerate(pairs):
                ins = h.matmul(psum_ap, l_, r_, start=(j == 0),
                               stop=(j == n - 1))
            return ins
        try:
            cost = sum(128 + r_.shape[-1] for (l_, r_) in pairs)
        except Exception:
            cost = 640 * len(pairs)
        return P.add("T", fn, deps, cost=cost)

    def vi(fn, deps):
        return P.add("V", fn, deps)

    def ci(fn, deps):
        return P.add("C", fn, deps)

    def load_cvt(dst_ap, src_ap, stage_ap, prev, scale=None):
        ld = dma("S", stage_ap, src_ap, [prev] if prev else [])
        if scale is None:
            return P.add("V", lambda h, d=dst_ap, s_=stage_ap:
                         h.tensor_copy(d, s_), [ld])
        return P.add("V", lambda h, d=dst_ap, s_=stage_ap, sc=scale:
                     h.tensor_scalar(d, s_, sc, None, ALU.mult), [ld])

    ns_done = [("nX", 8)]

    c0 = P.add("V", lambda h: (h.memset(sonebf[:], 1.0),
                               h.memset(sonesbf[:], 1.0),
                               h.memset(cstf[:, 0:1], 1.0 / D),
                               h.memset(cstf[:, 1:2], LN_EPS),
                               h.memset(cstf[:, 2:3], RMS_EPS),
                               h.memset(cstf[:, 3:4], 1.0))[-1],
               [])
    t_g = dma("S", gamb[:, :], gam_in, [])
    t_b = dma("S", betb[:, :], bet_in, [])
    t_bias = dma("S", biasc[:, :], biasfm_in, [])
    t_m = dma("S", m0v[:, :], m0_in, [])
    t_m = vi(lambda h: (h.tensor_copy(m0a[:, :], m0v[:, 0:1]),
                        h.tensor_copy(m0bv[:, :], m0v[:, 1:2]))[-1], [t_m])
    t_i = dma("S", smallf[:, 0:NCH], idv_in, [])
    t_s = dma("S", smallf[0:8, 12:14], sel_in, [])
    c_i = vi(lambda h: h.tensor_copy(idvbf[:, 0:2 * NCH:2], smallf[:, 0:NCH]), [t_i])
    c_i2 = ci(lambda h: h.activation(idvm[:, 0:NCH], smallf[:, 0:NCH],
                                     AF.Copy, scale=m0bv.ap()),
              [t_i, t_m, c0])
    c_s = vi(lambda h: h.tensor_copy(selbf[:, :], smallf[0:8, 12:14]), [t_s])
    prev = dma("S", cvt32[:, 0:128], id_in.bitcast(F32), [c_s])
    prev = vi(lambda h: h.tensor_copy(sidbf[:, :], cvt32[:, 0:128]), [prev])
    prev = dma("S", cvt32[:, 0:512], pool_in.transpose([1, 0, 2]), [prev])
    prev = vi(lambda h: h.tensor_copy(poolbf[:, :], cvt32[:, 0:512]), [prev])
    la4 = labf.ap().rearrange("p (i a r) -> p i a r", a=NCH, r=32)
    for i in range(6):
        prev = load_cvt(la4[:, i], _r3(la_in[i]),
                        cvt32[:, 0:256].rearrange("p (a r) -> p a r", r=32),
                        prev)
    consts_done = prev

    def load_nsT(i, deps):
        s = (i % 2) * D
        t1 = dma("S", cvt32[0:64, 0:512], sT_in[i][:, 0:512], deps)
        v1 = vi(lambda h, i=i, s=s: h.tensor_scalar(
            nsT2[0:64, s:s + 512], cvt32[0:64, 0:512], -1.0, None, ALU.mult),
            [t1])
        t2 = dma("S", cvt32[0:64, 0:512], sT_in[i][:, 512:D], [v1])
        v2 = vi(lambda h, i=i, s=s: h.tensor_scalar(
            nsT2[0:64, s + 512:s + D], cvt32[0:64, 0:512], -1.0, None,
            ALU.mult), [t2])
        return v2

    def load_lb(i, deps):
        s = (i % 2) * D
        t1 = dma("S", cvt32[0:32, 0:512], lb_in[i][:, 0:512], deps)
        v1 = vi(lambda h, i=i, s=s: h.tensor_copy(
            lb2[0:32, s:s + 512], cvt32[0:32, 0:512]), [t1])
        t2 = dma("S", cvt32[0:32, 0:512], lb_in[i][:, 512:D], [v1])
        v2 = vi(lambda h, i=i, s=s: h.tensor_copy(
            lb2[0:32, s + 512:s + D], cvt32[0:32, 0:512]), [t2])
        return v2

    TCP = TC + 2

    def emit_ln(xtile, xbft, sqt, wo, dst_fn, ld_tok, extra=()):
        cvt = vi(lambda h: [h.tensor_copy(
            xbft[:, d8 * TCP: d8 * TCP + TC],
            xtile[:, d8 * (TC + 1) + wo: d8 * (TC + 1) + wo + TC])
            for d8 in range(NCH)][-1],
            [ld_tok, state.get("last_lnmm")] + list(extra))
        sq_toks = []
        for d8 in range(NCH):
            of = d8 * (TC + 1) + wo
            o = d8 * TCP
            tok = vi(lambda h, o=o, of=of: h.tensor_mul(
                sqt[:, o:o + TC], xtile[:, of:of + TC],
                xtile[:, of:of + TC]), [cvt])
            sq_toks.append(tok)
        d0 = [state["pR_reader"], cvt]
        s0 = mmg(pR0[0:1, 0:TC],
                 [(sonesbf[:], xbft[:, d8 * TCP: d8 * TCP + TC])
                  for d8 in range(NCH)], d0 + [c0])
        s1 = mmg(pR1[0:1, 0:TC],
                 [(sonesbf[:], sqt[:, d8 * TCP: d8 * TCP + TC])
                  for d8 in range(NCH)], sq_toks)
        state["last_lnmm"] = s1
        cst = ci(lambda h: (h.activation(rowS[0:1, 0:TC], pR0[0:1, 0:TC],
                                         AF.Copy, scale=cstf[0:1, 0:1]),
                            h.activation(rowS[0:1, TC:2 * TC], pR1[0:1, 0:TC],
                                         AF.Copy, scale=cstf[0:1, 0:1]))[-1],
                 [s0, s1, c0])
        state["pR_reader"] = cst
        v0 = vi(lambda h: (h.tensor_mul(rowS[0:1, 2 * TC:3 * TC],
                                        rowS[0:1, 0:TC], rowS[0:1, 0:TC]),
                           h.tensor_sub(rowS[0:1, 2 * TC:3 * TC],
                                        rowS[0:1, TC:2 * TC],
                                        rowS[0:1, 2 * TC:3 * TC]))[-1], [cst])
        c1 = ci(lambda h: h.activation(rowS[0:1, 2 * TC:3 * TC],
                                       rowS[0:1, 2 * TC:3 * TC], AF.Sqrt,
                                       bias=cstf[0:1, 1:2]), [v0])
        v1 = vi(lambda h: h.reciprocal(rowS[0:1, 2 * TC:3 * TC],
                                       rowS[0:1, 2 * TC:3 * TC]), [c1])
        bc = mmg(pA[0][:, 0:TC], [(sonesr[:], rowS[0:1, 0:TC])],
                 [v1] + state["pB_readers"])
        bc2 = mmg(pA[1][:, 0:TC], [(sonesr[:], rowS[0:1, 2 * TC:3 * TC])],
                  [v1])
        state["pB_readers"] = []
        outs = []
        for d8 in range(NCH):
            o = d8 * TCP
            tok = vi(lambda h, o=o, d8=d8: (
                h.tensor_sub(fscr[:, 0:TC], xbft[:, o:o + TC],
                             pA[0][:, 0:TC]),
                h.tensor_mul(dst_fn(d8), fscr[:, 0:TC], pA[1][:, 0:TC]))[-1],
                [bc, bc2])
            outs.append(tok)
        state["pB_readers"] = outs
        return outs

    def hb_ap(d8, b):
        return hB[:, (d8 * B + b) * TC:(d8 * B + b) * TC + TC]

    def y1_ap(d8):
        return y1B[:, d8 * TC:(d8 + 1) * TC]

    def emit_ln_p2(b, dst_fn):
        cvs = []
        p = state.get("xstg_own")
        for d8 in range(NCH):
            ldc = dma("S", xstg[:, :], xs_in[b, d8], [p] if p else [])
            cv = vi(lambda h, d8=d8: (
                h.tensor_copy(xbfB[:, d8 * TCP: d8 * TCP + TC],
                              xstg[:, 1:TC + 1]),
                h.tensor_mul(sqB[:, d8 * TCP: d8 * TCP + TC],
                             xstg[:, 1:TC + 1], xstg[:, 1:TC + 1]))[-1],
                [ldc])
            p = cv
            cvs.append(cv)
        state["xstg_own"] = p
        s0 = mmg(pX[0][0:1, 0:TC],
                 [(sonesbf[:], xbfB[:, d8 * TCP: d8 * TCP + TC])
                  for d8 in range(NCH)],
                 cvs + [c0, state["pX_evac"][0]])
        s1 = mmg(pX[1][0:1, 0:TC],
                 [(sonesbf[:], sqB[:, d8 * TCP: d8 * TCP + TC])
                  for d8 in range(NCH)], cvs + [state["pX_evac"][1]])
        cst = ci(lambda h: (h.activation(rowS[0:1, 0:TC], pX[0][0:1, 0:TC],
                                         AF.Copy, scale=cstf[0:1, 0:1]),
                            h.activation(rowS[0:1, TC:2 * TC],
                                         pX[1][0:1, 0:TC],
                                         AF.Copy, scale=cstf[0:1, 0:1]))[-1],
                 [s0, s1, c0])
        v0 = vi(lambda h: (h.tensor_mul(rowS[0:1, 2 * TC:3 * TC],
                                        rowS[0:1, 0:TC], rowS[0:1, 0:TC]),
                           h.tensor_sub(rowS[0:1, 2 * TC:3 * TC],
                                        rowS[0:1, TC:2 * TC],
                                        rowS[0:1, 2 * TC:3 * TC]))[-1], [cst])
        c1 = ci(lambda h: h.activation(rowS[0:1, 2 * TC:3 * TC],
                                       rowS[0:1, 2 * TC:3 * TC], AF.Sqrt,
                                       bias=cstf[0:1, 1:2]), [v0])
        v1 = vi(lambda h: h.reciprocal(rowS[0:1, 2 * TC:3 * TC],
                                       rowS[0:1, 2 * TC:3 * TC]), [c1])
        bc = mmg(pX[0][:, 0:TC], [(sonesr[:], rowS[0:1, 0:TC])],
                 [v1, cst] + state["pB_readers"])
        bc2 = mmg(pX[1][:, 0:TC], [(sonesr[:], rowS[0:1, 2 * TC:3 * TC])],
                  [v1, cst])
        state["pB_readers"] = []
        outs = []
        for d8 in range(NCH):
            o = d8 * TCP
            tok = vi(lambda h, o=o, d8=d8: (
                h.tensor_sub(fscr[:, 0:TC], xbfB[:, o:o + TC],
                             pX[0][:, 0:TC]),
                h.tensor_mul(dst_fn(d8), fscr[:, 0:TC], pX[1][:, 0:TC]))[-1],
                [bc, bc2])
            outs.append(tok)
        state["pB_readers"] = outs
        state["pX_evac"][0] = outs[-1]
        state["pX_evac"][1] = outs[-1]
        return outs

    h_init = []
    for b in range(B):
        outs = emit_ln_p2(b, lambda d8, b=b: hb_ap(d8, b))
        h_init.extend(outs)

    def dequant(i, buf, prev_tok, extra_deps):
        p = prev_tok
        for d8 in range(NCH):
            for hf in range(2):
                sl = slice(hf * 512, hf * 512 + 512)
                l1 = dma("S", qraw[:, :], _r3(qw_in[i])[:, d8, sl],
                         [p] + (extra_deps if d8 == 0 and hf == 0 else []))
                l2 = dma("S", srepc[:, :], _r3(srep_in[i])[:, d8, sl], [l1])
                o = d8 * D + hf * 512
                p = vi(lambda h, o=o, buf=buf: (
                    h.tensor_copy(qfsc[:, :], qraw[:, :]),
                    h.tensor_scalar(qfsc[:, :], qfsc[:, :], R75, None,
                                    ALU.mult),
                    h.tensor_mul(buf[:, o:o + 512], qfsc[:, :],
                                 srepc[:, :]))[-1], [l1, l2])
        return p

    def qlora(i, buf, src_fn, b, dq_tok, ns_tok, evac_kind, src_toks):
        sL = (i % 2) * D
        a_deps = src_toks + [state["pX_evac"][0]]
        xg0 = mmg(pX[0][0:64, 0:256],
                  [(poolbf[:, d8 * 64:(d8 + 1) * 64], src_fn(d8)[:, 0:256])
                   for d8 in range(NCH)], a_deps)
        xg1 = mmg(pX[0][0:64, 256:512],
                  [(poolbf[:, d8 * 64:(d8 + 1) * 64], src_fn(d8)[:, 256:512])
                   for d8 in range(NCH)], [])
        md0 = mmg(pX[1][0:32, 0:256],
                  [(la4[:, i, d8], src_fn(d8)[:, 0:256])
                   for d8 in range(NCH)], [state["pX_evac"][1]])
        md1 = mmg(pX[1][0:32, 256:512],
                  [(la4[:, i, d8], src_fn(d8)[:, 256:512])
                   for d8 in range(NCH)], [])
        ev = vi(lambda h: (h.tensor_copy(auxB[0:64, :], pX[0][0:64, :]),
                           h.tensor_copy(auxB2[0:32, :], pX[1][0:32, :]))[-1],
                [xg0, xg1, md0, md1])
        state["pX_evac"][0] = ev
        state["pX_evac"][1] = ev
        out_toks = []
        last_mm = None
        prev_vadd = None
        for ot in range(NCH):
            deps = [ev, dq_tok, ns_tok, state["pX_evac"][ot % 2]]
            pairs = [(buf[:, d8 * D + ot * 128: d8 * D + ot * 128 + 128],
                      src_fn(d8)) for d8 in range(NCH)]
            pairs.append((nsT2[0:64, sL + ot * 128: sL + ot * 128 + 128],
                          auxB[0:64, :]))
            pairs.append((lb2[0:32, sL + ot * 128: sL + ot * 128 + 128],
                          auxB2[0:32, :]))
            mm = mmg(pX[ot % 2][:, 0:TC], pairs, deps)
            last_mm = mm
            bcol = biasc[:, i * 8 + ot: i * 8 + ot + 1]
            if evac_kind == "y1":
                tokc = ci(lambda h, ot=ot, bcol=bcol: h.activation(
                    y1_ap(ot), pX[ot % 2][:, 0:TC], AF.Identity,
                    scale=cstf[:, 3:4], bias=bcol), [mm, t_bias, c0])
                tok = ci(lambda h, ot=ot: h.activation(
                    sqB[:, ot * TCP: ot * TCP + TC], y1_ap(ot),
                    AF.Square), [tokc, state.get("last_lnmm")])
            else:
                tokc = ci(lambda h, ot=ot, bcol=bcol: h.activation(
                    fscr[:, 0:TC], pX[ot % 2][:, 0:TC], AF.Identity,
                    scale=cstf[:, 3:4], bias=bcol),
                    [mm, t_bias, c0] +
                    ([prev_vadd] if prev_vadd else []))
                tok = vi(lambda h, ot=ot, b=b: h.tensor_add(
                    hb_ap(ot, b), hb_ap(ot, b), fscr[:, 0:TC]), [tokc])
                prev_vadd = tok
            state["pX_evac"][ot % 2] = tok
            out_toks.append(tok)
        return out_toks, last_mm

    def ffn_layernorm(blk, y_toks):
        sq_toks = y_toks
        s0 = mmg(pX[0][0:1, 0:TC], [(sonesbf[:], y1_ap(d8))
                                    for d8 in range(NCH)],
                 [state["pX_evac"][0]] + y_toks)
        s1 = mmg(pX[1][0:1, 0:TC],
                 [(sonesbf[:], sqB[:, d8 * TCP: d8 * TCP + TC])
                  for d8 in range(NCH)], sq_toks + [state["pX_evac"][1]])
        state["last_lnmm"] = s1
        cst = ci(lambda h: (h.activation(rowS[0:1, 0:TC], pX[0][0:1, 0:TC],
                                         AF.Copy, scale=cstf[0:1, 0:1]),
                            h.activation(rowS[0:1, TC:2 * TC],
                                         pX[1][0:1, 0:TC],
                                         AF.Copy, scale=cstf[0:1, 0:1]))[-1],
                 [s0, s1, c0])
        v0 = vi(lambda h: (h.tensor_mul(rowS[0:1, 2 * TC:3 * TC],
                                        rowS[0:1, 0:TC], rowS[0:1, 0:TC]),
                           h.tensor_sub(rowS[0:1, 2 * TC:3 * TC],
                                        rowS[0:1, TC:2 * TC],
                                        rowS[0:1, 2 * TC:3 * TC]))[-1], [cst])
        c1 = ci(lambda h: h.activation(rowS[0:1, 2 * TC:3 * TC],
                                       rowS[0:1, 2 * TC:3 * TC], AF.Sqrt,
                                       bias=cstf[0:1, 1:2]), [v0])
        v1 = vi(lambda h: h.reciprocal(rowS[0:1, 2 * TC:3 * TC],
                                       rowS[0:1, 2 * TC:3 * TC]), [c1])
        bc = mmg(pX[0][:, 0:TC], [(sonesr[:], rowS[0:1, 0:TC])],
                 [v1, cst] + state["pB_readers"])
        bc2 = mmg(pX[1][:, 0:TC], [(sonesr[:], rowS[0:1, 2 * TC:3 * TC])],
                  [v1, cst])
        state["pB_readers"] = []
        outs = []
        for d8 in range(NCH):
            t1 = vi(lambda h, d8=d8: (
                h.tensor_sub(fscr[:, 0:TC], y1_ap(d8), pX[0][:, 0:TC]),
                h.tensor_mul(y1_ap(d8), fscr[:, 0:TC], pX[1][:, 0:TC]))[-1],
                [bc, bc2])
            t2 = ci(lambda h, d8=d8, blk=blk: h.activation(
                y1_ap(d8), y1_ap(d8), AF.Gelu,
                bias=betb[:, blk * NCH + d8: blk * NCH + d8 + 1],
                scale=gamb[:, blk * NCH + d8: blk * NCH + d8 + 1]),
                [t1, t_g, t_b])
            outs.append(t2)
        state["pB_readers"] = outs
        state["pX_evac"][0] = outs[-1]
        state["pX_evac"][1] = outs[-1]
        return outs

    dq_prev = dequant(0, qt[0], consts_done, h_init)
    dqA = dq_prev
    dqB = None
    ns_prev = load_nsT(0, [consts_done])
    ns_prev = load_lb(0, [ns_prev])
    nsA = ns_prev
    nsB = None
    qt_read = [None, None]
    for blk in range(3):
        i0, i1 = 2 * blk, 2 * blk + 1
        dqB = dequant(i1, qt[1], dq_prev, [qt_read[1]] if qt_read[1] else [])
        dq_prev = dqB
        ns_prev = load_nsT(i1, [ns_prev] +
                           ([qt_read[1]] if qt_read[1] else []))
        ns_prev = load_lb(i1, [ns_prev])
        nsB = ns_prev
        for b in range(B):
            y_toks, lmm = qlora(i0, qt[0], lambda d8, b=b: hb_ap(d8, b), b,
                                dqA, nsA, "y1",
                                h_init if blk == 0 and b == 0 else [])
            qt_read[0] = lmm
            y2 = ffn_layernorm(blk, y_toks)
            _, lmm = qlora(i1, qt[1], lambda d8: y1_ap(d8), b, dqB, nsB,
                           "hupd", y2)
            qt_read[1] = lmm
        if blk < 2:
            dqA = dequant(2 * blk + 2, qt[0], dq_prev,
                          [qt_read[0]] if qt_read[0] else [])
            dq_prev = dqA
            ns_prev = load_nsT(2 * blk + 2, [ns_prev] +
                               ([qt_read[0]] if qt_read[0] else []))
            ns_prev = load_lb(2 * blk + 2, [ns_prev])
            nsA = ns_prev
    g_spills = []
    sil_prev = P.barrier()
    for d8 in range(NCH):
        s_tok = ci(lambda h, d8=d8: h.activation(
            hB[:, d8 * B * TC:(d8 + 1) * B * TC],
            hB[:, d8 * B * TC:(d8 + 1) * B * TC], AF.Silu),
            sil_prev if d8 == 0 else [])
        g_tok = dma("G", gdram[d8], hB[:, d8 * B * TC:(d8 + 1) * B * TC],
                    [s_tok])
        g_spills.append(g_tok)

    def E_all(l, d8, j0, n, step=1):
        off = EOFF[l] + d8 * B * NLP[l]
        apx = Ebuf[:, off: off + B * NLP[l]].rearrange("p (b j) -> p b j",
                                                      j=NLP[l])
        if step == 1:
            return apx[:, :, j0: j0 + n]
        return apx[:, :, j0: j0 + (n - 1) * step + 1: step]

    def E_blk(l, d8, b, j0, n, step=1):
        off = EOFF[l] + (d8 * B + b) * NLP[l] + j0
        if step == 1:
            return Ebuf[:, off: off + n]
        return Ebuf[:, off: off + (n - 1) * step + 1: step]

    P2_END = len(P.items)
    p3bar = P.barrier()
    p = None
    for d8 in range(NCH):
        ld = dma("S", cvt3[:, :], _r3(wq_in)[:, d8],
                 (p3bar + g_spills + ns_done) if d8 == 0 else [p])
        p = vi(lambda h, d8=d8: h.tensor_copy(wbf[:, d8 * D:(d8 + 1) * D],
                                              cvt3[:, :]), [ld])
    wq_done = p

    E0_ready = []
    for b in range(B):
        ld = dma("S", xtl2.ap().rearrange("p (a c) -> p a c", c=TC + 1),
                 xs_in[b].transpose([1, 0, 2]),
                 p3bar + ns_done + (state["pB_readers"] if b else []))
        xn_toks = emit_ln(xtl2, xbf3, sqb3, 0,
                          lambda d8: actB[:, d8 * TC:(d8 + 1) * TC], ld)
        for et in range(NCH):
            mm = mmg(pX[et % 2][:, 0:TC],
                     [(wbf[:, k8 * D + et * 128: k8 * D + et * 128 + 128],
                       actB[:, k8 * TC:(k8 + 1) * TC]) for k8 in range(NCH)],
                     xn_toks + [wq_done, state["pX_evac"][et % 2]])
            tok = vi(lambda h, et=et, b=b: h.tensor_copy(
                E_blk(0, et, b, 0, TC), pX[et % 2][:, 0:TC]), [mm])
            ca = ci(lambda h, et=et, b=b: h.activation(
                fscr2[:, 0:1], E_blk(0, et, b, 0, 1), AF.Copy,
                scale=m0a.ap()), [tok, t_m, c_i2])
            cb = ci(lambda h, et=et, b=b: h.activation(
                E_blk(0, et, b, 0, 1), fscr2[:, 0:1], AF.Identity,
                scale=cstf[:, 3:4], bias=idvm[:, et:et + 1]), [ca])
            state["pX_evac"][et % 2] = tok
            E0_ready.append(cb)
            E0_ready.append(tok)
        xn2 = emit_ln(xtl2, xbf3, sqb3, 1,
                      lambda d8: actB[:, d8 * TC:(d8 + 1) * TC], tok,
                      extra=[state["pX_evac"][0], state["pX_evac"][1]])
        for et in range(NCH):
            mm = mmg(pX[et % 2][:, 0:1],
                     [(wbf[:, k8 * D + et * 128: k8 * D + et * 128 + 128],
                       actB[:, k8 * TC + TC - 1: k8 * TC + TC])
                      for k8 in range(NCH)],
                     xn2 + [state["pX_evac"][et % 2]])
            tok2 = vi(lambda h, et=et, b=b: h.tensor_copy(
                qlastB[:, (et * B + b) * 2: (et * B + b) * 2 + 1],
                pX[et % 2][:, 0:1]), [mm])
            state["pX_evac"][et % 2] = tok2
    state["E_ready"][0] = list(E0_ready) + ns_done
    if dbg:
        dbg_E0 = nc.dram_tensor("dbg_E0", [128, E_COLS], BF16,
                                kind="ExternalOutput").ap()
        tok_dump = dma("S", dbg_E0[:, :], Ebuf[:, :], P.barrier())
        state["E_ready"][0] = state["E_ready"][0] + [tok_dump]

    def comb(m, lhs, rhs, outw, blend=False, in_ready=(), sqoff=0):
        if m > 512:
            per_b = m // B
            nb = max(1, 512 // per_b)
            outs = []
            for b0 in range(0, B, nb):
                ne = min(nb, B - b0)

                def wrap(f, b0=b0, ne=ne):
                    return lambda k8: f(k8)[:, b0:b0 + ne, :]
                outs += comb(per_b * ne, wrap(lhs), wrap(rhs), wrap(outw),
                             blend, in_ready, sqoff)
            return outs
        outs = []
        ztoks = []
        for d8 in range(NCH):
            deps = [state["pX_evac"][d8 % 2]] + list(in_ready)
            pairs = [(P1T[:, k8 * D + d8 * 128: k8 * D + d8 * 128 + 128],
                      lhs(k8)) for k8 in range(NCH)]
            pairs += [(P2T[:, k8 * D + d8 * 128: k8 * D + d8 * 128 + 128],
                       rhs(k8)) for k8 in range(NCH)]
            mm = mmg(pX[d8 % 2][:, 0:m], pairs, deps)
            ev = vi(lambda h, d8=d8, m=m: h.tensor_copy(
                zscr[:, d8 * TC: d8 * TC + m], pX[d8 % 2][:, 0:m]), [mm])
            evc = ci(lambda h, d8=d8, m=m: h.activation(
                sqb3[:, d8 * TCP: d8 * TCP + m], pX[d8 % 2][:, 0:m],
                AF.Square), [mm, ev, state.get("last_s0")])
            state["pX_evac"][d8 % 2] = evc
            ztoks.append(evc)
        s0 = mmg(pR0[0:1, 0:m],
                 [(sonesbf[:], sqb3[:, d8 * TCP: d8 * TCP + m])
                  for d8 in range(NCH)],
                 [state["pR_reader"]] + ztoks)
        state["last_s0"] = s0
        cst = ci(lambda h, m=m: h.activation(rowS[0:1, 0:m], pR0[0:1, 0:m],
                                             AF.Sqrt, scale=cstf[0:1, 0:1],
                                             bias=cstf[0:1, 2:3]), [s0, c0])
        state["pR_reader"] = cst
        v1 = vi(lambda h, m=m: h.reciprocal(rowS[0:1, 0:m], rowS[0:1, 0:m]),
                [cst])
        bc = mmg(pA[0][:, 0:m], [(sonesr[:], rowS[0:1, 0:m])],
                 [v1] + state["pB_readers"])
        state["pB_readers"] = []
        for d8 in range(NCH):
            if not blend:
                tok = vi(lambda h, d8=d8, m=m: h.tensor_mul(
                    outw(d8), zscr[:, d8 * TC: d8 * TC + m],
                    pA[0][:, 0:m]), [bc])
            else:
                v_a = vi(lambda h, d8=d8, m=m: h.tensor_mul(
                    fscr[:, 0:m], zscr[:, d8 * TC: d8 * TC + m],
                    pA[0][:, 0:m]),
                    [bc] + ([state["blend_c"]] if state.get("blend_c") else []))
                c_a = ci(lambda h, m=m: h.activation(
                    fscr2[:, 0:m], fscr[:, 0:m], AF.Copy, scale=m0a.ap()),
                    [v_a, t_m] + ([state["blend_v"]]
                                  if state.get("blend_v") else []))
                c_b = ci(lambda h, d8=d8, m=m: h.activation(
                    fscr3[:, 0:m], outw(d8), AF.Copy, scale=m0bv.ap()), [c_a])
                tok = vi(lambda h, d8=d8, m=m: h.tensor_add(
                    outw(d8), fscr2[:, 0:m], fscr3[:, 0:m]), [c_b])
                state["blend_c"] = c_a
                state["blend_v"] = tok
            outs.append(tok)
        state["pB_readers"] = outs
        return outs

    for l in range(1, 10):
        n = NL[l]
        state["E_ready"][l] = comb(
            B * n,
            lambda k8, l=l, n=n: E_all(l - 1, k8, 0, n, 2),
            lambda k8, l=l, n=n: E_all(l - 1, k8, 1, n, 2),
            lambda d8, l=l, n=n: E_all(l, d8, 0, n),
            in_ready=state["E_ready"][l - 1])
        if dbg and l == 6:
            dbg_row = nc.dram_tensor("dbg_row", [1, 3 * TC], F32,
                                     kind="ExternalOutput").ap()
            dbg_sq = nc.dram_tensor("dbg_sq", [128, NCH * (TC + 2)], BF16,
                                    kind="ExternalOutput").ap()
            dbg_z = nc.dram_tensor("dbg_z", [128, NCH * TC], BF16,
                                   kind="ExternalOutput").ap()
            td1 = dma("S", dbg_row[:, :], rowS[:, :], P.barrier())
            td2 = dma("S", dbg_sq[:, :], sqb3[:, :], [td1])
            td3 = dma("S", dbg_z[:, :], zscr[:, :], [td2])
            state["E_ready"][l] = state["E_ready"][l] + [td3]

    if dbg:
        dbg_Er = nc.dram_tensor("dbg_Er", [128, E_COLS], BF16,
                                kind="ExternalOutput").ap()
        tok_dump2 = dma("S", dbg_Er[:, :], Ebuf[:, :], P.barrier())
        for _l in range(10):
            state["E_ready"][_l] = state["E_ready"].get(_l, []) + [tok_dump2]
    sx = dmas("S", sc_gin[:, :, :],
              Ebuf[:, EOFF[9]: EOFF[9] + NCH * B * 2].rearrange(
                  "p (a b t) -> p a b t", b=B, t=2)[:, :, :, 0],
              state["E_ready"][9])
    kx = P.add("K", lambda h: h.collective_compute(
        "AllGather", ALU.bypass, replica_groups=G8,
        ins=[sc_gin[:, :, :]], outs=[sc_gout[:, :, :, :]]), [sx])
    e9_toks = []
    for ci_ in range(8):
        out_ap = e9all[:, ci_ * NCH * B: (ci_ + 1) * NCH * B].rearrange(
            "p (a b) -> p a b", b=B)
        tok = dma("S", out_ap, sc_gout[ci_], [kx])
        e9_toks.append(tok)
    if dbg:
        dbg_e9a = nc.dram_tensor("dbg_e9a", [128, NCH * B * 8], BF16,
                                 kind="ExternalOutput").ap()
        tk9 = dma("S", dbg_e9a[:, :], e9all[:, :], P.barrier())
        e9_toks = e9_toks + [tk9]

    def e9g(k8, i0, n, step=1):
        apx = e9all.ap().rearrange("p (c a b) -> p c a b", a=NCH, b=B
                                   )[:, :, k8, :].transpose([0, 2, 1])
        return apx[:, :, i0: i0 + (n - 1) * step + 1: step] if step > 1 \
            else apx[:, :, i0: i0 + n]

    def emg(k8, i0, n, step=1):
        apx = emid[:, k8 * B * 8: (k8 + 1) * B * 8].rearrange(
            "p (b i) -> p b i", i=8)
        return apx[:, :, i0: i0 + (n - 1) * step + 1: step] if step > 1 \
            else apx[:, :, i0: i0 + n]

    def rmidall(i0, n, step=1):
        apx = rmid.ap().rearrange("p (ab i) -> p ab i", i=12)
        if step == 1:
            return apx[:, :, i0: i0 + n]
        return apx[:, :, i0: i0 + (n - 1) * step + 1: step]

    def emidall(i0, n, step=1):
        apx = emid.ap().rearrange("p (ab i) -> p ab i", i=8)
        if step == 1:
            return apx[:, :, i0: i0 + n]
        return apx[:, :, i0: i0 + (n - 1) * step + 1: step]

    def e9all_c(i0, n, step=1):
        apx = e9all.ap().rearrange("p (i ab) -> p i ab", ab=NCH * B
                                   ).transpose([0, 2, 1])
        if step == 1:
            return apx[:, :, i0: i0 + n]
        return apx[:, :, i0: i0 + (n - 1) * step + 1: step]

    def rmg(k8, i0, n, step=1):
        apx = rmid[:, k8 * B * 12: (k8 + 1) * B * 12].rearrange(
            "p (b i) -> p b i", i=12)
        return apx[:, :, i0: i0 + (n - 1) * step + 1: step] if step > 1 \
            else apx[:, :, i0: i0 + n]

    m10 = comb(B * 4, lambda k8: e9g(k8, 0, 4, 2), lambda k8: e9g(k8, 1, 4, 2),
               lambda d8: emg(d8, 0, 4), in_ready=e9_toks)
    m11 = comb(B * 2, lambda k8: emg(k8, 0, 2, 2), lambda k8: emg(k8, 1, 2, 2),
               lambda d8: emg(d8, 4, 2), in_ready=m10)
    m12 = comb(B, lambda k8: emg(k8, 4, 1), lambda k8: emg(k8, 5, 1),
               lambda d8: emg(d8, 6, 1), in_ready=m11)
    m10d = comb(B, lambda k8: emg(k8, 4, 1), lambda k8: emg(k8, 2, 1),
                lambda d8: rmg(d8, 2, 1), in_ready=m12)
    cpy1a = gcp(rmidall(0, 1), emidall(0, 1), m12 + m10d)
    cpy1b = gcp(rmidall(1, 1), emidall(4, 1), [])
    cpy1c = gcp(rmidall(3, 1), emidall(6, 1), [])
    cpy1 = cpy1c
    m9d = comb(B * 3, lambda k8: rmg(k8, 0, 3), lambda k8: e9g(k8, 2, 3, 2),
               lambda d8: rmg(d8, 6, 3, 2), in_ready=m10d + [cpy1])
    cpy2a = gcp(rmidall(4, 1), e9all_c(0, 1), m9d)
    cpy2b = gcp(rmidall(5, 4, 2), rmidall(0, 4), [])
    cpy2 = cpy2b
    tr_toks = []
    sel_back = []
    for bb in range(B):
        tr_toks = []
        for d8 in range(NCH):
            pTb = pT[(bb * NCH + d8) % 2].ap().bitcast(BF16)
            tk = P.add("T", lambda h, bb=bb, d8=d8, pTb=pTb: h.transpose(
                pTb[0:8, 0:128],
                rmid[:, d8 * B * 12 + bb * 12 + 4: d8 * B * 12 + bb * 12 + 12],
                sidbf[:, :]),
                [cpy2, state["pT_trans"][(bb * NCH + d8) % 2],
                 state["pR_reader"], c0] + state.get("r9tm_guard", []))
            ck = ci(lambda h, bb=bb, d8=d8, pTb=pTb: h.copy(
                r9tm[0:8, d8 * 128: (d8 + 1) * 128],
                pTb[0:8, 0:128]), [tk])
            state["pT_trans"][(bb * NCH + d8) % 2] = ck
            tr_toks.append(ck)
        selmms = []
        for hh in range(2):
            mm = mmg(pX[hh][0:2, 0:TC],
                     [(selbf[:, :], r9tm[0:8, hh * TC:(hh + 1) * TC])],
                     tr_toks + [state["pX_evac"][hh]])
            tok = vi(lambda h, hh=hh: h.tensor_copy(
                r9tm[0:2, hh * TC:(hh + 1) * TC],
                pX[hh][0:2, 0:TC]), [mm])
            state["pX_evac"][hh] = tok
            selmms.append(tok)
        for d8 in range(NCH):
            pTb = pT[(bb * NCH + d8) % 2].ap().bitcast(BF16)
            tk = P.add("T", lambda h, d8=d8, pTb=pTb: h.transpose(
                pTb[0:128, 0:2],
                r9tm[0:2, d8 * 128: (d8 + 1) * 128],
                sidbf[0:2, 0:2]),
                selmms + [state["pT_trans"][(bb * NCH + d8) % 2]])
            ck = ci(lambda h, bb=bb, d8=d8, pTb=pTb: h.copy(
                r9sel[:, (d8 * B + bb) * 2: (d8 * B + bb) * 2 + 2],
                pTb[0:128, 0:2]), [tk])
            state["pT_trans"][(bb * NCH + d8) % 2] = ck
            sel_back.append(ck)
        state["r9tm_guard"] = sel_back[-NCH:]
    state["pR_reader"] = sel_back[-1]

    def E_4d(l, j0, n, step=1):
        apx = Ebuf[:, EOFF[l]: EOFF[l] + NCH * B * NLP[l]].rearrange(
            "p (ab j) -> p ab j", j=NLP[l])
        if step == 1:
            return apx[:, :, j0: j0 + n]
        return apx[:, :, j0: j0 + (n - 1) * step + 1: step]

    def r9sel_4d(which):
        apx = r9sel.ap().rearrange("p (ab w) -> p ab w", w=2)
        return apx[:, :, which:which + 1]

    def r9sel_col(k8, which):
        apx = r9sel[:, k8 * B * 2: (k8 + 1) * B * 2].rearrange(
            "p (b w) -> p b w", w=2)
        return apx[:, :, which:which + 1]

    MB = 9 * B
    all_er = sum((state["E_ready"][l] for l in range(10)), [])
    stage_deps = sel_back + [t_m] + all_er

    def stg_l(k8):
        return actB[:, k8 * TC: k8 * TC + MB]

    def stg_r(k8):
        return actB[:, k8 * TC + 64: k8 * TC + 64 + MB]

    aL3 = actB.ap().rearrange("p (a c) -> p a c", c=TC)
    selw0 = r9sel.ap().rearrange("p (a b w) -> p a b w", b=B, w=2)[:, :, :, 0]
    stg_toks = []
    for l in range(8, -1, -1):
        li = 8 - l
        ev = Ebuf[:, EOFF[l]: EOFF[l] + NCH * B * NLP[l]].rearrange(
            "p (a b j) -> p a b j", b=B, j=NLP[l])[:, :, :, 0]
        t1 = gcp(aL3[:, :, li * B: (li + 1) * B], selw0, stage_deps)
        t2 = gcp(aL3[:, :, 64 + li * B: 64 + (li + 1) * B], ev, [t1])
        stg_toks.append(t2)
    mb_toks = comb(MB, stg_l, stg_r, lambda d8: stg_r(d8), blend=True,
                   in_ready=stg_toks + sel_back + [t_m])
    sc_tok = {}
    for l in range(8, -1, -1):
        li = 8 - l
        ev = Ebuf[:, EOFF[l]: EOFF[l] + NCH * B * NLP[l]].rearrange(
            "p (a b j) -> p a b j", b=B, j=NLP[l])[:, :, :, 0]
        sc_tok[l] = gcp(ev, aL3[:, :, 64 + li * B: 64 + (li + 1) * B],
                        mb_toks)
    sc_all = [sc_tok[l] for l in range(9)]

    prev_ready = sel_back + sc_all
    for l in range(8, -1, -1):
        n = NL[l]
        half = NL[l + 1]
        if l == 8:
            ps = gcp(E_4d(8, 1, 1, 2), r9sel_4d(1),
                     prev_ready + state["E_ready"][8])
        else:
            ps = gcp(E_4d(l, 1, half, 2), E_4d(l + 1, 0, half),
                     prev_ready + state["E_ready"][l] +
                     [sc_tok[l + 1], sc_tok[l]])
        ready = [ps, sc_tok[l]]
        if n // 2 >= 2:
            dn = comb(B * (n // 2 - 1),
                      lambda k8, l=l, n=n: E_all(l + 1, k8, 0, n // 2 - 1),
                      lambda k8, l=l, n=n: E_all(l, k8, 2, n // 2 - 1, 2),
                      lambda d8, l=l, n=n: E_all(l, d8, 2, n // 2 - 1, 2),
                      in_ready=prev_ready + state["E_ready"][l] +
                      [sc_tok[l + 1]])
            ready += dn
        prev_ready = ready
        state["E_ready"][l] = ready
    qs_ready = prev_ready + sc_all
    xcol_toks = comb(B, lambda k8: r9sel_col(k8, 1),
                     lambda k8: qlastB[:, k8 * B * 2: (k8 + 1) * B * 2: 2],
                     lambda d8: xcolB[:, d8 * B * 2: (d8 + 1) * B * 2: 2],
                     in_ready=sel_back + state["E_ready"][0])

    p5bar = qs_ready + xcol_toks
    prev_outs = []
    prev_ymm = None
    for b in range(B):
        p = None
        for d8 in range(NCH):
            deps0 = []
            if d8 == 0:
                deps0 = (p5bar if b == 0 else []) +                     ([prev_ymm] if prev_ymm else [])
            ld = dma("S", cvt3[:, :], _r3(wv_in)[:, d8],
                     deps0 + ([p] if p else []))
            p = vi(lambda h, d8=d8: h.tensor_copy(
                wbf[:, d8 * D:(d8 + 1) * D], cvt3[:, :]), [ld])
        wv_done = p
        ldx = dma("S", xtl2.ap().rearrange("p (a c) -> p a c", c=TC + 1),
                  xs_in[b].transpose([1, 0, 2]),
                  state["pB_readers"] + prev_outs +
                  (sc_all if b == 0 else []))
        xn_toks = emit_ln(xtl2, xbf3, sqb3, 1,
                          lambda d8: actB[:, d8 * TC:(d8 + 1) * TC], ldx)
        v_toks = []
        for et in range(NCH):
            mm = mmg(pX[et % 2][:, 0:TC],
                     [(wbf[:, k8 * D + et * 128: k8 * D + et * 128 + 128],
                       actB[:, k8 * TC:(k8 + 1) * TC]) for k8 in range(NCH)],
                     xn_toks + [wv_done, state["pX_evac"][et % 2]])
            tok = vi(lambda h, et=et: h.tensor_copy(
                vtile[:, et * TC:(et + 1) * TC], pX[et % 2][:, 0:TC]), [mm])
            state["pX_evac"][et % 2] = tok
            v_toks.append(tok)
        qsh3 = qsh.ap().rearrange("p (a t) -> p a t", t=TC)
        E03 = Ebuf[:, EOFF[0]: EOFF[0] + NCH * B * TC].rearrange(
            "p (a b j) -> p a b j", b=B, j=TC)
        qd1 = gcp(qsh3[:, :, 0:TC - 1], E03[:, :, b, 1:TC],
                  (p5bar if b == 0 else []) + state.get("qsh_guard", []))
        qd2 = gcp(qsh3[:, :, TC - 1:TC],
                  xcolB.ap().rearrange("p (a q) -> p a q", q=B * 2
                                       )[:, :, b * 2: b * 2 + 1], [qd1])
        quv_toks = []
        for d8 in range(NCH):
            tok = vi(lambda h, d8=d8: h.tensor_mul(
                actB[:, d8 * TC: (d8 + 1) * TC],
                qsh[:, d8 * TC: (d8 + 1) * TC],
                vtile[:, d8 * TC: (d8 + 1) * TC]),
                [v_toks[d8], qd2])
            quv_toks.append(tok)
        state["qsh_guard"] = quv_toks
        p = None
        for d8 in range(NCH):
            ld = dma("S", cvt3[:, :], _r3(wc_in)[:, d8],
                     [quv_toks[-1]] if d8 == 0 else [p])
            p = vi(lambda h, d8=d8: h.tensor_copy(
                wbf[:, d8 * D:(d8 + 1) * D], cvt3[:, :]), [ld])
        wc_done = p
        gld = dma("G", gt2.ap().rearrange("p (a c) -> p a c", c=TC),
                  gdram[:, :, b * TC:(b + 1) * TC].transpose([1, 0, 2]),
                  (g_spills if b == 0 else []) + prev_outs)
        od_prev = None
        cur_outs = []
        for et in range(NCH):
            mm = mmg(pX[et % 2][:, 0:TC],
                     [(wbf[:, k8 * D + et * 128: k8 * D + et * 128 + 128],
                       actB[:, k8 * TC:(k8 + 1) * TC]) for k8 in range(NCH)],
                     quv_toks + [wc_done, state["pX_evac"][et % 2]])
            prev_ymm = mm
            ot_tok = vi(lambda h, et=et, b=b: (
                h.tensor_mul(otile[:, :], pX[et % 2][:, 0:TC],
                             gt2[:, et * TC:(et + 1) * TC]),
                h.tensor_add(otile[:, :], otile[:, :],
                             xtl2[:, et * (TC + 1) + 1: et * (TC + 1) + 1 + TC])
            )[-1], [mm, gld] + ([od_prev] if od_prev else []))
            state["pX_evac"][et % 2] = ot_tok
            od_prev = dma("S", out_d[b, et], otile[:, :], [ot_tok])
            cur_outs.append(ot_tok)
        prev_outs = cur_outs + [od_prev]

    if dbg:
        dbg_pt = nc.dram_tensor("dbg_pt", [2, D, D], BF16, kind="ExternalOutput").ap()
        dbg_g = nc.dram_tensor("dbg_g", [NCH, 128, B * TC], BF16, kind="ExternalOutput").ap()
        dbg_E = nc.dram_tensor("dbg_E", [128, E_COLS], BF16, kind="ExternalOutput").ap()
        dbg_sel = nc.dram_tensor("dbg_sel", [128, NCH * B * 2], BF16, kind="ExternalOutput").ap()
        dbg_xc = nc.dram_tensor("dbg_xc", [128, NCH * B * 2], BF16, kind="ExternalOutput").ap()
        dbg_ql = nc.dram_tensor("dbg_ql", [128, NCH * B * 2], BF16, kind="ExternalOutput").ap()
        dbg_e9 = nc.dram_tensor("dbg_e9", [128, NCH * B * 8], BF16, kind="ExternalOutput").ap()
        dbg_sg = nc.dram_tensor("dbg_sg", [8, 128, NCH, B], BF16, kind="ExternalOutput").ap()
        dbg_sgi = nc.dram_tensor("dbg_sgi", [128, NCH, B], BF16, kind="ExternalOutput").ap()
        bar = P.barrier()
        dma("S", dbg_pt[:, :, :], pboth[:, :, :], bar)
        dma("S", dbg_g[:, :, :], gdram[:, :, :], bar)
        dma("S", dbg_E[:, :], Ebuf[:, :], bar)
        dma("S", dbg_sel[:, :], r9sel[:, :], bar)
        dma("S", dbg_xc[:, :], xcolB[:, :], bar)
        dma("S", dbg_ql[:, :], qlastB[:, :], bar)
        dma("S", dbg_e9[:, :], e9all[:, :], bar)
        dma("S", dbg_sg[:, :, :, :], sc_gout[:, :, :, :], bar)
        dma("S", dbg_sgi[:, :, :], sc_gin[:, :, :], bar)
    FIN = dict(S=P.count["S"], G=P.count["G"], K=P.count["K"],
               T=P.count["T"], V=P.count["V"], C=P.count["C"])

    csum = []
    acc = 0
    for c_ in P.tcost[:P2_END]:
        acc += c_
        csum.append(acc)
    cuts = [0]
    j = 0
    for k in range(1, NIT):
        tgt = acc * k / NIT
        while j < P2_END and csum[j] < tgt:
            j += 1
        cuts.append(j)
    cuts.append(P2_END)

    block = e(nc.Block())

    @block.sync
    def _(sync):
        sync.dma_start(out=Xf.ap().rearrange("p (a d) -> p a d", d=D),
                       in_=_r3(nsW_in)).then_inc(dL, 16)
        sync.dma_start(out=Xse.ap().rearrange("p (a c) -> p a c", c=CS),
                       in_=_r3(nsWs_in)).then_inc(dL, 16)
        sync.dma_start(out=sid[:], in_=id_in[:]).then_inc(dL, 16)
        sync.dma_start(out=m0v[:, :], in_=m0_in).then_inc(dL, 16)
        for k in range(NIT):
            sync.wait_ge(vU, 8 * (k + 1))
            if k < NIT - 1:
                src = XsE(k).ap()
                if XsE(k) is Xsr:
                    src = src.bitcast(F32)
                sync.dma_start(out=_r3(gin),
                               in_=src.rearrange("p (a c) -> p a c", c=CS)
                               ).then_inc(dL, 16)
            sync.wait_ge(cT, 64 + 16 * (k + 1))
            sync.dma_start(out=_r3(gin2),
                           in_=XTs.ap().rearrange("p (h d) -> p h d", d=D)
                           ).then_inc(dL, 16)
            P.replay("S", sync, SEMS, lo=cuts[k], hi=cuts[k + 1])
        sync.wait_ge(nX, 7)
        sync.dma_start(out=_r3(pexch),
                       in_=PBF.ap().rearrange("p (a d) -> p a d", d=D)
                       ).then_inc(dL, 16)
        P.replay("S", sync, SEMS, lo=P2_END)
        sync.wait_ge(eS, FIN["S"])

    @block.vector
    def _(vector):
        vector.wait_ge(dL, 64)
        vector.memset(sones[:], 1.0)
        vector.memset(sonesr[:], 1.0)
        for r8 in range(NCH):
            xb = Xf[:, r8 * D:(r8 + 1) * D]
            vector.tensor_mul(ssq, xb, xb)
            vector.tensor_reduce(sstat[:, r8:r8 + 1], ssq, AX.X, ALU.add)
        vector.tensor_reduce(sstat1, sstat, AX.X, ALU.add).then_inc(nX, 1)
        vector.wait_ge(nX, 2)
        vector.reciprocal(snrm_r, snrm_s).then_inc(nX, 1)
        vector.wait_ge(nX, 4)
        vector.tensor_scalar(Xse[:], Xse[:], sscal, None, ALU.mult
                             ).then_inc(nX, 1)
        for k in range(NIT):
            a_k, b_k = NS_SCHED[k]
            for d1 in range(NCH):
                if d1 == 0 and k >= 1:
                    vector.wait_ge(dL, 64 + 32 * k)
                vector.wait_ge(tG, 2 + 8 * k + d1 + 1)
                dst = AmK(k)[:, d1 * CS:(d1 + 1) * CS]
                if k == 0:
                    vector.tensor_scalar(dst, pAs[d1 % 2], sscal, None,
                                         ALU.mult).then_inc(vA, 1)
                else:
                    vector.tensor_copy(dst, pAs[d1 % 2]).then_inc(vA, 1)
            for r8 in range(NCH):
                vector.wait_ge(tU, 8 * k + r8 + 1)
                sl = slice(r8 * CS, (r8 + 1) * CS)
                srcb = XsE(k - 1)[:, sl]
                dstb = XsE(k)[:, sl]
                vector.tensor_scalar(dstb, srcb, a_k, None, ALU.mult)
                vector.tensor_scalar(pUs[r8 % 2], pUs[r8 % 2], b_k,
                                     None, ALU.mult)
                if k == 0:
                    vector.tensor_scalar(pUs[r8 % 2], pUs[r8 % 2], sscal,
                                         None, ALU.mult)
                vector.tensor_add(dstb, dstb, pUs[r8 % 2]).then_inc(vU, 1)
            P.replay("V", vector, SEMS, lo=cuts[k], hi=cuts[k + 1])
        vector.wait_ge(dRs, 128 * NIT - 64)
        vector.tensor_copy(PBF[:, :], XTr.ap().bitcast(F32)).then_inc(nX, 2)
        P.replay("V", vector, SEMS, lo=P2_END)

    @block.scalar
    def _(scalar):
        scalar.wait_ge(tG, 1)
        scalar.activation(snrm_s, pN1, AF.Sqrt).then_inc(nX, 1)
        scalar.wait_ge(tG, 2)
        scalar.copy(sscal, pN2).then_inc(nX, 1)
        for t in range(64):
            scalar.wait_ge(tT, t + 1)
            dst = tr_dst(t, 0).bitcast(F32)
            scalar.copy(dst, pTs[t % 2]).then_inc(cT, 1)
        for r in range(1, NIT + 1):
            k = r - 1
            scalar.wait_ge(tU, 8 * r)
            if k >= 1:
                scalar.wait_ge(dL, 64 + 32 * k)
            for t in range(16):
                scalar.wait_ge(tT, 64 + 16 * k + t + 1)
                r8, h = divmod(t, 2)
                scalar.copy(XTs[:, h * D + r8 * 128: h * D + r8 * 128 + 128],
                            pTs[t % 2]).then_inc(cT, 1)
            if r < NIT:
                P.replay("C", scalar, SEMS, lo=cuts[r - 1], hi=cuts[r])
        P.replay("C", scalar, SEMS, lo=cuts[NIT - 1])

    @block.gpsimd
    def _(gpsimd):
        for k in range(NIT):
            ncc = 2 * k
            if k < NIT - 1:
                gpsimd.wait_ge(dL, 64 + 32 * k + 16)
                gpsimd.collective_compute(
                    "AllGather", ALU.bypass, replica_groups=G4,
                    ins=[gin[:, :]], outs=[gout[:, :, :]],
                ).then_inc(cc, 1)
                ncc += 1
                gpsimd.wait_ge(dL, 64 + 32 * k + 32)
            else:
                gpsimd.wait_ge(dL, 64 + 32 * k + 16)
            gpsimd.wait_ge(cc, ncc)
            gpsimd.collective_compute(
                "AllGather", ALU.bypass, replica_groups=G4,
                ins=[gin2[:, :]], outs=[gout2[:, :, :]],
            ).then_inc(cc, 1)
            ncc += 1
            P.replay("G", gpsimd, SEMS, lo=cuts[k - 1] if k >= 1 else 0,
                     hi=cuts[k])
            gpsimd.wait_ge(cc, ncc)
            gpsimd.wait_ge(tG, 2 + 8 * (k + 1))
            gpsimd.wait_ge(tU, 8 * (k + 1))
            gpsimd.wait_ge(tT, 64)
            if k < NIT - 1:
                tgt = Xfr if (k + 1) >= NSPLIT else Xf
                for g in range(4):
                    gpsimd.dma_start(
                        out=tgt.ap().rearrange("p (a d) -> p a d", d=D
                                               )[:, :, g * CS:(g + 1) * CS],
                        in_=_r3(gout[g]),
                    ).then_inc(dRs, 16)
            tgt2 = XTr if (k + 1) >= NSPLIT else XTe
            for g in range(4):
                gpsimd.dma_start(
                    out=tgt2.ap().rearrange("p (a d) -> p a d", d=D
                                            )[:, 2 * g: 2 * g + 2, :],
                    in_=_r3(gout2[g]),
                ).then_inc(dRs, 16)
        gpsimd.wait_ge(dL, 64 + 32 * NIT - 16 + 16)
        gpsimd.collective_compute(
            "AllGather", ALU.bypass, replica_groups=GPAIR,
            ins=[pexch[:, :]], outs=[pboth[:, :, :]],
        ).then_inc(cc, 1)
        gpsimd.wait_ge(cc, 2 * NIT)
        gpsimd.dma_start(out=P1T.ap().rearrange("p (a d) -> p a d", d=D),
                         in_=_r3(pboth[0])).then_inc(dRs, 16)
        gpsimd.dma_start(out=P2T.ap().rearrange("p (a d) -> p a d", d=D),
                         in_=_r3(pboth[1])).then_inc(dRs, 16)
        gpsimd.wait_ge(dRs, 128 * NIT - 64 + 32)
        gpsimd.memset(smallf[0:1, 63:64], 0.0).then_inc(nX, 1)
        P.replay("GKH", gpsimd, SEMS, lo=cuts[NIT - 1])
        gpsimd.wait_ge(eG, FIN["G"])
        gpsimd.wait_ge(eK, FIN["K"])

    @block.tensor
    def _(tensor):
        tensor.wait_ge(nX, 1)
        tensor.matmul(pN1, sones[:], sstat1, start=True, stop=True
                      ).then_inc(tG, 1)
        tensor.wait_ge(nX, 3)
        tensor.matmul(pN2, sonesr[:], snrm_r, start=True, stop=True
                      ).then_inc(tG, 1)
        tensor.wait_ge(dL, 64)
        tensor.wait_ge(nX, 4)
        for t in range(64):
            if t >= 2:
                tensor.wait_ge(cT, t - 1)
            tensor.transpose(pTs[t % 2], tr_src(t, 0).bitcast(F32)
                             if False else tr_src(t, 0),
                             sid.ap().bitcast(F32)).then_inc(tT, 1)
        for k in range(NIT):
            P.replay("T", tensor, SEMS, lo=cuts[k - 1] if k >= 1 else 0,
                     hi=cuts[k])
            late = k >= NSPLIT
            if k >= 1:
                tensor.wait_ge(dRs, 128 * k - 64)
            tensor.wait_ge(vU, 8 * k)
            if k == 0:
                tensor.wait_ge(nX, 5)
            for d1 in range(NCH):
                if d1 >= 2:
                    tensor.wait_ge(vA, 8 * k + d1 - 1)
                ins = None
                for r8 in range(NCH):
                    o = r8 * D + d1 * 128
                    lhs = XfR(k)[:, o:o + 128]
                    rhs = XsE(k - 1)[:, r8 * CS:(r8 + 1) * CS]
                    ins = tensor.matmul(pAs[d1 % 2], lhs, rhs,
                                        start=(r8 == 0), stop=(r8 == NCH - 1))
                ins.then_inc(tG, 1)
            tensor.wait_ge(vA, 8 * (k + 1))
            if k == 0:
                tensor.wait_ge(cT, 64)
            else:
                tensor.wait_ge(dRs, 128 * k)
            for r8 in range(NCH):
                if r8 >= 2:
                    tensor.wait_ge(vU, 8 * k + r8 - 1)
                ins = None
                for d8 in range(NCH):
                    o = d8 * D + r8 * 128
                    ins = tensor.matmul(pUs[r8 % 2], XTk(k)[:, o:o + 128],
                                        AmK(k)[:, d8 * CS:(d8 + 1) * CS],
                                        start=(d8 == 0), stop=(d8 == NCH - 1))
                ins.then_inc(tU, 1)
            tensor.wait_ge(vU, 8 * (k + 1))
            src_r = XsE(k) is Xsr
            for t in range(16):
                tensor.wait_ge(cT, 64 + 16 * k + t - 1)
                r8, h = divmod(t, 2)
                blk_src = XsE(k)[:, r8 * CS + h * 128: r8 * CS + h * 128 + 128]
                out_ap = pTs[t % 2].bitcast(F32R) if src_r else pTs[t % 2]
                id_ap = sid.ap() if src_r else sid.ap().bitcast(F32)
                tensor.transpose(out_ap, blk_src, id_ap).then_inc(tT, 1)
        P.replay("T", tensor, SEMS, lo=cuts[NIT - 1])

    build.P = P
    build.cuts = cuts
    build.P2_END = P2_END
    build.FIN = FIN
    st.pop_all()
    return nc


def kernel(**inputs):
    x = np.asarray(inputs["L0_attn_x"], np.float32)
    Wq = np.asarray(inputs["L0_attn_Wq"], np.float32)
    Wv = np.asarray(inputs["L0_attn_Wv"], np.float32)
    Wc = np.asarray(inputs["L0_attn_Wc"], np.float32)
    Wp1 = np.asarray(inputs["L0_attn_Wp1"], np.float32)
    Wp2 = np.asarray(inputs["L0_attn_Wp2"], np.float32)
    ident = np.asarray(inputs["L0_attn_identity"], np.float32)
    qw = np.asarray(inputs["L0_ffn_qweights"], np.int32)
    scales = np.asarray(inputs["L0_ffn_scales"], np.float32)
    biases = np.asarray(inputs["L0_ffn_biases"], np.float32)
    la = np.asarray(inputs["L0_ffn_lora_a"], np.float32)
    lb = np.asarray(inputs["L0_ffn_lora_b"], np.float32)
    gam = np.asarray(inputs["L0_ffn_ln_gamma"], np.float32)
    bet = np.asarray(inputs["L0_ffn_ln_beta"], np.float32)

    bf = ml_dtypes.bfloat16
    eye = np.eye(128, dtype=np.float32)
    qw_t = np.ascontiguousarray(qw.transpose(0, 2, 1))
    srep_t = np.ascontiguousarray(
        np.repeat(scales, 16, axis=2).transpose(0, 2, 1))
    sT_t = np.ascontiguousarray(scales.transpose(0, 2, 1))
    la_t = np.ascontiguousarray(la.transpose(0, 2, 1))
    lb_t = np.ascontiguousarray(lb.transpose(0, 2, 1))
    gam_fm = np.ascontiguousarray(
        gam.reshape(3, NCH, 128).transpose(2, 0, 1).reshape(128, 3 * NCH))
    bet_fm = np.ascontiguousarray(
        bet.reshape(3, NCH, 128).transpose(2, 0, 1).reshape(128, 3 * NCH))
    bias_fm = np.ascontiguousarray(
        biases.reshape(6, NCH, 128).transpose(2, 0, 1).reshape(128, 48))
    pool = np.zeros((NCH, 128, 64), np.float32)
    for d8 in range(NCH):
        for p_ in range(128):
            g_ = d8 * 8 + p_ // 16
            pool[d8, p_, g_] = 1.0
    idvec_fm = np.ascontiguousarray(
        ident.reshape(NCH, 128).T)
    wq_t = np.ascontiguousarray(Wq.T)
    wv_t = np.ascontiguousarray(Wv.T)
    wc_t = np.ascontiguousarray(Wc.T)

    xpad = np.concatenate([np.zeros((B, 1, D), np.float32), x], axis=1)

    nc = build()
    in_maps = []
    for c in range(8):
        W = Wp1 if c < 4 else Wp2
        s_ = c % 4
        xs = xpad[:, TC * c: TC * c + TC + 1, :]
        xs_fm = np.ascontiguousarray(
            xs.transpose(0, 2, 1).reshape(B, NCH, 128, TC + 1))
        sel = np.zeros((8, 2), np.float32)
        if c > 0:
            sel[c - 1, 0] = 1.0
        sel[c, 1] = 1.0
        m0 = np.zeros((128, 2), np.float32)
        m0[:, 0] = 0.0 if c == 0 else 1.0
        m0[:, 1] = 1.0 if c == 0 else 0.0
        in_maps.append({
            "xs": xs_fm, "wq_t": wq_t, "wv_t": wv_t, "wc_t": wc_t,
            "nsW": W, "nsWs": np.ascontiguousarray(W[:, CS * s_:CS * (s_ + 1)]),
            "ident128": eye, "qw_t": qw_t, "srep_t": srep_t, "sT_t": sT_t,
            "bias_t": biases, "bias_fm": bias_fm, "la_t": la_t, "lb_t": lb_t,
            "gam_fm": gam_fm, "bet_fm": bet_fm, "pool_m": pool,
            "sel_m": sel, "m0v": m0, "idvec_fm": idvec_fm,
        })
    res = run_bass_kernel_spmd(nc, in_maps, core_ids=list(range(8)))
    out = np.empty((B, 4096, D), np.float32)
    for c in range(8):
        o = res.results[c]["out"]
        out[:, TC * c: TC * (c + 1), :] = o.reshape(B, D, TC).transpose(0, 2, 1)
    return out
